# version 13 (final)
# baseline (speedup 1.0000x reference)
import numpy as np
from contextlib import ExitStack

import concourse.bass as bass
import concourse.mybir as mybir
from concourse.bass_utils import run_bass_kernel_spmd

F32 = mybir.dt.float32
BF16 = mybir.dt.bfloat16
AF = mybir.ActivationFunctionType
ALU = mybir.AluOpType
AX = mybir.AxisListType.X

B, T, D = 2, 8192, 2048
NH, HD = 16, 128
NE, II, SI = 8, 1408, 2816
NI = II // 128
NSI = SI // 128
ND = D // 128
EPS = 1e-5
ISC = float(1.0 / np.sqrt(128.0))
TCORE = 2048
THALF = 1024


class KB:
    def __init__(self, dbg=()):
        self.dbg = set(dbg)
        nc = bass.Bass("TRN2", target_bir_lowering=False)
        self.nc = nc
        self.cnt = {}
        self.sems = {}
        self.es_all = ExitStack()

        g = lambda n, s, k="ExternalInput": nc.dram_tensor(n, s, F32, kind=k).ap()
        self.x = g("x", [T, D])
        self.xres = g("xres", [TCORE, D])
        self.wq = g("wq", [4, 128, ND, 128])
        self.wk = g("wk", [4, 128, ND, 128])
        self.wv = g("wv", [128, ND, 512])
        self.wo = g("wo", [ND, 128, D])
        self.bo_row = g("bo_row", [1, D])
        self.rw = g("rw", [NE, D])
        self.gw = g("gw", [NE, NI, 128, ND, 128])
        self.uw = g("uw", [NE, NI, 128, ND, 128])
        self.dw = g("dw", [NE, NI, 128, D])
        self.sgw = g("sgw", [NSI, 128, ND, 128])
        self.suw = g("suw", [NSI, 128, ND, 128])
        self.sdw = g("sdw", [NSI, 128, D])
        self.sgb_t = g("sgb_t", [128, NSI])
        self.sub_t = g("sub_t", [128, NSI])
        self.sdb = g("sdb", [1, D])
        self.identf = g("identf", [128, 128])
        self.onesf = g("onesf", [1, 128])
        self.out = nc.dram_tensor("out", [TCORE, D], F32, kind="ExternalOutput").ap()

        kind = "ExternalOutput" if "qkv" in self.dbg else "Internal"
        self.qT_d = nc.dram_tensor("qT_d", [512, T], BF16, kind=kind).ap()
        self.kT_d = nc.dram_tensor("kT_d", [512, T], BF16, kind=kind).ap()
        self.vT_d = nc.dram_tensor("vT_d", [4, 128, T], BF16, kind=kind).ap()
        self.scr1_d = nc.dram_tensor("scr1_d", [128, ND * THALF], BF16, kind="Internal").ap()
        if "scr" in self.dbg:
            self.y_scr = nc.dram_tensor("y_scr", [2, 128, ND * THALF], BF16,
                                        kind="ExternalOutput").ap()
        if "s1" in self.dbg:
            self.y_s1 = nc.dram_tensor("y_s1", [8, 128, D], F32, kind="ExternalOutput").ap()
        if "ffn" in self.dbg:
            self.y_ln2 = nc.dram_tensor("y_ln2", [ND, 128, THALF], BF16, kind="ExternalOutput").ap()
            self.y_cmb = nc.dram_tensor("y_cmb", [128, THALF], F32, kind="ExternalOutput").ap()

        names = (["x", "w", "o", "pe", "act", "dve", "vf", "sf"]
                 + [f"r{i}" for i in range(24)] + [f"g{i}" for i in range(16)])
        for s in names:
            self.sems[s] = self.es_all.enter_context(nc.semaphore("sem_" + s))
            self.cnt[s] = 0

    def inc(self, ins, key, n=1):
        self.cnt[key] += n
        ins.then_inc(self.sems[key], n)
        return self.cnt[key]

    def w(self, eng, key, val):
        if val <= 0:
            return
        ename = getattr(eng, "engine", None) or getattr(eng, "name", None) or id(eng)
        wk = (str(ename), key)
        if not hasattr(self, "_wmax"):
            self._wmax = {}
        if self._wmax.get(wk, -1) >= val:
            return
        self._wmax[wk] = val
        eng.wait_ge(self.sems[key], val)

    def vfence(self, v, ins):
        c = self.inc(ins, "vf")
        v.wait_ge(self.sems["vf"], c)

    def sfence(self, s, ins):
        c = self.inc(ins, "sf")
        s.wait_ge(self.sems["sf"], c)

    def sb(self, es, name, shape, dt=F32):
        self._uid = getattr(self, "_uid", 0) + 1
        return es.enter_context(self.nc.sbuf_tensor(f"{name}_u{self._uid}", shape, dt))

    def ps(self, es, name, shape, dt=F32):
        self._uid = getattr(self, "_uid", 0) + 1
        return es.enter_context(self.nc.psum_tensor(f"{name}_u{self._uid}", shape, dt))

    def p0(self, es, e1):
        nc, blk = self.nc, self.blk
        self.identb = self.sb(es, "identb", [128, 128], BF16)
        self.identf_sb = self.sb(es, "identf_sb", [128, 128], F32)
        self.onesb = self.sb(es, "onesb", [128, 128], BF16)
        self.onesf_sb = self.sb(es, "onesf_sb", [1, 128], F32)

        self.sgb_sb = self.sb(es, "sgb_sb", [128, NSI], F32)
        self.sub_sb = self.sb(es, "sub_sb", [128, NSI], F32)
        self.rwT = self.sb(es, "rwT", [128, 128], BF16)
        self.eps_c = self.sb(es, "eps_c", [128, 1], F32)
        self.wq_sb = self.sb(e1, "wq_sb", [128, 4 * D], BF16)
        self.wk_sb = self.sb(e1, "wk_sb", [128, 4 * D], BF16)
        self.wv_sb = self.sb(e1, "wv_sb", [128, 4 * D], BF16)

        rw_pad = self.sb(e1, "rw_pad", [128, D], BF16)

        def gp(g):
            for dst, src in ((self.identb, self.identf), (self.identf_sb, self.identf),
                             (self.onesf_sb, self.onesf),
                             (self.sgb_sb, self.sgb_t), (self.sub_sb, self.sub_t)):
                self.inc(g.dma_start(out=dst[:], in_=src), "w", 16)
            self.c_consts = self.cnt["w"]
            for w_sb, w_in in ((self.wq_sb, self.wq), (self.wk_sb, self.wk)):
                for oc in range(4):
                    ins = g.dma_start(
                        out=w_sb[:, oc * D:(oc + 1) * D].rearrange("p (d c) -> p d c", c=128),
                        in_=w_in[oc])
                    self.inc(ins, "w", 16)
            ins = g.dma_start(
                out=self.wv_sb[:].rearrange("p (d c) -> p d c", c=512), in_=self.wv)
            self.inc(ins, "w", 16)
            self.c_wqkv = self.cnt["w"]
            g.wait_ge(self.sems["dve"], self.c_rwpad_ms)
            ins = g.dma_start(out=rw_pad[0:8, :], in_=self.rw)
            self.inc(ins, "w", 16)
            self.c_rw = self.cnt["w"]

        def ve(v):
            v.memset(self.onesb[:], 1.0)
            v.memset(self.eps_c[:], EPS)
            ins = v.memset(rw_pad[:], 0.0)
            self.c_rwpad_ms = self.inc(ins, "dve")

        self.blk.vector(ve)
        self.blk.gpsimd(gp)

        pstr = [self.ps(e1, f"p0tr{i}", [128, 128], BF16) for i in range(2)]
        tr_pe = []
        rwT_evac = [self.cnt["dve"] + j + 1 for j in range(ND)]

        def pe(t):
            t.wait_ge(self.sems["w"], self.c_rw)
            for d in range(ND):
                if d >= 2:
                    self.w(t, "dve", rwT_evac[d - 2])
                ins = t.transpose(pstr[d % 2][:], rw_pad[:, 128 * d:128 * (d + 1)], self.identb[:])
                tr_pe.append(self.inc(ins, "pe"))

        def ve2(v):
            for d in range(ND):
                self.w(v, "pe", tr_pe[d])
                ins = v.tensor_copy(self.rwT[:, 8 * d:8 * (d + 1)], pstr[d % 2][:, 0:8])
                assert self.inc(ins, "dve") == rwT_evac[d]

        self.blk.tensor(pe)
        self.blk.vector(ve2)
        self.c_rwT_done = rwT_evac[-1]

    def p1(self, es):
        nc = self.nc
        NX = 8
        NL = 5
        NS = 12
        xa = [self.sb(es, f"xa{i}", [128, D]) for i in range(NX)]
        junk = self.sb(es, "junk", [128, D])
        ln_tm = [self.sb(es, f"lntm{i}", [128, D], BF16) for i in range(NL)]
        lnT = [self.sb(es, f"lnT{i}", [128, ND * 512], BF16) for i in range(2)]
        stage = [self.sb(es, f"qst{i}", [128, 512], BF16) for i in range(NS)]
        st_sr = self.sb(es, "st_sr", [128, 8])
        st_nm = self.sb(es, "st_nm", [128, 8])
        st_sq = self.sb(es, "st_sq", [128, 8])
        st_sd = self.sb(es, "st_sd", [128, 8])
        st_rs = self.sb(es, "st_rs", [128, 8])
        ps_tr = [self.ps(es, f"p1tr{i}", [128, 1024], BF16) for i in range(2)]
        ps_mm = [self.ps(es, f"p1mm{i}", [128, 512], F32) for i in range(4)]

        NG = 16
        xload = [0] * 64
        norm_done = [0] * 64
        trans_pe = {}
        trans_done = [0] * 64
        tr_evac_bank = [0, 0]
        te_last = [0] * NG
        mm_done = [0] * NG
        mm_cnt = {}
        evac_cnt = {}
        store_done = {}
        stats_war = [0, 0]
        mats = [(self.wq_sb, self.qT_d), (self.wk_sb, self.kT_d)]
        self.p1_store_last = 0

        def sync_loads(g):
            if g >= NG:
                return
            def f(sy):
                for k in range(4):
                    tt = 4 * g + k
                    slot = tt % NX
                    if tt >= NX:
                        self.w(sy, "dve", norm_done[tt - NX])
                    ins = sy.dma_start(out=xa[slot][:], in_=self.x[128 * tt:128 * (tt + 1), :])
                    xload[tt] = self.inc(ins, f"r{tt % NX}", 16)
            self.blk.sync(f)

        def vec_a(g):
            p = g % 2
            def f(v):
                for k in range(4):
                    tt = 4 * g + k
                    self.w(v, f"r{tt % NX}", xload[tt])
                if g >= 2:
                    self.w(v, "act", stats_war[p])
                last = None
                for k in range(4):
                    c = 4 * p + k
                    last = v.reduce_sum(st_sr[:, c:c + 1], xa[(4 * g + k) % NX][:], axis=AX)
                self.vfence(v, last)
                for k in range(4):
                    c = 4 * p + k
                    last = v.tensor_scalar_mul(st_nm[:, c:c + 1], st_sr[:, c:c + 1], -1.0 / D)
                self.vfence(v, last)
                self.ln_stats = self.inc(v.engine_nop(), "dve")
            self.blk.vector(f)

        def sca_a(g):
            p = g % 2
            def f(s):
                self.w(s, "dve", self.ln_stats)
                last = None
                for k in range(4):
                    c = 4 * p + k
                    last = s.activation(junk[:], xa[(4 * g + k) % NX][:], AF.Square,
                                        bias=st_nm[:, c:c + 1], accum_out=st_sq[:, c:c + 1])
                    self.sfence(s, last)
                for k in range(4):
                    c = 4 * p + k
                    last = s.activation(st_sd[:, c:c + 1], st_sq[:, c:c + 1], AF.Sqrt,
                                        scale=1.0 / D, bias=self.eps_c[:])
                self.sqrt_done = self.inc(last, "act")
                stats_war[p] = self.sqrt_done
            self.blk.scalar(f)

        def vec_b(g):
            p = g % 2
            def f(v):
                self.w(v, "act", self.sqrt_done)
                last = None
                for k in range(4):
                    c = 4 * p + k
                    last = v.reciprocal(st_rs[:, c:c + 1], st_sd[:, c:c + 1])
                self.vfence(v, last)
                for k in range(4):
                    tt = 4 * g + k
                    c = 4 * p + k
                    if tt >= NL:
                        self.w(v, "pe", trans_done[tt - NL])
                    ins = v.tensor_scalar(ln_tm[tt % NL][:], xa[tt % NX][:],
                                          st_nm[:, c:c + 1], st_rs[:, c:c + 1],
                                          ALU.add, ALU.mult)
                    norm_done[tt] = self.inc(ins, "dve")
            self.blk.vector(f)

        def pe_trans(g):
            ev_base = self.cnt["dve"]
            ev_plan = {d: ev_base + d + 1 for d in range(ND)}
            def f(t):
                if g == 0:
                    t.wait_ge(self.sems["w"], self.c_rw)
                for d in range(ND):
                    bank = d % 2
                    for k in range(4):
                        tt = 4 * g + k
                        self.w(t, "dve", norm_done[tt])
                        if d >= 2:
                            self.w(t, "dve", ev_plan[d - 2])
                        else:
                            self.w(t, "dve", tr_evac_bank[bank])
                        ins = t.transpose(ps_tr[bank][:, 128 * k:128 * (k + 1)],
                                          ln_tm[tt % NL][:, 128 * d:128 * (d + 1)],
                                          self.identb[:])
                        trans_pe[(tt, d)] = self.inc(ins, "pe")
                for k in range(4):
                    trans_done[4 * g + k] = self.cnt["pe"]
            self.blk.tensor(f)
            return ev_plan

        def vec_evac(g, ev_plan):
            pbuf = g % 2
            def f(v):
                if g >= 2:
                    self.w(v, "pe", mm_done[g - 2])
                for d in range(ND):
                    self.w(v, "pe", trans_pe[(4 * g + 3, d)])
                    ins = v.tensor_copy(lnT[pbuf][:, 512 * d: 512 * (d + 1)],
                                        ps_tr[d % 2][:, 0:512])
                    assert self.inc(ins, "dve") == ev_plan[d]
                    tr_evac_bank[d % 2] = ev_plan[d]
                te_last[g] = self.cnt["dve"]
            self.blk.vector(f)

        def pe_mm(g, ev_plan):
            pbuf = g % 2
            ev_base = self.cnt["act"]
            off = 1 if g + 1 < NG else 0
            for j in range(12):
                evac_cnt[g * 12 + j] = ev_base + j + 1 + (off if j >= 2 else 0)
            def f(t):
                for j in range(12):
                    bk = (g * 12 + j) % 4
                    prev = (g * 12 + j - 4)
                    if prev >= 0:
                        self.w(t, "act", evac_cnt.get(prev, 0))
                    if j < 8:
                        m, oc = j // 4, j % 4
                        w_sb = mats[m][0]
                        for d in range(ND):
                            self.w(t, "dve", ev_plan[d])
                            ins = t.matmul(ps_mm[bk][:],
                                           w_sb[:, oc * D + 128 * d: oc * D + 128 * (d + 1)],
                                           lnT[pbuf][:, 512 * d: 512 * (d + 1)],
                                           start=(d == 0), stop=(d == ND - 1))
                    else:
                        k = j - 8
                        for d in range(ND):
                            self.w(t, "dve", ev_plan[d])
                            ins = t.matmul(ps_mm[bk][:],
                                           lnT[pbuf][:, 512 * d + 128 * k:
                                               512 * d + 128 * (k + 1)],
                                           self.wv_sb[:, 512 * d: 512 * (d + 1)],
                                           start=(d == 0), stop=(d == ND - 1))
                    mm_cnt[(g, j)] = self.inc(ins, "pe")
                mm_done[g] = self.cnt["pe"]
            self.blk.tensor(f)

        def sca_evac(g, js):
            def f(s):
                for j in js:
                    gi = g * 12 + j
                    slot = gi % NS
                    bk = gi % 4
                    self.w(s, "pe", mm_cnt[(g, j)])
                    if gi >= NS:
                        self.w(s, f"r{8 + (gi - NS) % NS}", store_done[gi - NS])
                    ins = s.activation(stage[slot][:], ps_mm[bk][:], AF.Copy)
                    assert self.inc(ins, "act") == evac_cnt[gi], (g, j)
            self.blk.scalar(f)

        def sync_store(g):
            def f(sy):
                for j in range(12):
                    gi = g * 12 + j
                    slot = gi % NS
                    self.w(sy, "act", evac_cnt[gi])
                    if j < 8:
                        m, oc = j // 4, j % 4
                        ins = sy.dma_start(
                            out=mats[m][1][128 * oc:128 * (oc + 1), 512 * g:512 * (g + 1)],
                            in_=stage[slot][:])
                    else:
                        k = j - 8
                        tt = 4 * g + k
                        for hl in range(4):
                            ins = sy.dma_start(
                                out=self.vT_d[hl][:, 128 * tt:128 * (tt + 1)],
                                in_=stage[slot][:, 128 * hl:128 * (hl + 1)])
                            if hl < 3:
                                self.inc(ins, f"r{8 + gi % NS}", 16)
                    store_done[gi] = self.inc(ins, f"r{8 + gi % NS}", 16)
            self.blk.sync(f)

        sync_loads(0); sync_loads(1)
        vec_a(0); sca_a(0); vec_b(0)
        evp = {0: pe_trans(0)}
        vec_evac(0, evp[0])
        for g in range(NG):
            pe_mm(g, evp[g])
            sca_evac(g, range(2))
            if g + 1 < NG:
                sync_loads(g + 2)
                vec_a(g + 1); sca_a(g + 1); vec_b(g + 1)
                evp[g + 1] = pe_trans(g + 1)
                vec_evac(g + 1, evp[g + 1])
            sca_evac(g, range(2, 12))
            sync_store(g)
        self.p1_store_final = {f"r{8 + i}": self.cnt[f"r{8 + i}"] for i in range(NS)}

    def p2(self, es):
        nc = self.nc
        qT = self.sb(es, "qT", [128, T], BF16)
        kT = self.sb(es, "kT", [128, T], BF16)
        v_tm = self.sb(es, "v_tm", [128, T], BF16)
        wT = [self.sb(es, f"wT{i}", [128, 1024], BF16) for i in range(2)]
        ctx_sb = [self.sb(es, f"ctx{i}", [128, 1024], BF16) for i in range(2)]
        lnD_sb = self.sb(es, "lnD", [128, 1024])
        recip_sb = self.sb(es, "recip", [128, 1024])
        psA = self.ps(es, "psA", [128, 1024], F32)
        psB = self.ps(es, "psB", [128, 1024], F32)
        ps_sums = self.ps(es, "ps_sums", [128, 1024], F32)
        ps_ctx = self.ps(es, "ps_ctx", [128, 1024], F32)
        vtA = [psA[:, 0:64].bitcast(BF16), psA[:, 512:576].bitcast(BF16)]

        NQC = 8
        self.p2_gates = (self.cnt["act"], self.cnt["dve"])
        st = dict(q_ld=0, k_ld=0, vt_pe={}, sc={}, sp={}, lnD=0,
                  ctxm=[0, 0], last_pv=0, ctr_pe={}, sc_last={},
                  exp_last_prev=0, sp_prev2=None, vt_evac_last=0,
                  ctxm_last_head=0, recip=0)
        self.scr_evac_last = 0

        def emit_ctx_evac(v, h, qc, hl2):
            g = qc // 2
            scrT = self.scrT[h // 2]
            for dloc in range(8):
                dchunk = 8 * (qc % 2) + dloc
                self.w(v, "pe", st["ctr_pe"][(qc, dloc)])
                ins = v.tensor_copy(
                    scrT[:, dchunk * THALF + 512 * hl2 + 128 * g:
                         dchunk * THALF + 512 * hl2 + 128 * g + 128],
                    vtA[dloc % 2][:])
                self.scr_evac_last = self.inc(ins, "dve")

        def loads(h):
            def f(sy):
                if h == 0:
                    for k2, v2 in self.p1_store_final.items():
                        self.w(sy, k2, v2)
                else:
                    self.w(sy, "pe", st["sc_last"][h - 1])
                ins = sy.dma_start(out=qT[:], in_=self.qT_d[128 * h:128 * (h + 1), :])
                st["q_ld"] = self.inc(ins, "r20", 16)
                ins = sy.dma_start(out=kT[:], in_=self.kT_d[128 * h:128 * (h + 1), :])
                st["k_ld"] = self.inc(ins, "r21", 16)
                if h > 0:
                    self.w(sy, "pe", st["last_pv"])
                ins = sy.dma_start(out=v_tm[:], in_=self.vT_d[h])
                st["v_ld"] = self.inc(ins, "r22", 16)
            self.blk.sync(f)

        def pe_chunk(h, qc, hl2):
            act_base = self.cnt["act"]
            exp_plan = {j: act_base + j + 1 for j in range(64)}
            dve_base = self.cnt["dve"]
            ctx_evac_plan = {dloc: dve_base + dloc + 1 for dloc in range(8)}
            prev_scr_last = self.scr_evac_last

            def f(t):
                if qc > 0:
                    self.w(t, "dve", st["ctxm"][(qc - 1) % 2])
                    for dloc in range(8):
                        if dloc >= 2:
                            self.w(t, "dve", ctx_evac_plan[dloc - 2])
                        else:
                            self.w(t, "act", st["exp_last_prev"])
                            self.w(t, "dve", prev_scr_last)
                        ins = t.transpose(vtA[dloc % 2][:],
                                          ctx_sb[(qc - 1) % 2][:, 128 * dloc:128 * (dloc + 1)],
                                          self.identb[:])
                        st["ctr_pe"][(qc - 1, dloc)] = self.inc(ins, "pe")
                def emit_scores(j):
                    psS = psA if j % 2 == 0 else psB
                    if j <= 1:
                        self.w(t, "act", st["exp_last_prev"])
                        if qc == 0:
                            if j == 0:
                                self.w(t, "r20", st["q_ld"])
                                self.w(t, "r21", st["k_ld"])
                                if h > 0:
                                    self.w(t, "dve", prev_scr_last)
                        elif j == 0:
                            self.w(t, "dve", ctx_evac_plan[7])
                    else:
                        self.w(t, "act", exp_plan[j - 2])
                    ins = None
                    for n in range(2):
                        ins = t.matmul(psS[:, 512 * n:512 * (n + 1)],
                                       kT[:, 128 * j:128 * (j + 1)],
                                       qT[:, 1024 * qc + 512 * n:1024 * qc + 512 * (n + 1)],
                                       start=True, stop=True)
                    st["sc"][j] = self.inc(ins, "pe")

                emit_scores(0)
                emit_scores(1)
                for j in range(64):
                    self.w(t, "act", exp_plan[j])
                    if j == 0:
                        if qc == 0:
                            self.w(t, "r22", st["v_ld"])
                            if st["lnD"]:
                                self.w(t, "act", st["lnD"])
                            self.w(t, "dve", st["ctxm_last_head"])
                        else:
                            if st["lnD"]:
                                self.w(t, "act", st["lnD"])
                            self.w(t, "dve", st["ctxm"][(qc - 1) % 2])
                    for n in range(2):
                        t.matmul(ps_sums[:, 512 * n:512 * (n + 1)], self.onesb[:],
                                 wT[j % 2][:, 512 * n:512 * (n + 1)],
                                 start=(j == 0), stop=(j == 63))
                    for n in range(2):
                        ins = t.matmul(ps_ctx[:, 512 * n:512 * (n + 1)],
                                       v_tm[:, 128 * j:128 * (j + 1)],
                                       wT[j % 2][:, 512 * n:512 * (n + 1)],
                                       start=(j == 0), stop=(j == 63))
                    st["sp"][j] = self.inc(ins, "pe")
                    if j + 2 < 64:
                        emit_scores(j + 2)
                st["sc_last"][h] = st["sc"][63]
            self.blk.tensor(f)

            def fa(s):
                for j in range(64):
                    self.w(s, "pe", st["sc"][j])
                    if j >= 2:
                        self.w(s, "pe", st["sp"][j - 2])
                    elif st["sp_prev2"] is not None:
                        self.w(s, "pe", st["sp_prev2"][j])
                    ins = s.activation(wT[j % 2][:],
                                       (psA if j % 2 == 0 else psB)[:],
                                       AF.Exp, scale=ISC)
                    assert self.inc(ins, "act") == exp_plan[j]
                self.w(s, "pe", st["sp"][63])
                ins = s.activation(lnD_sb[:], ps_sums[:], AF.Ln)
                st["lnD"] = self.inc(ins, "act")
                self.sfence(s, ins) if False else None
                s.wait_ge(self.sems["act"], st["lnD"])
                ins = s.activation(recip_sb[:], lnD_sb[:], AF.Exp, scale=-1.0)
                st["recip"] = self.inc(ins, "act")
                st["exp_last_prev"] = exp_plan[63]
                st["sp_prev2"] = {0: st["sp"][62], 1: st["sp"][63]}
            self.blk.scalar(fa)

            def fv(v):
                if qc > 0:
                    emit_ctx_evac(v, h, qc - 1, hl2)
                self.w(v, "act", st["recip"])
                self.w(v, "pe", st["sp"][63])
                if qc >= 2:
                    self.w(v, "pe", st["ctr_pe"][(qc - 2, 7)])
                elif st["ctxm_last_head"] and qc == 0 and h > 0:
                    self.w(v, "pe", st["ctr_pe"][(7, 7)])
                elif qc == 1:
                    pass
                ins = v.tensor_mul(ctx_sb[qc % 2][:], ps_ctx[:], recip_sb[:])
                st["ctxm"][qc % 2] = self.inc(ins, "dve")
            self.blk.vector(fv)

        def pe_tail(h, hl2):
            dve_base = self.cnt["dve"]
            evac_plan = {dloc: dve_base + dloc + 1 for dloc in range(8)}
            prev_scr_last = self.scr_evac_last
            def f(t):
                self.w(t, "dve", st["ctxm"][1])
                for dloc in range(8):
                    if dloc >= 2:
                        self.w(t, "dve", evac_plan[dloc - 2])
                    else:
                        self.w(t, "act", st["exp_last_prev"])
                        self.w(t, "dve", prev_scr_last)
                    ins = t.transpose(vtA[dloc % 2][:],
                                      ctx_sb[1][:, 128 * dloc:128 * (dloc + 1)],
                                      self.identb[:])
                    st["ctr_pe"][(7, dloc)] = self.inc(ins, "pe")
            self.blk.tensor(f)
            def fv(v):
                emit_ctx_evac(v, h, 7, hl2)
            self.blk.vector(fv)

        for h in range(4):
            hl2 = h % 2
            loads(h)
            for qc in range(NQC):
                pe_chunk(h, qc, hl2)
            pe_tail(h, hl2)
            st["ctxm_last_head"] = st["ctxm"][1]
            st["last_pv"] = st["sp"][63]

        if "scr" in self.dbg:
            last = self.scr_evac_last
            def fd(sy):
                self.w(sy, "dve", last)
                self.inc(sy.dma_start(out=self.y_scr[0], in_=self.scrT[0][:]), "x", 16)
                self.inc(sy.dma_start(out=self.y_scr[1], in_=self.scrT[1][:]), "x", 16)
            self.blk.sync(fd)

        p2_last = self.scr_evac_last
        def fsp(sy):
            self.w(sy, "dve", p2_last)
            self.spill_done = self.inc(sy.dma_start(out=self.scr1_d, in_=self.scrT[1][:]),
                                       "r18", 16)
        self.blk.sync(fsp)
        self.p2_last = p2_last

    def p3(self, es, hf):
        nc = self.nc
        g_pe, g_act, g_dve = self.cnt["pe"], self.cnt["act"], self.cnt["dve"]
        if hf == 0:
            scrT = self.scrT[0]
            gate_r18 = self.spill_done
        else:
            scrT = self.sb(es, "scrT1b", [128, ND * THALF], BF16)
            gate_r18 = 0
        wo_sb = [self.sb(es, f"wo{i}", [128, D], BF16) for i in range(ND)]
        bo_sb = self.sb(es, "bo_sb", [1, D], F32)
        psP = [self.ps(es, f"psP{i}", [128, D], F32) for i in range(2)]

        st = dict(ld={}, mm={}, add={})
        self.p3_add = {}

        if hf == 1:
            def frl(sy):
                self.w(sy, "dve", g_dve)
                self.w(sy, "pe", g_pe)
                st["rl"] = self.inc(sy.dma_start(out=scrT[:], in_=self.scr1_d), "r18", 16)
            self.blk.sync(frl)

        def init_s1(sy):
            if hf == 1 or True:
                self.w(sy, "dve", g_dve)
            st["bo"] = self.inc(sy.dma_start(out=bo_sb[:], in_=self.bo_row), "r16", 16)
            for ut in range(8):
                if hf == 1:
                    self.w(sy, "o", self.out_done_half0)
                base = 1024 * hf + 512 * (ut // 4) + (ut % 4)
                ins = sy.dma_start(out=self.S1h[ut][:],
                                   in_=self.xres[base: base + 509: 4, :])
                st[("s1", ut)] = self.inc(ins, f"r{8 + ut}", 16)
        self.blk.sync(init_s1)

        def loads(g):
            for d in range(ND):
                if d < 2:
                    self.w(g, "dve", g_dve)
                    self.w(g, "pe", g_pe)
                    if gate_r18:
                        self.w(g, "r18", gate_r18)
                ins = g.dma_start(out=wo_sb[d][:], in_=self.wo[d])
                st["ld"][d] = self.inc(ins, f"g{d}", 16)
        self.blk.gpsimd(loads)

        def pe_mm(ut):
            def f(t):
                if ut == 0:
                    self.w(t, "r16", st["bo"])
                    if hf == 1:
                        self.w(t, "r18", st["rl"])
                if ut < 2:
                    self.w(t, "dve", g_dve)
                    self.w(t, "act", g_act)
                else:
                    self.w(t, "dve", st["add"][ut - 2])
                ins = None
                for n in range(4):
                    ins = t.matmul(psP[ut % 2][:, 512 * n:512 * (n + 1)],
                                   self.onesf_sb[:], bo_sb[0:1, 512 * n:512 * (n + 1)],
                                   start=True, stop=False)
                for d in range(ND):
                    if ut == 0:
                        self.w(t, f"g{d}", st["ld"][d])
                    for n in range(4):
                        ins = t.matmul(psP[ut % 2][:, 512 * n:512 * (n + 1)],
                                       scrT[:, d * THALF + 128 * ut:d * THALF + 128 * (ut + 1)],
                                       wo_sb[d][:, 512 * n:512 * (n + 1)],
                                       start=False, stop=(d == ND - 1))
                st["mm"][ut] = self.inc(ins, "pe")
            self.blk.tensor(f)

        def dve_add(ut):
            def f(v):
                self.w(v, "pe", st["mm"][ut])
                self.w(v, f"r{8 + ut}", st[("s1", ut)])
                ins = v.tensor_tensor(self.S1h[ut][:], self.S1h[ut][:],
                                      psP[ut % 2][:], ALU.add)
                st["add"][ut] = self.inc(ins, "dve")
                self.p3_add[ut] = st["add"][ut]
            self.blk.vector(f)

        for ut in range(8):
            pe_mm(ut)
            dve_add(ut)
        self.p3_last = self.cnt["dve"]

        if "s1" in self.dbg and hf == 0:
            last = self.p3_last
            def fd(sy):
                self.w(sy, "dve", last)
                for ut in range(8):
                    self.inc(sy.dma_start(out=self.y_s1[ut], in_=self.S1h[ut][:]), "x", 16)
            self.blk.sync(fd)

    def ffn_half(self, es, hf):
        nc = self.nc
        g_pe, g_act, g_dve = self.cnt["pe"], self.cnt["act"], self.cnt["dve"]
        self.gate3 = (g_pe, g_act, g_dve)
        ln2T = self.sb(es, "ln2T", [128, ND * THALF], BF16)
        self.ln2T_ap = ln2T
        silu_sb = self.sb(es, "silu_sb", [128, THALF])

        with ExitStack() as e45:
            ln2tm = [self.sb(e45, f"ln2tm{i}", [128, D], BF16) for i in range(8)]
            junk2 = self.sb(e45, "junk2", [128, D])
            s2 = self.sb(e45, "st2", [128, 40])
            pstr = [self.ps(e45, f"p4tr{i}", [128, 1024], BF16) for i in range(4)]
            st = dict(norm={}, tr={}, tr_last=0, lnst=0)

            tr_plan_base = None

            chains = {}
            for grp in range(4):
                uts = (2 * grp, 2 * grp + 1)
                def ve_a(v, uts=uts):
                    for ut in uts:
                        self.w(v, "dve", self.p3_add[ut])
                    last = None
                    for ut in uts:
                        last = v.reduce_sum(s2[:, ut:ut + 1], self.S1h[ut][:], axis=AX)
                    self.vfence(v, last)
                    for ut in uts:
                        last = v.tensor_scalar_mul(s2[:, 8 + ut:9 + ut], s2[:, ut:ut + 1],
                                                   -1.0 / D)
                    self.vfence(v, last)
                    st["lnst"] = self.inc(v.engine_nop(), "dve")

                def sc_a(s, uts=uts):
                    self.w(s, "dve", st["lnst"])
                    for ut in uts:
                        ins = s.activation(junk2[:], self.S1h[ut][:], AF.Square,
                                           bias=s2[:, 8 + ut:9 + ut],
                                           accum_out=s2[:, 16 + ut:17 + ut])
                        self.sfence(s, ins)
                    last = None
                    for ut in uts:
                        last = s.activation(s2[:, 24 + ut:25 + ut], s2[:, 16 + ut:17 + ut],
                                            AF.Sqrt, scale=1.0 / D, bias=self.eps_c[:])
                    st["sqrt"] = self.inc(last, "act")

                def ve_b(v, uts=uts):
                    self.w(v, "act", st["sqrt"])
                    last = None
                    for ut in uts:
                        last = v.reciprocal(s2[:, 32 + ut:33 + ut], s2[:, 24 + ut:25 + ut])
                    self.vfence(v, last)
                    for ut in uts:
                        ins = v.tensor_scalar(ln2tm[ut][:], self.S1h[ut][:],
                                              s2[:, 8 + ut:9 + ut], s2[:, 32 + ut:33 + ut],
                                              ALU.add, ALU.mult)
                        st["norm"][ut] = self.inc(ins, "dve")
                chains[grp] = (ve_a, sc_a, ve_b)
            dve_base4 = self.cnt["dve"]
            evac_plan = {}
            _c = dve_base4 + 1
            for ut in range(8):
                if ut % 2 == 0:
                    _c += 3
                for d in range(ND):
                    evac_plan[(ut, d)] = _c + d + 1
                _c += 16 + (1 if ut >= 1 else 0) + 2 + 1

            pad = self.sb(e45, "pad", [128, 128])
            lg = [self.sb(e45, f"lg{i}", [128, 8]) for i in range(2)]
            top8 = [self.sb(e45, f"top8{i}", [128, 8]) for i in range(2)]
            cstat = [self.sb(e45, f"cstat{i}", [128, 8]) for i in range(2)]
            sel = [self.sb(e45, f"sel{i}", [128, 8]) for i in range(2)]
            e1t = [self.sb(e45, f"e1t{i}", [128, 8]) for i in range(2)]
            comb = [self.sb(e45, f"comb{i}", [128, 8]) for i in range(2)]
            ps_lgF = [self.ps(e45, f"ps_lg{i}", [128, 512], F32) for i in range(2)]
            ps_lg = [p[:, 0:8] for p in ps_lgF]
            ps_ptF = self.ps(e45, "ps_pt", [128, 512], F32)
            ps_pt = ps_ptF[:, 0:128]
            rst = dict(mm={}, evac={}, e1={}, sel={}, combw={}, ptr={}, pev={})

            def ms(v):
                ins = v.memset(pad[:], 0.0)
                rst["ms"] = self.inc(ins, "dve")
                assert rst["ms"] == dve_base4 + 1
            self.blk.vector(ms)

            for ut in range(8):
                if ut % 2 == 0:
                    ve_a_f, sc_a_f, ve_b_f = chains[ut // 2]
                    self.blk.vector(ve_a_f)
                    self.blk.scalar(sc_a_f)
                    self.blk.vector(ve_b_f)
                def f_tr(t, ut=ut):
                    self.w(t, "dve", st["norm"][ut])
                    if ut == 0:
                        self.w(t, "dve", g_dve)
                        self.w(t, "act", g_act)
                    for d in range(ND):
                        j = ut * ND + d
                        if j >= 4:
                            self.w(t, "dve", evac_plan[((j - 4) // ND, (j - 4) % ND)])
                        ins = t.transpose(pstr[j % 4][:, 0:128],
                                          ln2tm[ut][:, 128 * d:128 * (d + 1)],
                                          self.identb[:])
                        st["tr"][(ut, d)] = self.inc(ins, "pe")
                    if ut >= 1:
                        self.w(t, "dve", rst["combw"][ut - 1])
                        if ut >= 2:
                            self.w(t, "dve", rst["pev"][ut - 2])
                        else:
                            self.w(t, "dve", g_dve)
                        ins = t.transpose(ps_pt, pad[:], self.identf_sb[:])
                        rst["ptr"][ut - 1] = self.inc(ins, "pe")
                    ins = None
                    if ut >= 2:
                        self.w(t, "dve", rst["evac"][ut - 2])
                    for d in range(ND):
                        self.w(t, "dve", evac_plan[(ut, d)])
                        ins = t.matmul(ps_lg[ut % 2],
                                       ln2T[:, d * THALF + 128 * ut:d * THALF + 128 * (ut + 1)],
                                       self.rwT[:, 8 * d:8 * (d + 1)],
                                       start=(d == 0), stop=(d == ND - 1))
                    rst["mm"][ut] = self.inc(ins, "pe")
                self.blk.tensor(f_tr)

                def f_evac(v, ut=ut):
                    for d in range(ND):
                        j = ut * ND + d
                        self.w(v, "pe", st["tr"][(ut, d)])
                        ins = v.tensor_copy(ln2T[:, d * THALF + 128 * ut:
                                                 d * THALF + 128 * (ut + 1)],
                                            pstr[j % 4][:, 0:128])
                        assert self.inc(ins, "dve") == evac_plan[(ut, d)], (ut, d)
                    if ut >= 1:
                        self.w(v, "pe", rst["ptr"][ut - 1])
                        ins = v.tensor_copy(self.combT[:, 128 * (ut - 1):128 * ut],
                                            ps_pt)
                        rst["pev"][ut - 1] = self.inc(ins, "dve")
                self.blk.vector(f_evac)

                def f_ve1(v, ut=ut):
                    p = ut % 2
                    self.w(v, "pe", rst["mm"][ut])
                    ins = v.tensor_copy(lg[p][:], ps_lg[p])
                    rst["evac"][ut] = self.inc(ins, "dve")
                    v.wait_ge(self.sems["dve"], rst["evac"][ut])
                    ins = v.max(out=top8[p][:], in_=lg[p][:])
                    self.vfence(v, ins)
                    ins = v.tensor_single_scalar(cstat[p][:, 0:1], top8[p][:, 2:3],
                                                 top8[p][:, 1:2], ALU.add)
                    self.vfence(v, ins)
                    ins = v.tensor_scalar_mul(cstat[p][:, 0:1], cstat[p][:, 0:1], 0.5)
                    ins2 = v.tensor_scalar_mul(cstat[p][:, 1:2], top8[p][:, 0:1], -1.0)
                    self.vfence(v, ins2)
                    ins = v.tensor_scalar_sub(sel[p][:], lg[p][:], cstat[p][:, 0:1])
                    self.vfence(v, ins)
                    ins = v.tensor_scalar_mul(sel[p][:], sel[p][:], 1e7)
                    self.vfence(v, ins)
                    ins = v.tensor_scalar_min(sel[p][:], sel[p][:], 1.0)
                    self.vfence(v, ins)
                    ins = v.tensor_scalar_max(sel[p][:], sel[p][:], 0.0)
                    rst["sel"][ut] = self.inc(ins, "dve")
                self.blk.vector(f_ve1)

                def f_sc(s, ut=ut):
                    p = ut % 2
                    self.w(s, "dve", rst["sel"][ut])
                    ins = s.activation(e1t[p][:], lg[p][:], AF.Exp, bias=cstat[p][:, 1:2])
                    rst["e1"][ut] = self.inc(ins, "act")
                self.blk.scalar(f_sc)

                def f_ve2(v, ut=ut):
                    p = ut % 2
                    self.w(v, "act", rst["e1"][ut])
                    ins = v.tensor_mul(e1t[p][:], e1t[p][:], sel[p][:])
                    self.vfence(v, ins)
                    ins = v.reduce_sum(cstat[p][:, 2:3], e1t[p][:], axis=AX)
                    self.vfence(v, ins)
                    ins = v.reciprocal(cstat[p][:, 3:4], cstat[p][:, 2:3])
                    self.vfence(v, ins)
                    ins = v.tensor_scalar_mul(comb[p][:], e1t[p][:], cstat[p][:, 3:4])
                    self.vfence(v, ins)
                    if ut == 0:
                        self.w(v, "dve", rst["ms"])
                    if ut >= 1:
                        self.w(v, "pe", rst["ptr"][ut - 1])
                    ins = v.tensor_copy(pad[:, 0:8], comb[p][:])
                    rst["combw"][ut] = self.inc(ins, "dve")
                self.blk.vector(f_ve2)

            def f_pe2_last(t):
                self.w(t, "dve", rst["combw"][7])
                self.w(t, "dve", rst["pev"][6])
                ins = t.transpose(ps_pt, pad[:], self.identf_sb[:])
                rst["ptr"][7] = self.inc(ins, "pe")
            self.blk.tensor(f_pe2_last)

            def f_ve3_last(v):
                self.w(v, "pe", rst["ptr"][7])
                ins = v.tensor_copy(self.combT[:, 128 * 7:128 * 8], ps_pt)
                rst["pev"][7] = self.inc(ins, "dve")
            self.blk.vector(f_ve3_last)
            self.comb_done = self.cnt["dve"]
            self.ln2_done = self.comb_done
            self.p4_act_last = self.cnt["act"]
            self.p5_act_last = self.cnt["act"]
            if "ffn" in self.dbg and hf == 0:
                cd = self.comb_done
                def fdmp(sy):
                    self.w(sy, "dve", cd)
                    self.inc(sy.dma_start(
                        out=self.y_ln2,
                        in_=ln2T[:].rearrange("p (d u) -> d p u", u=THALF)), "x", 16)
                    self.inc(sy.dma_start(out=self.y_cmb, in_=self.combT[:]), "x", 16)
                self.blk.sync(fdmp)

        with ExitStack() as e6:
            A = self.sb(e6, "A_sb", [128, NI * THALF], BF16)
            wd_sb = self.sb(e6, "wd_sb", [128, NI * D], BF16)
            crow = self.sb(e6, "crow", [1, THALF])
            cb_sb = self.sb(e6, "cb_sb", [128, THALF])
            gw_r = [self.sb(e6, f"gw{i}", [128, D], BF16) for i in range(2)]
            uw_r = [self.sb(e6, f"uw{i}", [128, D], BF16) for i in range(2)]
            psG = [self.ps(e6, f"psG{i}", [128, THALF], F32) for i in range(2)]
            psU = [self.ps(e6, f"psU{i}", [128, THALF], F32) for i in range(2)]
            prev = dict(silu={}, t2={}, dadd_last=0, dmm_all=0, cbe=0)
            for e in range(NE):
                self._expert(hf, e, crow, cb_sb, silu_sb, A, wd_sb, gw_r, uw_r,
                             psG, psU, prev)

        with ExitStack() as e7:
            A7 = self.sb(e7, "A7_sb", [128, NSI * THALF], BF16)
            u7 = self.sb(e7, "u7", [128, THALF])
            psG = [self.ps(e7, f"psG7{i}", [128, THALF], F32) for i in range(2)]
            psU = [self.ps(e7, f"psU7{i}", [128, THALF], F32) for i in range(2)]
            self._p7(hf, e7, silu_sb, A7, u7, psG, psU, prev)

        def fout(s):
            for ut in range(8):
                self.w(s, "dve", self.sh_add_last[ut])
                base = 1024 * hf + 512 * (ut // 4) + (ut % 4)
                ins = s.dma_start(out=self.out[base: base + 509: 4, :],
                                  in_=self.S1h[ut][:])
                self.inc(ins, "o", 16)
        self.blk.scalar(fout)
        if hf == 0:
            self.out_done_half0 = self.cnt["o"]

    def _expert(self, hf, e, crow, cb_sb, silu_sb, A, wd_sb, gw_r, uw_r,
                psG, psU, prev):
        st6 = dict(gld={}, uld={}, dld={}, gmm={}, umm={}, silu={}, t2={},
                   dmm={}, dadd={})
        g_pe, g_act, g_dve = self.gate3

        def f_crow(sy):
            if e >= 1:
                self.w(sy, "dve", prev["cbe"])
            else:
                self.w(sy, "dve", self.comb_done)
            ins = sy.dma_start(out=crow[:], in_=self.combT[e:e + 1, :])
            st6["crow_ld"] = self.inc(ins, "r17", 16)
        self.blk.sync(f_crow)

        def f_cb(t):
            self.w(t, "r17", st6["crow_ld"])
            if e == 0:
                self.w(t, "dve", self.comb_done)
                self.w(t, "act", self.p5_act_last)
            else:
                self.w(t, "dve", prev["dadd_last"])
            ins = None
            for n in range(2):
                ins = t.matmul(psG[(NI - 1) % 2][:, 512 * n:512 * (n + 1)],
                               self.onesf_sb[:], crow[0:1, 512 * n:512 * (n + 1)],
                               start=True, stop=True)
            st6["cbmm"] = self.inc(ins, "pe")
        self.blk.tensor(f_cb)

        def f_cbe(v):
            self.w(v, "pe", st6["cbmm"])
            if e >= 1:
                self.w(v, "dve", prev["t2"][NI - 1])
            ins = v.tensor_copy(cb_sb[:], psG[(NI - 1) % 2][:])
            st6["cb"] = self.inc(ins, "dve")
        self.blk.vector(f_cbe)

        for i in range(NI):
            def f_ld(g, i=i):
                if e == 0 and i < 2:
                    self.w(g, "dve", self.comb_done)
                    self.w(g, "pe", g_pe)
                elif i < 2:
                    self.w(g, "pe", prev["umm_last"])
                else:
                    self.w(g, "pe", st6["umm"][i - 2])
                ins = g.dma_start(out=gw_r[i % 2][:].rearrange("p (d c) -> p d c", c=128),
                                  in_=self.gw[e, i])
                st6["gld"][i] = self.inc(ins, f"g{i % 2}", 16)
                ins = g.dma_start(out=uw_r[i % 2][:].rearrange("p (d c) -> p d c", c=128),
                                  in_=self.uw[e, i])
                st6["uld"][i] = self.inc(ins, f"g{2 + i % 2}", 16)
                if e >= 1:
                    self.w(g, "pe", prev["dmm_all"])
                ins = g.dma_start(out=wd_sb[:, i * D:(i + 1) * D], in_=self.dw[e, i])
                st6["dld"][i] = self.inc(ins, f"g{4 + i}", 16)
            self.blk.gpsimd(f_ld)

            def f_mm(t, i=i):
                self.w(t, f"g{i % 2}", st6["gld"][i])
                self.w(t, f"g{2 + i % 2}", st6["uld"][i])
                if i >= 2:
                    self.w(t, "act", st6["silu"][i - 2])
                    self.w(t, "dve", st6["t2"][i - 2])
                elif e >= 1:
                    self.w(t, "dve", prev["dadd_last"])
                if i == 0:
                    self.w(t, "dve", st6["cb"])
                ins = None
                for d in range(ND):
                    for n in range(2):
                        ins = t.matmul(psG[i % 2][:, 512 * n:512 * (n + 1)],
                                       gw_r[i % 2][:, 128 * d:128 * (d + 1)],
                                       self.ln2T_ap[:, d * THALF + 512 * n:
                                                    d * THALF + 512 * (n + 1)],
                                       start=(d == 0), stop=(d == ND - 1))
                st6["gmm"][i] = self.inc(ins, "pe")
                for d in range(ND):
                    for n in range(2):
                        ins = t.matmul(psU[i % 2][:, 512 * n:512 * (n + 1)],
                                       uw_r[i % 2][:, 128 * d:128 * (d + 1)],
                                       self.ln2T_ap[:, d * THALF + 512 * n:
                                                    d * THALF + 512 * (n + 1)],
                                       start=(d == 0), stop=(d == ND - 1))
                st6["umm"][i] = self.inc(ins, "pe")
            self.blk.tensor(f_mm)

            def f_silu(s, i=i):
                self.w(s, "pe", st6["gmm"][i])
                if i >= 1:
                    self.w(s, "dve", st6["t2"][i - 1])
                elif e >= 1:
                    self.w(s, "dve", prev["t2"][NI - 1])
                elif e == 0:
                    self.w(s, "dve", g_dve)
                ins = s.activation(silu_sb[:], psG[i % 2][:], AF.Silu)
                st6["silu"][i] = self.inc(ins, "act")
            self.blk.scalar(f_silu)

            def f_t2(v, i=i):
                self.w(v, "act", st6["silu"][i])
                self.w(v, "pe", st6["umm"][i])
                ins = v.tensor_mul(silu_sb[:], silu_sb[:], psU[i % 2][:])
                c1 = self.inc(ins, "dve")
                v.wait_ge(self.sems["dve"], c1)
                if e >= 1 and i == 0:
                    self.w(v, "pe", prev["dmm_all"])
                self.w(v, "dve", st6["cb"])
                ins = v.tensor_mul(A[:, i * THALF:(i + 1) * THALF], silu_sb[:], cb_sb[:])
                st6["t2"][i] = self.inc(ins, "dve")
            self.blk.vector(f_t2)

        for ut in range(8):
            def f_dmm(t, ut=ut):
                pair = psG if ut % 2 == 0 else psU
                if ut < 2:
                    self.w(t, "act", st6["silu"][NI - 1])
                    self.w(t, "dve", st6["t2"][NI - 1])
                    for i in range(NI):
                        self.w(t, f"g{4 + i}", st6["dld"][i])
                else:
                    self.w(t, "dve", st6["dadd"][(ut - 2, 1)])
                ins = None
                for k in range(2):
                    for i in range(NI):
                        for n in range(2):
                            ins = t.matmul(
                                pair[k][:, 512 * n:512 * (n + 1)],
                                A[:, i * THALF + 128 * ut:i * THALF + 128 * (ut + 1)],
                                wd_sb[:, i * D + 1024 * k + 512 * n:
                                      i * D + 1024 * k + 512 * (n + 1)],
                                start=(i == 0), stop=(i == NI - 1))
                    st6["dmm"][(ut, k)] = self.inc(ins, "pe")
            self.blk.tensor(f_dmm)

            def f_dadd(v, ut=ut):
                pair = psG if ut % 2 == 0 else psU
                for k in range(2):
                    self.w(v, "pe", st6["dmm"][(ut, k)])
                    if e == 0 and k == 0:
                        self.w(v, "dve", self.ln2_done)
                        self.w(v, "act", self.p4_act_last)
                    ins = v.tensor_tensor(self.S1h[ut][:, 1024 * k:1024 * (k + 1)],
                                          self.S1h[ut][:, 1024 * k:1024 * (k + 1)],
                                          pair[k][:], ALU.add)
                    st6["dadd"][(ut, k)] = self.inc(ins, "dve")
            self.blk.vector(f_dadd)

        prev["silu"] = dict(st6["silu"]); prev["t2"] = dict(st6["t2"])
        prev["umm_last"] = st6["umm"][NI - 1]
        prev["dmm_all"] = st6["dmm"][(7, 1)]
        prev["cbe"] = st6["cb"]
        prev["dadd_last"] = st6["dadd"][(7, 1)]
        self.p6_dadd_last = {ut: st6["dadd"][(ut, 1)] for ut in range(8)}
        self.p6_dmm_all = st6["dmm"][(7, 1)]
        self.p6_prev = prev

    def _p7(self, hf, e7, silu_sb, A, u7, psG, psU, prev):
        st = dict(gld={}, uld={}, gmm={}, umm={}, silu={}, t2={}, sld={},
                  dmm={}, dadd={})

        with ExitStack() as e7a:
            gw_r = [self.sb(e7a, f"sgwr{i}", [128, D], BF16) for i in range(2)]
            uw_r = [self.sb(e7a, f"suwr{i}", [128, D], BF16) for i in range(2)]
            for i in range(NSI):
                def f_ld(g, i=i):
                    if i < 2:
                        self.w(g, "pe", self.p6_dmm_all)
                        self.w(g, "dve", prev["dadd_last"])
                    else:
                        self.w(g, "pe", st["umm"][i - 2])
                    ins = g.dma_start(out=gw_r[i % 2][:].rearrange("p (d c) -> p d c", c=128),
                                      in_=self.sgw[i])
                    st["gld"][i] = self.inc(ins, f"g{i % 2}", 16)
                    ins = g.dma_start(out=uw_r[i % 2][:].rearrange("p (d c) -> p d c", c=128),
                                      in_=self.suw[i])
                    st["uld"][i] = self.inc(ins, f"g{2 + i % 2}", 16)
                self.blk.gpsimd(f_ld)

                def f_mm(t, i=i):
                    self.w(t, f"g{i % 2}", st["gld"][i])
                    self.w(t, f"g{2 + i % 2}", st["uld"][i])
                    if i < 2:
                        self.w(t, "dve", prev["dadd_last"])
                    else:
                        self.w(t, "act", st["silu"][i - 2])
                        self.w(t, "dve", st["t2"][i - 2])
                    ins = None
                    for d in range(ND):
                        for n in range(2):
                            ins = t.matmul(psG[i % 2][:, 512 * n:512 * (n + 1)],
                                           gw_r[i % 2][:, 128 * d:128 * (d + 1)],
                                           self.ln2T_ap[:, d * THALF + 512 * n:
                                                        d * THALF + 512 * (n + 1)],
                                           start=(d == 0), stop=(d == ND - 1))
                    st["gmm"][i] = self.inc(ins, "pe")
                    for d in range(ND):
                        for n in range(2):
                            ins = t.matmul(psU[i % 2][:, 512 * n:512 * (n + 1)],
                                           uw_r[i % 2][:, 128 * d:128 * (d + 1)],
                                           self.ln2T_ap[:, d * THALF + 512 * n:
                                                        d * THALF + 512 * (n + 1)],
                                           start=(d == 0), stop=(d == ND - 1))
                    st["umm"][i] = self.inc(ins, "pe")
                self.blk.tensor(f_mm)

                def f_act(s, i=i):
                    self.w(s, "pe", st["gmm"][i])
                    if i >= 1:
                        self.w(s, "dve", st["t2"][i - 1])
                    else:
                        self.w(s, "dve", prev["t2"][NI - 1])
                    ins = s.activation(silu_sb[:], psG[i % 2][:], AF.Silu,
                                       bias=self.sgb_sb[:, i:i + 1])
                    c1 = self.inc(ins, "act")
                    s.wait_ge(self.sems["act"], c1)
                    self.w(s, "pe", st["umm"][i])
                    ins = s.activation(u7[:], psU[i % 2][:], AF.Identity,
                                       bias=self.sub_sb[:, i:i + 1])
                    st["silu"][i] = self.inc(ins, "act")
                self.blk.scalar(f_act)

                def f_t2(v, i=i):
                    self.w(v, "act", st["silu"][i])
                    if i == 0:
                        self.w(v, "pe", self.p6_dmm_all)
                    ins = v.tensor_mul(A[:, i * THALF:(i + 1) * THALF], silu_sb[:], u7[:])
                    st["t2"][i] = self.inc(ins, "dve")
                self.blk.vector(f_t2)

        self.sh_add_last = {}
        with ExitStack() as e7b:
            sd_sb = self.sb(e7b, "sd_sb", [128, 11 * D], BF16)
            sdb_row = self.sb(e7b, "sdb_row", [1, D])
            def f_sdb(sy):
                self.w(sy, "pe", st["umm"][NSI - 1])
                ins = sy.dma_start(out=sdb_row[:], in_=self.sdb)
                st["sdb_ld"] = self.inc(ins, "r18", 16)
            self.blk.sync(f_sdb)

            for ih in range(2):
                ibase = ih * 11
                def f_sld(g, ih=ih, ibase=ibase):
                    for i in range(11):
                        if ih == 0:
                            self.w(g, "pe", st["umm"][NSI - 1])
                        else:
                            self.w(g, "pe", st["dmm"][(0, 7, 1)])
                        ins = g.dma_start(out=sd_sb[:, i * D:(i + 1) * D],
                                          in_=self.sdw[ibase + i])
                        st["sld"][(ih, i)] = self.inc(ins, f"g{4 + i}", 16)
                self.blk.gpsimd(f_sld)

                for ut in range(8):
                    def f_dmm(t, ih=ih, ut=ut, ibase=ibase):
                        pair = psG if ut % 2 == 0 else psU
                        if ut < 2:
                            self.w(t, "act", st["silu"][NSI - 1])
                            self.w(t, "dve", st["t2"][NSI - 1])
                            if ih == 1:
                                self.w(t, "dve", st["dadd"][(0, 7, 1)])
                                self.w(t, "r18", st["sdb_ld"])
                        else:
                            self.w(t, "dve", st["dadd"][(ih, ut - 2, 1)])
                        ins = None
                        for k in range(2):
                            if ih == 1:
                                for n in range(2):
                                    ins = t.matmul(pair[k][:, 512 * n:512 * (n + 1)],
                                                   self.onesf_sb[:],
                                                   sdb_row[0:1, 1024 * k + 512 * n:
                                                           1024 * k + 512 * (n + 1)],
                                                   start=True, stop=False)
                            for i in range(11):
                                if ut == 0:
                                    self.w(t, f"g{4 + i}", st["sld"][(ih, i)])
                                for n in range(2):
                                    ins = t.matmul(
                                        pair[k][:, 512 * n:512 * (n + 1)],
                                        A[:, (ibase + i) * THALF + 128 * ut:
                                          (ibase + i) * THALF + 128 * (ut + 1)],
                                        sd_sb[:, i * D + 1024 * k + 512 * n:
                                              i * D + 1024 * k + 512 * (n + 1)],
                                        start=(ih == 0 and i == 0), stop=(i == 10))
                            st["dmm"][(ih, ut, k)] = self.inc(ins, "pe")
                    self.blk.tensor(f_dmm)

                    def f_dadd(v, ih=ih, ut=ut):
                        pair = psG if ut % 2 == 0 else psU
                        for k in range(2):
                            self.w(v, "pe", st["dmm"][(ih, ut, k)])
                            ins = v.tensor_tensor(self.S1h[ut][:, 1024 * k:1024 * (k + 1)],
                                                  self.S1h[ut][:, 1024 * k:1024 * (k + 1)],
                                                  pair[k][:], ALU.add)
                            st["dadd"][(ih, ut, k)] = self.inc(ins, "dve")
                        if ih == 1:
                            self.sh_add_last[ut] = st["dadd"][(1, ut, 1)]
                    self.blk.vector(f_dadd)

    def build(self):
        nc = self.nc
        with ExitStack() as top:
            self.blk = top.enter_context(nc.Block())
            ep = top.enter_context(ExitStack())
            with ExitStack() as e1:
                self.p0(ep, e1)
                self.p1(e1)
            if not getattr(self, "only_p1", False):
                self.S1h = [self.sb(ep, f"s1h{i}", [128, D]) for i in range(8)]
                self.combT = self.sb(ep, "combT", [128, THALF])
                with ExitStack() as e_scr0:
                    self.scrT = [None, None]
                    self.scrT[0] = self.sb(e_scr0, "scrT0", [128, ND * THALF], BF16)
                    with ExitStack() as e_scr1:
                        self.scrT[1] = self.sb(e_scr1, "scrT1", [128, ND * THALF], BF16)
                        with ExitStack() as e2:
                            self.p2(e2)
                    if not getattr(self, "only_p2", False):
                        with ExitStack() as e3:
                            self.p3(e3, 0)
                if not getattr(self, "only_p2", False):
                    with ExitStack() as e4:
                        self.ffn_half(e4, 0)
                    with ExitStack() as e3:
                        self.p3(e3, 1)
                    with ExitStack() as e4:
                        self.ffn_half(e4, 1)
            rem = nc.sbuf_bytes_remaining
            final_counts = [(k, self.cnt[k]) for k in self.cnt
                            if k.startswith("r") or k in ("x", "w", "o")]
            def fin(sy):
                for k, v in final_counts:
                    self.w(sy, k, v)
            self.blk.sync(fin)
        return nc



def _restage(inp, core):
    b, qd = core // 4, core % 4
    d = {}
    x = inp["L0_attn_x"][b]
    d["x"] = np.ascontiguousarray(x)
    d["xres"] = np.ascontiguousarray(x[TCORE * qd: TCORE * (qd + 1)])
    def restage_w(w):
        noc = w.shape[1] // 128
        return np.ascontiguousarray(
            w.reshape(ND, 128, noc, 128).transpose(2, 1, 0, 3))

    for nm, w in (("wq", inp["L0_attn_Wq"]), ("wk", inp["L0_attn_Wk"])):
        d[nm] = restage_w(w[:, 512 * qd:512 * (qd + 1)])
    d["wv"] = np.ascontiguousarray(
        inp["L0_attn_Wv"][:, 512 * qd:512 * (qd + 1)]
        .reshape(ND, 128, 512).transpose(1, 0, 2))
    d["wo"] = np.ascontiguousarray(inp["L0_attn_Wo"].reshape(ND, 128, D))
    d["bo_row"] = np.ascontiguousarray(inp["L0_attn_bo"].reshape(1, D))
    d["rw"] = np.ascontiguousarray(inp["L0_ffn_router_w"])
    d["gw"] = np.stack([restage_w(inp["L0_ffn_gate_w"][e]) for e in range(NE)])
    d["uw"] = np.stack([restage_w(inp["L0_ffn_up_w"][e]) for e in range(NE)])
    d["dw"] = np.ascontiguousarray(inp["L0_ffn_down_w"].reshape(NE, NI, 128, D))
    d["sgw"] = restage_w(inp["L0_ffn_s_gate_w"])
    d["suw"] = restage_w(inp["L0_ffn_s_up_w"])
    d["sdw"] = np.ascontiguousarray(inp["L0_ffn_s_down_w"].reshape(NSI, 128, D))
    d["sgb_t"] = np.ascontiguousarray(inp["L0_ffn_s_gate_b"].reshape(NSI, 128).T)
    d["sub_t"] = np.ascontiguousarray(inp["L0_ffn_s_up_b"].reshape(NSI, 128).T)
    d["sdb"] = np.ascontiguousarray(inp["L0_ffn_s_down_b"].reshape(1, D))
    d["identf"] = np.eye(128, dtype=np.float32)
    d["onesf"] = np.ones((1, 128), dtype=np.float32)
    return d


def kernel(**inputs):
    kb = KB()
    nc = kb.build()
    in_maps = [_restage(inputs, c) for c in range(8)]
    res = run_bass_kernel_spmd(nc, in_maps, core_ids=list(range(8)))
    out = np.empty((B, T, D), np.float32)
    for c in range(8):
        b, qd = c // 4, c % 4
        out[b, TCORE * qd:TCORE * (qd + 1)] = res.results[c]["out"]
    return out


# version 14
# speedup vs baseline: 1.0145x; 1.0145x over previous
import numpy as np
from contextlib import ExitStack

import concourse.bass as bass
import concourse.mybir as mybir
from concourse.bass_utils import run_bass_kernel_spmd

F32 = mybir.dt.float32
BF16 = mybir.dt.bfloat16
AF = mybir.ActivationFunctionType
ALU = mybir.AluOpType
AX = mybir.AxisListType.X

B, T, D = 2, 8192, 2048
NH, HD = 16, 128
NE, II, SI = 8, 1408, 2816
NI = II // 128
NSI = SI // 128
ND = D // 128
EPS = 1e-5
ISC = float(1.0 / np.sqrt(128.0))
TCORE = 2048
THALF = 1024


class KB:
    def __init__(self, dbg=()):
        self.dbg = set(dbg)
        nc = bass.Bass("TRN2", target_bir_lowering=False)
        self.nc = nc
        self.cnt = {}
        self.sems = {}
        self.es_all = ExitStack()

        g = lambda n, s, k="ExternalInput": nc.dram_tensor(n, s, F32, kind=k).ap()
        self.x = g("x", [T, D])
        self.xres = g("xres", [TCORE, D])
        self.wq = g("wq", [4, 128, ND, 128])
        self.wk = g("wk", [4, 128, ND, 128])
        self.wv = g("wv", [128, ND, 512])
        self.wo = g("wo", [ND, 128, D])
        self.bo_row = g("bo_row", [1, D])
        self.rw = g("rw", [NE, D])
        self.gw = g("gw", [NE, NI, 128, ND, 128])
        self.uw = g("uw", [NE, NI, 128, ND, 128])
        self.dw = g("dw", [NE, NI, 128, D])
        self.sgw = g("sgw", [NSI, 128, ND, 128])
        self.suw = g("suw", [NSI, 128, ND, 128])
        self.sdw = g("sdw", [NSI, 128, D])
        self.sgb_t = g("sgb_t", [128, NSI])
        self.sub_t = g("sub_t", [128, NSI])
        self.sdb = g("sdb", [1, D])
        self.identf = g("identf", [128, 128])
        self.onesf = g("onesf", [1, 128])
        self.out = nc.dram_tensor("out", [TCORE, D], F32, kind="ExternalOutput").ap()

        kind = "ExternalOutput" if "qkv" in self.dbg else "Internal"
        self.qT_d = nc.dram_tensor("qT_d", [512, T], BF16, kind=kind).ap()
        self.kT_d = nc.dram_tensor("kT_d", [512, T], BF16, kind=kind).ap()
        self.vT_d = nc.dram_tensor("vT_d", [4, 128, T], BF16, kind=kind).ap()
        self.scr1_d = nc.dram_tensor("scr1_d", [128, ND * THALF], BF16, kind="Internal").ap()
        if "scr" in self.dbg:
            self.y_scr = nc.dram_tensor("y_scr", [2, 128, ND * THALF], BF16,
                                        kind="ExternalOutput").ap()
        if "s1" in self.dbg:
            self.y_s1 = nc.dram_tensor("y_s1", [8, 128, D], F32, kind="ExternalOutput").ap()
        if "ffn" in self.dbg:
            self.y_ln2 = nc.dram_tensor("y_ln2", [ND, 128, THALF], BF16, kind="ExternalOutput").ap()
            self.y_cmb = nc.dram_tensor("y_cmb", [128, THALF], F32, kind="ExternalOutput").ap()

        names = (["x", "w", "o", "pe", "act", "dve", "vf", "sf"]
                 + [f"r{i}" for i in range(24)] + [f"g{i}" for i in range(16)])
        for s in names:
            self.sems[s] = self.es_all.enter_context(nc.semaphore("sem_" + s))
            self.cnt[s] = 0

    def inc(self, ins, key, n=1):
        self.cnt[key] += n
        ins.then_inc(self.sems[key], n)
        return self.cnt[key]

    def w(self, eng, key, val):
        if val <= 0:
            return
        ename = getattr(eng, "engine", None) or getattr(eng, "name", None) or id(eng)
        wk = (str(ename), key)
        if not hasattr(self, "_wmax"):
            self._wmax = {}
        if self._wmax.get(wk, -1) >= val:
            return
        self._wmax[wk] = val
        eng.wait_ge(self.sems[key], val)

    def vfence(self, v, ins):
        c = self.inc(ins, "vf")
        v.wait_ge(self.sems["vf"], c)

    def sfence(self, s, ins):
        c = self.inc(ins, "sf")
        s.wait_ge(self.sems["sf"], c)

    def sb(self, es, name, shape, dt=F32):
        self._uid = getattr(self, "_uid", 0) + 1
        return es.enter_context(self.nc.sbuf_tensor(f"{name}_u{self._uid}", shape, dt))

    def ps(self, es, name, shape, dt=F32):
        self._uid = getattr(self, "_uid", 0) + 1
        return es.enter_context(self.nc.psum_tensor(f"{name}_u{self._uid}", shape, dt))

    def p0(self, es, e1):
        nc, blk = self.nc, self.blk
        self.identb = self.sb(es, "identb", [128, 128], BF16)
        self.identf_sb = self.sb(es, "identf_sb", [128, 128], F32)
        self.onesb = self.sb(es, "onesb", [128, 128], BF16)
        self.onesf_sb = self.sb(es, "onesf_sb", [1, 128], F32)

        self.sgb_sb = self.sb(es, "sgb_sb", [128, NSI], F32)
        self.sub_sb = self.sb(es, "sub_sb", [128, NSI], F32)
        self.rwT = self.sb(es, "rwT", [128, 128], BF16)
        self.eps_c = self.sb(es, "eps_c", [128, 1], F32)
        self.wq_sb = self.sb(e1, "wq_sb", [128, 4 * D], BF16)
        self.wk_sb = self.sb(e1, "wk_sb", [128, 4 * D], BF16)
        self.wv_sb = self.sb(e1, "wv_sb", [128, 4 * D], BF16)

        rw_pad = self.sb(e1, "rw_pad", [128, D], BF16)

        def gp(g):
            for dst, src in ((self.identb, self.identf), (self.identf_sb, self.identf),
                             (self.onesf_sb, self.onesf),
                             (self.sgb_sb, self.sgb_t), (self.sub_sb, self.sub_t)):
                self.inc(g.dma_start(out=dst[:], in_=src), "w", 16)
            self.c_consts = self.cnt["w"]
            for w_sb, w_in in ((self.wq_sb, self.wq), (self.wk_sb, self.wk)):
                for oc in range(4):
                    ins = g.dma_start(
                        out=w_sb[:, oc * D:(oc + 1) * D].rearrange("p (d c) -> p d c", c=128),
                        in_=w_in[oc])
                    self.inc(ins, "w", 16)
            ins = g.dma_start(
                out=self.wv_sb[:].rearrange("p (d c) -> p d c", c=512), in_=self.wv)
            self.inc(ins, "w", 16)
            self.c_wqkv = self.cnt["w"]
            g.wait_ge(self.sems["dve"], self.c_rwpad_ms)
            ins = g.dma_start(out=rw_pad[0:8, :], in_=self.rw)
            self.inc(ins, "w", 16)
            self.c_rw = self.cnt["w"]

        def ve(v):
            v.memset(self.onesb[:], 1.0)
            v.memset(self.eps_c[:], EPS)
            ins = v.memset(rw_pad[:], 0.0)
            self.c_rwpad_ms = self.inc(ins, "dve")

        self.blk.vector(ve)
        self.blk.gpsimd(gp)

        pstr = [self.ps(e1, f"p0tr{i}", [128, 128], BF16) for i in range(2)]
        tr_pe = []
        rwT_evac = [self.cnt["dve"] + j + 1 for j in range(ND)]

        def pe(t):
            t.wait_ge(self.sems["w"], self.c_rw)
            for d in range(ND):
                if d >= 2:
                    self.w(t, "dve", rwT_evac[d - 2])
                ins = t.transpose(pstr[d % 2][:], rw_pad[:, 128 * d:128 * (d + 1)], self.identb[:])
                tr_pe.append(self.inc(ins, "pe"))

        def ve2(v):
            for d in range(ND):
                self.w(v, "pe", tr_pe[d])
                ins = v.tensor_copy(self.rwT[:, 8 * d:8 * (d + 1)], pstr[d % 2][:, 0:8])
                assert self.inc(ins, "dve") == rwT_evac[d]

        self.blk.tensor(pe)
        self.blk.vector(ve2)
        self.c_rwT_done = rwT_evac[-1]

    def p1(self, es):
        nc = self.nc
        NX = 8
        NL = 5
        NS = 12
        xa = [self.sb(es, f"xa{i}", [128, D]) for i in range(NX)]
        junk = self.sb(es, "junk", [128, D])
        ln_tm = [self.sb(es, f"lntm{i}", [128, D], BF16) for i in range(NL)]
        lnT = [self.sb(es, f"lnT{i}", [128, ND * 512], BF16) for i in range(2)]
        stage = [self.sb(es, f"qst{i}", [128, 512], BF16) for i in range(NS)]
        st_sr = self.sb(es, "st_sr", [128, 8])
        st_nm = self.sb(es, "st_nm", [128, 8])
        st_sq = self.sb(es, "st_sq", [128, 8])
        st_sd = self.sb(es, "st_sd", [128, 8])
        st_rs = self.sb(es, "st_rs", [128, 8])
        ps_tr = [self.ps(es, f"p1tr{i}", [128, 1024], BF16) for i in range(2)]
        ps_mm = [self.ps(es, f"p1mm{i}", [128, 512], F32) for i in range(4)]

        NG = 16
        xload = [0] * 64
        norm_done = [0] * 64
        trans_pe = {}
        trans_done = [0] * 64
        tr_evac_bank = [0, 0]
        te_last = [0] * NG
        mm_done = [0] * NG
        mm_cnt = {}
        evac_cnt = {}
        store_done = {}
        stats_war = [0, 0]
        mats = [(self.wq_sb, self.qT_d), (self.wk_sb, self.kT_d)]
        self.p1_store_last = 0

        def sync_loads(g):
            if g >= NG:
                return
            def f(sy):
                for k in range(4):
                    tt = 4 * g + k
                    slot = tt % NX
                    if tt >= NX:
                        self.w(sy, "dve", norm_done[tt - NX])
                    ins = sy.dma_start(out=xa[slot][:], in_=self.x[128 * tt:128 * (tt + 1), :])
                    xload[tt] = self.inc(ins, f"r{tt % NX}", 16)
            self.blk.sync(f)

        def vec_a(g):
            p = g % 2
            def f(v):
                for k in range(4):
                    tt = 4 * g + k
                    self.w(v, f"r{tt % NX}", xload[tt])
                if g >= 2:
                    self.w(v, "act", stats_war[p])
                last = None
                for k in range(4):
                    c = 4 * p + k
                    last = v.reduce_sum(st_sr[:, c:c + 1], xa[(4 * g + k) % NX][:], axis=AX)
                self.vfence(v, last)
                for k in range(4):
                    c = 4 * p + k
                    last = v.tensor_scalar_mul(st_nm[:, c:c + 1], st_sr[:, c:c + 1], -1.0 / D)
                self.vfence(v, last)
                self.ln_stats = self.inc(v.engine_nop(), "dve")
            self.blk.vector(f)

        def sca_a(g):
            p = g % 2
            def f(s):
                self.w(s, "dve", self.ln_stats)
                last = None
                for k in range(4):
                    c = 4 * p + k
                    last = s.activation(junk[:], xa[(4 * g + k) % NX][:], AF.Square,
                                        bias=st_nm[:, c:c + 1], accum_out=st_sq[:, c:c + 1])
                    self.sfence(s, last)
                for k in range(4):
                    c = 4 * p + k
                    last = s.activation(st_sd[:, c:c + 1], st_sq[:, c:c + 1], AF.Sqrt,
                                        scale=1.0 / D, bias=self.eps_c[:])
                self.sqrt_done = self.inc(last, "act")
                stats_war[p] = self.sqrt_done
            self.blk.scalar(f)

        def vec_b(g):
            p = g % 2
            def f(v):
                self.w(v, "act", self.sqrt_done)
                last = None
                for k in range(4):
                    c = 4 * p + k
                    last = v.reciprocal(st_rs[:, c:c + 1], st_sd[:, c:c + 1])
                self.vfence(v, last)
                for k in range(4):
                    tt = 4 * g + k
                    c = 4 * p + k
                    if tt >= NL:
                        self.w(v, "pe", trans_done[tt - NL])
                    ins = v.tensor_scalar(ln_tm[tt % NL][:], xa[tt % NX][:],
                                          st_nm[:, c:c + 1], st_rs[:, c:c + 1],
                                          ALU.add, ALU.mult)
                    norm_done[tt] = self.inc(ins, "dve")
            self.blk.vector(f)

        def pe_trans(g):
            ev_base = self.cnt["dve"]
            ev_plan = {d: ev_base + d + 1 for d in range(ND)}
            def f(t):
                if g == 0:
                    t.wait_ge(self.sems["w"], self.c_rw)
                for d in range(ND):
                    bank = d % 2
                    for k in range(4):
                        tt = 4 * g + k
                        self.w(t, "dve", norm_done[tt])
                        if d >= 2:
                            self.w(t, "dve", ev_plan[d - 2])
                        else:
                            self.w(t, "dve", tr_evac_bank[bank])
                        ins = t.transpose(ps_tr[bank][:, 128 * k:128 * (k + 1)],
                                          ln_tm[tt % NL][:, 128 * d:128 * (d + 1)],
                                          self.identb[:])
                        trans_pe[(tt, d)] = self.inc(ins, "pe")
                for k in range(4):
                    trans_done[4 * g + k] = self.cnt["pe"]
            self.blk.tensor(f)
            return ev_plan

        def vec_evac(g, ev_plan):
            pbuf = g % 2
            def f(v):
                if g >= 2:
                    self.w(v, "pe", mm_done[g - 2])
                for d in range(ND):
                    self.w(v, "pe", trans_pe[(4 * g + 3, d)])
                    ins = v.tensor_copy(lnT[pbuf][:, 512 * d: 512 * (d + 1)],
                                        ps_tr[d % 2][:, 0:512])
                    assert self.inc(ins, "dve") == ev_plan[d]
                    tr_evac_bank[d % 2] = ev_plan[d]
                te_last[g] = self.cnt["dve"]
            self.blk.vector(f)

        def pe_mm(g, ev_plan):
            pbuf = g % 2
            ev_base = self.cnt["act"]
            off = 1 if g + 1 < NG else 0
            for j in range(12):
                evac_cnt[g * 12 + j] = ev_base + j + 1 + (off if j >= 2 else 0)
            def f(t):
                for j in range(12):
                    bk = (g * 12 + j) % 4
                    prev = (g * 12 + j - 4)
                    if prev >= 0:
                        self.w(t, "act", evac_cnt.get(prev, 0))
                    if j < 8:
                        m, oc = j // 4, j % 4
                        w_sb = mats[m][0]
                        for d in range(ND):
                            self.w(t, "dve", ev_plan[d])
                            ins = t.matmul(ps_mm[bk][:],
                                           w_sb[:, oc * D + 128 * d: oc * D + 128 * (d + 1)],
                                           lnT[pbuf][:, 512 * d: 512 * (d + 1)],
                                           start=(d == 0), stop=(d == ND - 1))
                    else:
                        k = j - 8
                        for d in range(ND):
                            self.w(t, "dve", ev_plan[d])
                            ins = t.matmul(ps_mm[bk][:],
                                           lnT[pbuf][:, 512 * d + 128 * k:
                                               512 * d + 128 * (k + 1)],
                                           self.wv_sb[:, 512 * d: 512 * (d + 1)],
                                           start=(d == 0), stop=(d == ND - 1))
                    mm_cnt[(g, j)] = self.inc(ins, "pe")
                mm_done[g] = self.cnt["pe"]
            self.blk.tensor(f)

        def sca_evac(g, js):
            def f(s):
                for j in js:
                    gi = g * 12 + j
                    slot = gi % NS
                    bk = gi % 4
                    self.w(s, "pe", mm_cnt[(g, j)])
                    if gi >= NS:
                        self.w(s, f"r{8 + (gi - NS) % NS}", store_done[gi - NS])
                    ins = s.activation(stage[slot][:], ps_mm[bk][:], AF.Copy)
                    assert self.inc(ins, "act") == evac_cnt[gi], (g, j)
            self.blk.scalar(f)

        def sync_store(g):
            def f(sy):
                for j in range(12):
                    gi = g * 12 + j
                    slot = gi % NS
                    self.w(sy, "act", evac_cnt[gi])
                    if j < 8:
                        m, oc = j // 4, j % 4
                        ins = sy.dma_start(
                            out=mats[m][1][128 * oc:128 * (oc + 1), 512 * g:512 * (g + 1)],
                            in_=stage[slot][:])
                    else:
                        k = j - 8
                        tt = 4 * g + k
                        for hl in range(4):
                            ins = sy.dma_start(
                                out=self.vT_d[hl][:, 128 * tt:128 * (tt + 1)],
                                in_=stage[slot][:, 128 * hl:128 * (hl + 1)])
                            if hl < 3:
                                self.inc(ins, f"r{8 + gi % NS}", 16)
                    store_done[gi] = self.inc(ins, f"r{8 + gi % NS}", 16)
            self.blk.sync(f)

        sync_loads(0); sync_loads(1)
        vec_a(0); sca_a(0); vec_b(0)
        evp = {0: pe_trans(0)}
        vec_evac(0, evp[0])
        for g in range(NG):
            pe_mm(g, evp[g])
            sca_evac(g, range(2))
            if g + 1 < NG:
                sync_loads(g + 2)
                vec_a(g + 1); sca_a(g + 1); vec_b(g + 1)
                evp[g + 1] = pe_trans(g + 1)
                vec_evac(g + 1, evp[g + 1])
            sca_evac(g, range(2, 12))
            sync_store(g)
        self.p1_store_final = {f"r{8 + i}": self.cnt[f"r{8 + i}"] for i in range(NS)}
        self.p1_eng_final = {e: self.cnt[e] for e in ("pe", "dve", "act")}

    def p2(self, es):
        nc = self.nc
        qT = self.sb(es, "qT", [128, T], BF16)
        kT = self.sb(es, "kT", [128, T], BF16)
        v_tm = self.sb(es, "v_tm", [128, T], BF16)
        wT = [self.sb(es, f"wT{i}", [128, 1024], BF16) for i in range(2)]
        ctx_sb = [self.sb(es, f"ctx{i}", [128, 1024], BF16) for i in range(2)]
        lnD_sb = self.sb(es, "lnD", [128, 1024])
        recip_sb = self.sb(es, "recip", [128, 1024])
        psA = self.ps(es, "psA", [128, 1024], F32)
        psB = self.ps(es, "psB", [128, 1024], F32)
        ps_sums = self.ps(es, "ps_sums", [128, 1024], F32)
        ps_ctx = self.ps(es, "ps_ctx", [128, 1024], F32)
        vtA = [psA[:, 0:64].bitcast(BF16), psA[:, 512:576].bitcast(BF16)]

        NQC = 8
        self.p2_gates = (self.cnt["act"], self.cnt["dve"])
        st = dict(q_ld=0, k_ld=0, vt_pe={}, sc={}, sp={}, lnD=0,
                  ctxm=[0, 0], last_pv=0, ctr_pe={}, sc_last={},
                  exp_last_prev=0, sp_prev2=None, vt_evac_last=0,
                  ctxm_last_head=0, recip=0)
        self.scr_evac_last = 0

        def emit_ctx_evac(v, h, qc, hl2):
            g = qc // 2
            scrT = self.scrT[h // 2]
            for dloc in range(8):
                dchunk = 8 * (qc % 2) + dloc
                self.w(v, "pe", st["ctr_pe"][(qc, dloc)])
                ins = v.tensor_copy(
                    scrT[:, dchunk * THALF + 512 * hl2 + 128 * g:
                         dchunk * THALF + 512 * hl2 + 128 * g + 128],
                    vtA[dloc % 2][:])
                self.scr_evac_last = self.inc(ins, "dve")

        def loads(h):
            def f(sy):
                if h == 0:
                    for k2, v2 in self.p1_store_final.items():
                        self.w(sy, k2, v2)
                else:
                    self.w(sy, "pe", st["sc_last"][h - 1])
                ins = sy.dma_start(out=qT[:], in_=self.qT_d[128 * h:128 * (h + 1), :])
                st["q_ld"] = self.inc(ins, "r20", 16)
                ins = sy.dma_start(out=kT[:], in_=self.kT_d[128 * h:128 * (h + 1), :])
                st["k_ld"] = self.inc(ins, "r21", 16)
                if h > 0:
                    self.w(sy, "pe", st["last_pv"])
                ins = sy.dma_start(out=v_tm[:], in_=self.vT_d[h])
                st["v_ld"] = self.inc(ins, "r22", 16)
            self.blk.sync(f)

        def pe_chunk(h, qc, hl2):
            act_base = self.cnt["act"]
            exp_plan = {j: act_base + j + 1 for j in range(64)}
            dve_base = self.cnt["dve"]
            ctx_evac_plan = {dloc: dve_base + dloc + 1 for dloc in range(8)}
            prev_scr_last = self.scr_evac_last

            def f(t):
                if qc > 0:
                    self.w(t, "dve", st["ctxm"][(qc - 1) % 2])
                    for dloc in range(8):
                        if dloc >= 2:
                            self.w(t, "dve", ctx_evac_plan[dloc - 2])
                        else:
                            self.w(t, "act", st["exp_last_prev"])
                            self.w(t, "dve", prev_scr_last)
                        ins = t.transpose(vtA[dloc % 2][:],
                                          ctx_sb[(qc - 1) % 2][:, 128 * dloc:128 * (dloc + 1)],
                                          self.identb[:])
                        st["ctr_pe"][(qc - 1, dloc)] = self.inc(ins, "pe")
                def emit_scores(j):
                    psS = psA if j % 2 == 0 else psB
                    if j <= 1:
                        self.w(t, "act", st["exp_last_prev"])
                        if qc == 0:
                            if j == 0:
                                self.w(t, "r20", st["q_ld"])
                                self.w(t, "r21", st["k_ld"])
                                if h > 0:
                                    self.w(t, "dve", prev_scr_last)
                        elif j == 0:
                            self.w(t, "dve", ctx_evac_plan[7])
                    else:
                        self.w(t, "act", exp_plan[j - 2])
                    ins = None
                    for n in range(2):
                        ins = t.matmul(psS[:, 512 * n:512 * (n + 1)],
                                       kT[:, 128 * j:128 * (j + 1)],
                                       qT[:, 1024 * qc + 512 * n:1024 * qc + 512 * (n + 1)],
                                       start=True, stop=True)
                    st["sc"][j] = self.inc(ins, "pe")

                emit_scores(0)
                emit_scores(1)
                for j in range(64):
                    self.w(t, "act", exp_plan[j])
                    if j == 0:
                        if qc == 0:
                            self.w(t, "r22", st["v_ld"])
                            if st["lnD"]:
                                self.w(t, "act", st["lnD"])
                            self.w(t, "dve", st["ctxm_last_head"])
                        else:
                            if st["lnD"]:
                                self.w(t, "act", st["lnD"])
                            self.w(t, "dve", st["ctxm"][(qc - 1) % 2])
                    for n in range(2):
                        t.matmul(ps_sums[:, 512 * n:512 * (n + 1)], self.onesb[:],
                                 wT[j % 2][:, 512 * n:512 * (n + 1)],
                                 start=(j == 0), stop=(j == 63))
                    for n in range(2):
                        ins = t.matmul(ps_ctx[:, 512 * n:512 * (n + 1)],
                                       v_tm[:, 128 * j:128 * (j + 1)],
                                       wT[j % 2][:, 512 * n:512 * (n + 1)],
                                       start=(j == 0), stop=(j == 63))
                    st["sp"][j] = self.inc(ins, "pe")
                    if j + 2 < 64:
                        emit_scores(j + 2)
                st["sc_last"][h] = st["sc"][63]
            self.blk.tensor(f)

            def fa(s):
                for j in range(64):
                    self.w(s, "pe", st["sc"][j])
                    if j >= 2:
                        self.w(s, "pe", st["sp"][j - 2])
                    elif st["sp_prev2"] is not None:
                        self.w(s, "pe", st["sp_prev2"][j])
                    ins = s.activation(wT[j % 2][:],
                                       (psA if j % 2 == 0 else psB)[:],
                                       AF.Exp, scale=ISC)
                    assert self.inc(ins, "act") == exp_plan[j]
                self.w(s, "pe", st["sp"][63])
                ins = s.activation(lnD_sb[:], ps_sums[:], AF.Ln)
                st["lnD"] = self.inc(ins, "act")
                self.sfence(s, ins) if False else None
                s.wait_ge(self.sems["act"], st["lnD"])
                ins = s.activation(recip_sb[:], lnD_sb[:], AF.Exp, scale=-1.0)
                st["recip"] = self.inc(ins, "act")
                st["exp_last_prev"] = exp_plan[63]
                st["sp_prev2"] = {0: st["sp"][62], 1: st["sp"][63]}
            self.blk.scalar(fa)

            def fv(v):
                if qc > 0:
                    emit_ctx_evac(v, h, qc - 1, hl2)
                self.w(v, "act", st["recip"])
                self.w(v, "pe", st["sp"][63])
                if qc >= 2:
                    self.w(v, "pe", st["ctr_pe"][(qc - 2, 7)])
                elif st["ctxm_last_head"] and qc == 0 and h > 0:
                    self.w(v, "pe", st["ctr_pe"][(7, 7)])
                elif qc == 1:
                    pass
                ins = v.tensor_mul(ctx_sb[qc % 2][:], ps_ctx[:], recip_sb[:])
                st["ctxm"][qc % 2] = self.inc(ins, "dve")
            self.blk.vector(fv)

        def pe_tail(h, hl2):
            dve_base = self.cnt["dve"]
            evac_plan = {dloc: dve_base + dloc + 1 for dloc in range(8)}
            prev_scr_last = self.scr_evac_last
            def f(t):
                self.w(t, "dve", st["ctxm"][1])
                for dloc in range(8):
                    if dloc >= 2:
                        self.w(t, "dve", evac_plan[dloc - 2])
                    else:
                        self.w(t, "act", st["exp_last_prev"])
                        self.w(t, "dve", prev_scr_last)
                    ins = t.transpose(vtA[dloc % 2][:],
                                      ctx_sb[1][:, 128 * dloc:128 * (dloc + 1)],
                                      self.identb[:])
                    st["ctr_pe"][(7, dloc)] = self.inc(ins, "pe")
            self.blk.tensor(f)
            def fv(v):
                emit_ctx_evac(v, h, 7, hl2)
            self.blk.vector(fv)

        for h in range(4):
            hl2 = h % 2
            loads(h)
            for qc in range(NQC):
                pe_chunk(h, qc, hl2)
            pe_tail(h, hl2)
            st["ctxm_last_head"] = st["ctxm"][1]
            st["last_pv"] = st["sp"][63]

        if "scr" in self.dbg:
            last = self.scr_evac_last
            def fd(sy):
                self.w(sy, "dve", last)
                self.inc(sy.dma_start(out=self.y_scr[0], in_=self.scrT[0][:]), "x", 16)
                self.inc(sy.dma_start(out=self.y_scr[1], in_=self.scrT[1][:]), "x", 16)
            self.blk.sync(fd)

        p2_last = self.scr_evac_last
        def fsp(sy):
            self.w(sy, "dve", p2_last)
            self.spill_done = self.inc(sy.dma_start(out=self.scr1_d, in_=self.scrT[1][:]),
                                       "r18", 16)
        self.blk.sync(fsp)
        self.p2_last = p2_last

    def p3(self, es, hf):
        nc = self.nc
        g_pe, g_act, g_dve = self.cnt["pe"], self.cnt["act"], self.cnt["dve"]
        if hf == 0:
            scrT = self.scrT[0]
            gate_r18 = self.spill_done
        else:
            scrT = self.sb(es, "scrT1b", [128, ND * THALF], BF16)
            gate_r18 = 0
        wo_sb = [self.sb(es, f"wo{i}", [128, D], BF16) for i in range(ND)]
        bo_sb = self.sb(es, "bo_sb", [1, D], F32)
        psP = [self.ps(es, f"psP{i}", [128, D], F32) for i in range(2)]

        st = dict(ld={}, mm={}, add={})
        self.p3_add = {}

        if hf == 1:
            def frl(sy):
                self.w(sy, "dve", g_dve)
                self.w(sy, "pe", g_pe)
                st["rl"] = self.inc(sy.dma_start(out=scrT[:], in_=self.scr1_d), "r18", 16)
            self.blk.sync(frl)

        def init_s1(sy):
            self.w(sy, "dve", g_dve)
            st["bo"] = self.inc(sy.dma_start(out=bo_sb[:], in_=self.bo_row), "r16", 16)
        self.blk.sync(init_s1)

        def init_s1g(g):
            if hf == 0:
                for e, c in self.p1_eng_final.items():
                    self.w(g, e, c)
                for k2, v2 in self.p1_store_final.items():
                    self.w(g, k2, v2)
            else:
                self.w(g, "o", self.out_done_half0)
            for ut in range(8):
                base = 1024 * hf + 512 * (ut // 4) + (ut % 4)
                ins = g.dma_start(out=self.S1h[ut][:],
                                  in_=self.xres[base: base + 509: 4, :])
                self.inc(ins, "w", 16)
            st["s1all"] = self.cnt["w"]
        self.blk.gpsimd(init_s1g)

        def loads(g):
            for d in range(ND):
                if d < 2:
                    self.w(g, "dve", g_dve)
                    self.w(g, "pe", g_pe)
                    if gate_r18:
                        self.w(g, "r18", gate_r18)
                ins = g.dma_start(out=wo_sb[d][:], in_=self.wo[d])
                st["ld"][d] = self.inc(ins, f"g{d}", 16)
        self.blk.gpsimd(loads)

        def pe_mm(ut):
            def f(t):
                if ut == 0:
                    self.w(t, "r16", st["bo"])
                    if hf == 1:
                        self.w(t, "r18", st["rl"])
                if ut < 2:
                    self.w(t, "dve", g_dve)
                    self.w(t, "act", g_act)
                else:
                    self.w(t, "dve", st["add"][ut - 2])
                ins = None
                for n in range(4):
                    ins = t.matmul(psP[ut % 2][:, 512 * n:512 * (n + 1)],
                                   self.onesf_sb[:], bo_sb[0:1, 512 * n:512 * (n + 1)],
                                   start=True, stop=False)
                for d in range(ND):
                    if ut == 0:
                        self.w(t, f"g{d}", st["ld"][d])
                    for n in range(4):
                        ins = t.matmul(psP[ut % 2][:, 512 * n:512 * (n + 1)],
                                       scrT[:, d * THALF + 128 * ut:d * THALF + 128 * (ut + 1)],
                                       wo_sb[d][:, 512 * n:512 * (n + 1)],
                                       start=False, stop=(d == ND - 1))
                st["mm"][ut] = self.inc(ins, "pe")
            self.blk.tensor(f)

        def dve_add(ut):
            def f(v):
                self.w(v, "pe", st["mm"][ut])
                self.w(v, "w", st["s1all"])
                ins = v.tensor_tensor(self.S1h[ut][:], self.S1h[ut][:],
                                      psP[ut % 2][:], ALU.add)
                st["add"][ut] = self.inc(ins, "dve")
                self.p3_add[ut] = st["add"][ut]
            self.blk.vector(f)

        for ut in range(8):
            pe_mm(ut)
            dve_add(ut)
        self.p3_last = self.cnt["dve"]

        if "s1" in self.dbg and hf == 0:
            last = self.p3_last
            def fd(sy):
                self.w(sy, "dve", last)
                for ut in range(8):
                    self.inc(sy.dma_start(out=self.y_s1[ut], in_=self.S1h[ut][:]), "x", 16)
            self.blk.sync(fd)

    def ffn_half(self, es, hf):
        nc = self.nc
        g_pe, g_act, g_dve = self.cnt["pe"], self.cnt["act"], self.cnt["dve"]
        self.gate3 = (g_pe, g_act, g_dve)
        ln2T = self.sb(es, "ln2T", [128, ND * THALF], BF16)
        self.ln2T_ap = ln2T
        silu_sb = self.sb(es, "silu_sb", [128, THALF])

        with ExitStack() as e45:
            ln2tm = [self.sb(e45, f"ln2tm{i}", [128, D], BF16) for i in range(8)]
            junk2 = self.sb(e45, "junk2", [128, D])
            s2 = self.sb(e45, "st2", [128, 40])
            pstr = [self.ps(e45, f"p4tr{i}", [128, 1024], BF16) for i in range(4)]
            st = dict(norm={}, tr={}, tr_last=0, lnst=0)

            tr_plan_base = None

            chains = {}
            for grp in range(4):
                uts = (2 * grp, 2 * grp + 1)
                def ve_a(v, uts=uts):
                    for ut in uts:
                        self.w(v, "dve", self.p3_add[ut])
                    last = None
                    for ut in uts:
                        last = v.reduce_sum(s2[:, ut:ut + 1], self.S1h[ut][:], axis=AX)
                    self.vfence(v, last)
                    for ut in uts:
                        last = v.tensor_scalar_mul(s2[:, 8 + ut:9 + ut], s2[:, ut:ut + 1],
                                                   -1.0 / D)
                    self.vfence(v, last)
                    st["lnst"] = self.inc(v.engine_nop(), "dve")

                def sc_a(s, uts=uts):
                    self.w(s, "dve", st["lnst"])
                    for ut in uts:
                        ins = s.activation(junk2[:], self.S1h[ut][:], AF.Square,
                                           bias=s2[:, 8 + ut:9 + ut],
                                           accum_out=s2[:, 16 + ut:17 + ut])
                        self.sfence(s, ins)
                    last = None
                    for ut in uts:
                        last = s.activation(s2[:, 24 + ut:25 + ut], s2[:, 16 + ut:17 + ut],
                                            AF.Sqrt, scale=1.0 / D, bias=self.eps_c[:])
                    st["sqrt"] = self.inc(last, "act")

                def ve_b(v, uts=uts):
                    self.w(v, "act", st["sqrt"])
                    last = None
                    for ut in uts:
                        last = v.reciprocal(s2[:, 32 + ut:33 + ut], s2[:, 24 + ut:25 + ut])
                    self.vfence(v, last)
                    for ut in uts:
                        ins = v.tensor_scalar(ln2tm[ut][:], self.S1h[ut][:],
                                              s2[:, 8 + ut:9 + ut], s2[:, 32 + ut:33 + ut],
                                              ALU.add, ALU.mult)
                        st["norm"][ut] = self.inc(ins, "dve")
                chains[grp] = (ve_a, sc_a, ve_b)
            dve_base4 = self.cnt["dve"]
            evac_plan = {}
            _c = dve_base4 + 1
            for ut in range(8):
                if ut % 2 == 0:
                    _c += 3
                for d in range(ND):
                    evac_plan[(ut, d)] = _c + d + 1
                _c += 16 + (1 if ut >= 1 else 0) + 2 + 1

            pad = self.sb(e45, "pad", [128, 128])
            lg = [self.sb(e45, f"lg{i}", [128, 8]) for i in range(2)]
            top8 = [self.sb(e45, f"top8{i}", [128, 8]) for i in range(2)]
            cstat = [self.sb(e45, f"cstat{i}", [128, 8]) for i in range(2)]
            sel = [self.sb(e45, f"sel{i}", [128, 8]) for i in range(2)]
            e1t = [self.sb(e45, f"e1t{i}", [128, 8]) for i in range(2)]
            comb = [self.sb(e45, f"comb{i}", [128, 8]) for i in range(2)]
            ps_lgF = [self.ps(e45, f"ps_lg{i}", [128, 512], F32) for i in range(2)]
            ps_lg = [p[:, 0:8] for p in ps_lgF]
            ps_ptF = self.ps(e45, "ps_pt", [128, 512], F32)
            ps_pt = ps_ptF[:, 0:128]
            rst = dict(mm={}, evac={}, e1={}, sel={}, combw={}, ptr={}, pev={})

            def ms(v):
                ins = v.memset(pad[:], 0.0)
                rst["ms"] = self.inc(ins, "dve")
                assert rst["ms"] == dve_base4 + 1
            self.blk.vector(ms)

            for ut in range(8):
                if ut % 2 == 0:
                    ve_a_f, sc_a_f, ve_b_f = chains[ut // 2]
                    self.blk.vector(ve_a_f)
                    self.blk.scalar(sc_a_f)
                    self.blk.vector(ve_b_f)
                def f_tr(t, ut=ut):
                    self.w(t, "dve", st["norm"][ut])
                    if ut == 0:
                        self.w(t, "dve", g_dve)
                        self.w(t, "act", g_act)
                    for d in range(ND):
                        j = ut * ND + d
                        if j >= 4:
                            self.w(t, "dve", evac_plan[((j - 4) // ND, (j - 4) % ND)])
                        ins = t.transpose(pstr[j % 4][:, 0:128],
                                          ln2tm[ut][:, 128 * d:128 * (d + 1)],
                                          self.identb[:])
                        st["tr"][(ut, d)] = self.inc(ins, "pe")
                    if ut >= 1:
                        self.w(t, "dve", rst["combw"][ut - 1])
                        if ut >= 2:
                            self.w(t, "dve", rst["pev"][ut - 2])
                        else:
                            self.w(t, "dve", g_dve)
                        ins = t.transpose(ps_pt, pad[:], self.identf_sb[:])
                        rst["ptr"][ut - 1] = self.inc(ins, "pe")
                    ins = None
                    if ut >= 2:
                        self.w(t, "dve", rst["evac"][ut - 2])
                    for d in range(ND):
                        self.w(t, "dve", evac_plan[(ut, d)])
                        ins = t.matmul(ps_lg[ut % 2],
                                       ln2T[:, d * THALF + 128 * ut:d * THALF + 128 * (ut + 1)],
                                       self.rwT[:, 8 * d:8 * (d + 1)],
                                       start=(d == 0), stop=(d == ND - 1))
                    rst["mm"][ut] = self.inc(ins, "pe")
                self.blk.tensor(f_tr)

                def f_evac(v, ut=ut):
                    for d in range(ND):
                        j = ut * ND + d
                        self.w(v, "pe", st["tr"][(ut, d)])
                        ins = v.tensor_copy(ln2T[:, d * THALF + 128 * ut:
                                                 d * THALF + 128 * (ut + 1)],
                                            pstr[j % 4][:, 0:128])
                        assert self.inc(ins, "dve") == evac_plan[(ut, d)], (ut, d)
                    if ut >= 1:
                        self.w(v, "pe", rst["ptr"][ut - 1])
                        ins = v.tensor_copy(self.combT[:, 128 * (ut - 1):128 * ut],
                                            ps_pt)
                        rst["pev"][ut - 1] = self.inc(ins, "dve")
                self.blk.vector(f_evac)

                def f_ve1(v, ut=ut):
                    p = ut % 2
                    self.w(v, "pe", rst["mm"][ut])
                    ins = v.tensor_copy(lg[p][:], ps_lg[p])
                    rst["evac"][ut] = self.inc(ins, "dve")
                    v.wait_ge(self.sems["dve"], rst["evac"][ut])
                    ins = v.max(out=top8[p][:], in_=lg[p][:])
                    self.vfence(v, ins)
                    ins = v.tensor_single_scalar(cstat[p][:, 0:1], top8[p][:, 2:3],
                                                 top8[p][:, 1:2], ALU.add)
                    self.vfence(v, ins)
                    ins = v.tensor_scalar_mul(cstat[p][:, 0:1], cstat[p][:, 0:1], 0.5)
                    ins2 = v.tensor_scalar_mul(cstat[p][:, 1:2], top8[p][:, 0:1], -1.0)
                    self.vfence(v, ins2)
                    ins = v.tensor_scalar_sub(sel[p][:], lg[p][:], cstat[p][:, 0:1])
                    self.vfence(v, ins)
                    ins = v.tensor_scalar_mul(sel[p][:], sel[p][:], 1e7)
                    self.vfence(v, ins)
                    ins = v.tensor_scalar_min(sel[p][:], sel[p][:], 1.0)
                    self.vfence(v, ins)
                    ins = v.tensor_scalar_max(sel[p][:], sel[p][:], 0.0)
                    rst["sel"][ut] = self.inc(ins, "dve")
                self.blk.vector(f_ve1)

                def f_sc(s, ut=ut):
                    p = ut % 2
                    self.w(s, "dve", rst["sel"][ut])
                    ins = s.activation(e1t[p][:], lg[p][:], AF.Exp, bias=cstat[p][:, 1:2])
                    rst["e1"][ut] = self.inc(ins, "act")
                self.blk.scalar(f_sc)

                def f_ve2(v, ut=ut):
                    p = ut % 2
                    self.w(v, "act", rst["e1"][ut])
                    ins = v.tensor_mul(e1t[p][:], e1t[p][:], sel[p][:])
                    self.vfence(v, ins)
                    ins = v.reduce_sum(cstat[p][:, 2:3], e1t[p][:], axis=AX)
                    self.vfence(v, ins)
                    ins = v.reciprocal(cstat[p][:, 3:4], cstat[p][:, 2:3])
                    self.vfence(v, ins)
                    ins = v.tensor_scalar_mul(comb[p][:], e1t[p][:], cstat[p][:, 3:4])
                    self.vfence(v, ins)
                    if ut == 0:
                        self.w(v, "dve", rst["ms"])
                    if ut >= 1:
                        self.w(v, "pe", rst["ptr"][ut - 1])
                    ins = v.tensor_copy(pad[:, 0:8], comb[p][:])
                    rst["combw"][ut] = self.inc(ins, "dve")
                self.blk.vector(f_ve2)

            def f_pe2_last(t):
                self.w(t, "dve", rst["combw"][7])
                self.w(t, "dve", rst["pev"][6])
                ins = t.transpose(ps_pt, pad[:], self.identf_sb[:])
                rst["ptr"][7] = self.inc(ins, "pe")
            self.blk.tensor(f_pe2_last)

            def f_ve3_last(v):
                self.w(v, "pe", rst["ptr"][7])
                ins = v.tensor_copy(self.combT[:, 128 * 7:128 * 8], ps_pt)
                rst["pev"][7] = self.inc(ins, "dve")
            self.blk.vector(f_ve3_last)
            self.comb_done = self.cnt["dve"]
            self.ln2_done = self.comb_done
            self.p4_act_last = self.cnt["act"]
            self.p5_act_last = self.cnt["act"]
            if "ffn" in self.dbg and hf == 0:
                cd = self.comb_done
                def fdmp(sy):
                    self.w(sy, "dve", cd)
                    self.inc(sy.dma_start(
                        out=self.y_ln2,
                        in_=ln2T[:].rearrange("p (d u) -> d p u", u=THALF)), "x", 16)
                    self.inc(sy.dma_start(out=self.y_cmb, in_=self.combT[:]), "x", 16)
                self.blk.sync(fdmp)

        with ExitStack() as e6:
            A = self.sb(e6, "A_sb", [128, NI * THALF], BF16)
            wd_sb = self.sb(e6, "wd_sb", [128, NI * D], BF16)
            crow = self.sb(e6, "crow", [1, THALF])
            cb_sb = self.sb(e6, "cb_sb", [128, THALF])
            gw_r = [self.sb(e6, f"gw{i}", [128, D], BF16) for i in range(2)]
            uw_r = [self.sb(e6, f"uw{i}", [128, D], BF16) for i in range(2)]
            psG = [self.ps(e6, f"psG{i}", [128, THALF], F32) for i in range(2)]
            psU = [self.ps(e6, f"psU{i}", [128, THALF], F32) for i in range(2)]
            prev = dict(silu={}, t2={}, dadd_last=0, dmm_all=0, cbe=0)
            for e in range(NE):
                self._expert(hf, e, crow, cb_sb, silu_sb, A, wd_sb, gw_r, uw_r,
                             psG, psU, prev)

        with ExitStack() as e7:
            A7 = self.sb(e7, "A7_sb", [128, NSI * THALF], BF16)
            u7 = self.sb(e7, "u7", [128, THALF])
            psG = [self.ps(e7, f"psG7{i}", [128, THALF], F32) for i in range(2)]
            psU = [self.ps(e7, f"psU7{i}", [128, THALF], F32) for i in range(2)]
            self._p7(hf, e7, silu_sb, A7, u7, psG, psU, prev)

        def fout(s):
            for ut in range(8):
                self.w(s, "dve", self.sh_add_last[ut])
                base = 1024 * hf + 512 * (ut // 4) + (ut % 4)
                ins = s.dma_start(out=self.out[base: base + 509: 4, :],
                                  in_=self.S1h[ut][:])
                self.inc(ins, "o", 16)
        self.blk.scalar(fout)
        if hf == 0:
            self.out_done_half0 = self.cnt["o"]

    def _expert(self, hf, e, crow, cb_sb, silu_sb, A, wd_sb, gw_r, uw_r,
                psG, psU, prev):
        st6 = dict(gld={}, uld={}, dld={}, gmm={}, umm={}, silu={}, t2={},
                   dmm={}, dadd={})
        g_pe, g_act, g_dve = self.gate3

        def f_crow(sy):
            if e >= 1:
                self.w(sy, "dve", prev["cbe"])
            else:
                self.w(sy, "dve", self.comb_done)
            ins = sy.dma_start(out=crow[:], in_=self.combT[e:e + 1, :])
            st6["crow_ld"] = self.inc(ins, "r17", 16)
        self.blk.sync(f_crow)

        def f_cb(t):
            self.w(t, "r17", st6["crow_ld"])
            if e == 0:
                self.w(t, "dve", self.comb_done)
                self.w(t, "act", self.p5_act_last)
            else:
                self.w(t, "dve", prev["dadd_last"])
            ins = None
            for n in range(2):
                ins = t.matmul(psG[(NI - 1) % 2][:, 512 * n:512 * (n + 1)],
                               self.onesf_sb[:], crow[0:1, 512 * n:512 * (n + 1)],
                               start=True, stop=True)
            st6["cbmm"] = self.inc(ins, "pe")
        self.blk.tensor(f_cb)

        def f_cbe(v):
            self.w(v, "pe", st6["cbmm"])
            if e >= 1:
                self.w(v, "dve", prev["t2"][NI - 1])
            ins = v.tensor_copy(cb_sb[:], psG[(NI - 1) % 2][:])
            st6["cb"] = self.inc(ins, "dve")
        self.blk.vector(f_cbe)

        for i in range(NI):
            def f_ld(g, i=i):
                if e == 0 and i < 2:
                    self.w(g, "dve", self.comb_done)
                    self.w(g, "pe", g_pe)
                elif i < 2:
                    self.w(g, "pe", prev["umm_last"])
                else:
                    self.w(g, "pe", st6["umm"][i - 2])
                ins = g.dma_start(out=gw_r[i % 2][:].rearrange("p (d c) -> p d c", c=128),
                                  in_=self.gw[e, i])
                st6["gld"][i] = self.inc(ins, f"g{i % 2}", 16)
                ins = g.dma_start(out=uw_r[i % 2][:].rearrange("p (d c) -> p d c", c=128),
                                  in_=self.uw[e, i])
                st6["uld"][i] = self.inc(ins, f"g{2 + i % 2}", 16)
                if e >= 1:
                    self.w(g, "pe", prev["dmm_all"])
                ins = g.dma_start(out=wd_sb[:, i * D:(i + 1) * D], in_=self.dw[e, i])
                st6["dld"][i] = self.inc(ins, f"g{4 + i}", 16)
            self.blk.gpsimd(f_ld)

            def f_mm(t, i=i):
                self.w(t, f"g{i % 2}", st6["gld"][i])
                self.w(t, f"g{2 + i % 2}", st6["uld"][i])
                if i >= 2:
                    self.w(t, "act", st6["silu"][i - 2])
                    self.w(t, "dve", st6["t2"][i - 2])
                elif e >= 1:
                    self.w(t, "dve", prev["dadd_last"])
                if i == 0:
                    self.w(t, "dve", st6["cb"])
                ins = None
                for d in range(ND):
                    for n in range(2):
                        ins = t.matmul(psG[i % 2][:, 512 * n:512 * (n + 1)],
                                       gw_r[i % 2][:, 128 * d:128 * (d + 1)],
                                       self.ln2T_ap[:, d * THALF + 512 * n:
                                                    d * THALF + 512 * (n + 1)],
                                       start=(d == 0), stop=(d == ND - 1))
                st6["gmm"][i] = self.inc(ins, "pe")
                for d in range(ND):
                    for n in range(2):
                        ins = t.matmul(psU[i % 2][:, 512 * n:512 * (n + 1)],
                                       uw_r[i % 2][:, 128 * d:128 * (d + 1)],
                                       self.ln2T_ap[:, d * THALF + 512 * n:
                                                    d * THALF + 512 * (n + 1)],
                                       start=(d == 0), stop=(d == ND - 1))
                st6["umm"][i] = self.inc(ins, "pe")
            self.blk.tensor(f_mm)

            def f_silu(s, i=i):
                self.w(s, "pe", st6["gmm"][i])
                if i >= 1:
                    self.w(s, "dve", st6["t2"][i - 1])
                elif e >= 1:
                    self.w(s, "dve", prev["t2"][NI - 1])
                elif e == 0:
                    self.w(s, "dve", g_dve)
                ins = s.activation(silu_sb[:], psG[i % 2][:], AF.Silu)
                st6["silu"][i] = self.inc(ins, "act")
            self.blk.scalar(f_silu)

            def f_t2(v, i=i):
                self.w(v, "act", st6["silu"][i])
                self.w(v, "pe", st6["umm"][i])
                ins = v.tensor_mul(silu_sb[:], silu_sb[:], psU[i % 2][:])
                c1 = self.inc(ins, "dve")
                v.wait_ge(self.sems["dve"], c1)
                if e >= 1 and i == 0:
                    self.w(v, "pe", prev["dmm_all"])
                self.w(v, "dve", st6["cb"])
                ins = v.tensor_mul(A[:, i * THALF:(i + 1) * THALF], silu_sb[:], cb_sb[:])
                st6["t2"][i] = self.inc(ins, "dve")
            self.blk.vector(f_t2)

        for ut in range(8):
            def f_dmm(t, ut=ut):
                pair = psG if ut % 2 == 0 else psU
                if ut < 2:
                    self.w(t, "act", st6["silu"][NI - 1])
                    self.w(t, "dve", st6["t2"][NI - 1])
                    for i in range(NI):
                        self.w(t, f"g{4 + i}", st6["dld"][i])
                else:
                    self.w(t, "dve", st6["dadd"][(ut - 2, 1)])
                ins = None
                for k in range(2):
                    for i in range(NI):
                        for n in range(2):
                            ins = t.matmul(
                                pair[k][:, 512 * n:512 * (n + 1)],
                                A[:, i * THALF + 128 * ut:i * THALF + 128 * (ut + 1)],
                                wd_sb[:, i * D + 1024 * k + 512 * n:
                                      i * D + 1024 * k + 512 * (n + 1)],
                                start=(i == 0), stop=(i == NI - 1))
                    st6["dmm"][(ut, k)] = self.inc(ins, "pe")
            self.blk.tensor(f_dmm)

            def f_dadd(v, ut=ut):
                pair = psG if ut % 2 == 0 else psU
                for k in range(2):
                    self.w(v, "pe", st6["dmm"][(ut, k)])
                    if e == 0 and k == 0:
                        self.w(v, "dve", self.ln2_done)
                        self.w(v, "act", self.p4_act_last)
                    ins = v.tensor_tensor(self.S1h[ut][:, 1024 * k:1024 * (k + 1)],
                                          self.S1h[ut][:, 1024 * k:1024 * (k + 1)],
                                          pair[k][:], ALU.add)
                    st6["dadd"][(ut, k)] = self.inc(ins, "dve")
            self.blk.vector(f_dadd)

        prev["silu"] = dict(st6["silu"]); prev["t2"] = dict(st6["t2"])
        prev["umm_last"] = st6["umm"][NI - 1]
        prev["dmm_all"] = st6["dmm"][(7, 1)]
        prev["cbe"] = st6["cb"]
        prev["dadd_last"] = st6["dadd"][(7, 1)]
        self.p6_dadd_last = {ut: st6["dadd"][(ut, 1)] for ut in range(8)}
        self.p6_dmm_all = st6["dmm"][(7, 1)]
        self.p6_prev = prev

    def _p7(self, hf, e7, silu_sb, A, u7, psG, psU, prev):
        st = dict(gld={}, uld={}, gmm={}, umm={}, silu={}, t2={}, sld={},
                  dmm={}, dadd={})

        with ExitStack() as e7a:
            gw_r = [self.sb(e7a, f"sgwr{i}", [128, D], BF16) for i in range(2)]
            uw_r = [self.sb(e7a, f"suwr{i}", [128, D], BF16) for i in range(2)]
            for i in range(NSI):
                def f_ld(g, i=i):
                    if i < 2:
                        self.w(g, "pe", self.p6_dmm_all)
                        self.w(g, "dve", prev["dadd_last"])
                    else:
                        self.w(g, "pe", st["umm"][i - 2])
                    ins = g.dma_start(out=gw_r[i % 2][:].rearrange("p (d c) -> p d c", c=128),
                                      in_=self.sgw[i])
                    st["gld"][i] = self.inc(ins, f"g{i % 2}", 16)
                    ins = g.dma_start(out=uw_r[i % 2][:].rearrange("p (d c) -> p d c", c=128),
                                      in_=self.suw[i])
                    st["uld"][i] = self.inc(ins, f"g{2 + i % 2}", 16)
                self.blk.gpsimd(f_ld)

                def f_mm(t, i=i):
                    self.w(t, f"g{i % 2}", st["gld"][i])
                    self.w(t, f"g{2 + i % 2}", st["uld"][i])
                    if i < 2:
                        self.w(t, "dve", prev["dadd_last"])
                    else:
                        self.w(t, "act", st["silu"][i - 2])
                        self.w(t, "dve", st["t2"][i - 2])
                    ins = None
                    for d in range(ND):
                        for n in range(2):
                            ins = t.matmul(psG[i % 2][:, 512 * n:512 * (n + 1)],
                                           gw_r[i % 2][:, 128 * d:128 * (d + 1)],
                                           self.ln2T_ap[:, d * THALF + 512 * n:
                                                        d * THALF + 512 * (n + 1)],
                                           start=(d == 0), stop=(d == ND - 1))
                    st["gmm"][i] = self.inc(ins, "pe")
                    for d in range(ND):
                        for n in range(2):
                            ins = t.matmul(psU[i % 2][:, 512 * n:512 * (n + 1)],
                                           uw_r[i % 2][:, 128 * d:128 * (d + 1)],
                                           self.ln2T_ap[:, d * THALF + 512 * n:
                                                        d * THALF + 512 * (n + 1)],
                                           start=(d == 0), stop=(d == ND - 1))
                    st["umm"][i] = self.inc(ins, "pe")
                self.blk.tensor(f_mm)

                def f_act(s, i=i):
                    self.w(s, "pe", st["gmm"][i])
                    if i >= 1:
                        self.w(s, "dve", st["t2"][i - 1])
                    else:
                        self.w(s, "dve", prev["t2"][NI - 1])
                    ins = s.activation(silu_sb[:], psG[i % 2][:], AF.Silu,
                                       bias=self.sgb_sb[:, i:i + 1])
                    c1 = self.inc(ins, "act")
                    s.wait_ge(self.sems["act"], c1)
                    self.w(s, "pe", st["umm"][i])
                    ins = s.activation(u7[:], psU[i % 2][:], AF.Identity,
                                       bias=self.sub_sb[:, i:i + 1])
                    st["silu"][i] = self.inc(ins, "act")
                self.blk.scalar(f_act)

                def f_t2(v, i=i):
                    self.w(v, "act", st["silu"][i])
                    if i == 0:
                        self.w(v, "pe", self.p6_dmm_all)
                    ins = v.tensor_mul(A[:, i * THALF:(i + 1) * THALF], silu_sb[:], u7[:])
                    st["t2"][i] = self.inc(ins, "dve")
                self.blk.vector(f_t2)

        self.sh_add_last = {}
        with ExitStack() as e7b:
            sd_sb = self.sb(e7b, "sd_sb", [128, 11 * D], BF16)
            sdb_row = self.sb(e7b, "sdb_row", [1, D])
            def f_sdb(sy):
                self.w(sy, "pe", st["umm"][NSI - 1])
                ins = sy.dma_start(out=sdb_row[:], in_=self.sdb)
                st["sdb_ld"] = self.inc(ins, "r18", 16)
            self.blk.sync(f_sdb)

            for ih in range(2):
                ibase = ih * 11
                def f_sld(g, ih=ih, ibase=ibase):
                    for i in range(11):
                        if ih == 0:
                            self.w(g, "pe", st["umm"][NSI - 1])
                        else:
                            self.w(g, "pe", st["dmm"][(0, 7, 1)])
                        ins = g.dma_start(out=sd_sb[:, i * D:(i + 1) * D],
                                          in_=self.sdw[ibase + i])
                        st["sld"][(ih, i)] = self.inc(ins, f"g{4 + i}", 16)
                self.blk.gpsimd(f_sld)

                for ut in range(8):
                    def f_dmm(t, ih=ih, ut=ut, ibase=ibase):
                        pair = psG if ut % 2 == 0 else psU
                        if ut < 2:
                            self.w(t, "act", st["silu"][NSI - 1])
                            self.w(t, "dve", st["t2"][NSI - 1])
                            if ih == 1:
                                self.w(t, "dve", st["dadd"][(0, 7, 1)])
                                self.w(t, "r18", st["sdb_ld"])
                        else:
                            self.w(t, "dve", st["dadd"][(ih, ut - 2, 1)])
                        ins = None
                        for k in range(2):
                            if ih == 1:
                                for n in range(2):
                                    ins = t.matmul(pair[k][:, 512 * n:512 * (n + 1)],
                                                   self.onesf_sb[:],
                                                   sdb_row[0:1, 1024 * k + 512 * n:
                                                           1024 * k + 512 * (n + 1)],
                                                   start=True, stop=False)
                            for i in range(11):
                                if ut == 0:
                                    self.w(t, f"g{4 + i}", st["sld"][(ih, i)])
                                for n in range(2):
                                    ins = t.matmul(
                                        pair[k][:, 512 * n:512 * (n + 1)],
                                        A[:, (ibase + i) * THALF + 128 * ut:
                                          (ibase + i) * THALF + 128 * (ut + 1)],
                                        sd_sb[:, i * D + 1024 * k + 512 * n:
                                              i * D + 1024 * k + 512 * (n + 1)],
                                        start=(ih == 0 and i == 0), stop=(i == 10))
                            st["dmm"][(ih, ut, k)] = self.inc(ins, "pe")
                    self.blk.tensor(f_dmm)

                    def f_dadd(v, ih=ih, ut=ut):
                        pair = psG if ut % 2 == 0 else psU
                        for k in range(2):
                            self.w(v, "pe", st["dmm"][(ih, ut, k)])
                            ins = v.tensor_tensor(self.S1h[ut][:, 1024 * k:1024 * (k + 1)],
                                                  self.S1h[ut][:, 1024 * k:1024 * (k + 1)],
                                                  pair[k][:], ALU.add)
                            st["dadd"][(ih, ut, k)] = self.inc(ins, "dve")
                        if ih == 1:
                            self.sh_add_last[ut] = st["dadd"][(1, ut, 1)]
                    self.blk.vector(f_dadd)

    def build(self):
        nc = self.nc
        with ExitStack() as top:
            self.blk = top.enter_context(nc.Block())
            ep = top.enter_context(ExitStack())
            with ExitStack() as e1:
                self.p0(ep, e1)
                self.p1(e1)
            if not getattr(self, "only_p1", False):
                self.S1h = [self.sb(ep, f"s1h{i}", [128, D]) for i in range(8)]
                self.combT = self.sb(ep, "combT", [128, THALF])
                with ExitStack() as e_scr0:
                    self.scrT = [None, None]
                    self.scrT[0] = self.sb(e_scr0, "scrT0", [128, ND * THALF], BF16)
                    with ExitStack() as e_scr1:
                        self.scrT[1] = self.sb(e_scr1, "scrT1", [128, ND * THALF], BF16)
                        with ExitStack() as e2:
                            self.p2(e2)
                    if not getattr(self, "only_p2", False):
                        with ExitStack() as e3:
                            self.p3(e3, 0)
                if not getattr(self, "only_p2", False):
                    with ExitStack() as e4:
                        self.ffn_half(e4, 0)
                    with ExitStack() as e3:
                        self.p3(e3, 1)
                    with ExitStack() as e4:
                        self.ffn_half(e4, 1)
            rem = nc.sbuf_bytes_remaining
            final_counts = [(k, self.cnt[k]) for k in self.cnt
                            if k.startswith("r") or k in ("x", "w", "o")]
            def fin(sy):
                for k, v in final_counts:
                    self.w(sy, k, v)
            self.blk.sync(fin)
        return nc



def _restage(inp, core):
    b, qd = core // 4, core % 4
    d = {}
    x = inp["L0_attn_x"][b]
    d["x"] = np.ascontiguousarray(x)
    d["xres"] = np.ascontiguousarray(x[TCORE * qd: TCORE * (qd + 1)])
    def restage_w(w):
        noc = w.shape[1] // 128
        return np.ascontiguousarray(
            w.reshape(ND, 128, noc, 128).transpose(2, 1, 0, 3))

    for nm, w in (("wq", inp["L0_attn_Wq"]), ("wk", inp["L0_attn_Wk"])):
        d[nm] = restage_w(w[:, 512 * qd:512 * (qd + 1)])
    d["wv"] = np.ascontiguousarray(
        inp["L0_attn_Wv"][:, 512 * qd:512 * (qd + 1)]
        .reshape(ND, 128, 512).transpose(1, 0, 2))
    d["wo"] = np.ascontiguousarray(inp["L0_attn_Wo"].reshape(ND, 128, D))
    d["bo_row"] = np.ascontiguousarray(inp["L0_attn_bo"].reshape(1, D))
    d["rw"] = np.ascontiguousarray(inp["L0_ffn_router_w"])
    d["gw"] = np.stack([restage_w(inp["L0_ffn_gate_w"][e]) for e in range(NE)])
    d["uw"] = np.stack([restage_w(inp["L0_ffn_up_w"][e]) for e in range(NE)])
    d["dw"] = np.ascontiguousarray(inp["L0_ffn_down_w"].reshape(NE, NI, 128, D))
    d["sgw"] = restage_w(inp["L0_ffn_s_gate_w"])
    d["suw"] = restage_w(inp["L0_ffn_s_up_w"])
    d["sdw"] = np.ascontiguousarray(inp["L0_ffn_s_down_w"].reshape(NSI, 128, D))
    d["sgb_t"] = np.ascontiguousarray(inp["L0_ffn_s_gate_b"].reshape(NSI, 128).T)
    d["sub_t"] = np.ascontiguousarray(inp["L0_ffn_s_up_b"].reshape(NSI, 128).T)
    d["sdb"] = np.ascontiguousarray(inp["L0_ffn_s_down_b"].reshape(1, D))
    d["identf"] = np.eye(128, dtype=np.float32)
    d["onesf"] = np.ones((1, 128), dtype=np.float32)
    return d


def kernel(**inputs):
    kb = KB()
    nc = kb.build()
    in_maps = [_restage(inputs, c) for c in range(8)]
    res = run_bass_kernel_spmd(nc, in_maps, core_ids=list(range(8)))
    out = np.empty((B, T, D), np.float32)
    for c in range(8):
        b, qd = c // 4, c % 4
        out[b, TCORE * qd:TCORE * (qd + 1)] = res.results[c]["out"]
    return out


# version 15
# speedup vs baseline: 1.0221x; 1.0075x over previous
import numpy as np
from contextlib import ExitStack

import concourse.bass as bass
import concourse.mybir as mybir
from concourse.bass_utils import run_bass_kernel_spmd

F32 = mybir.dt.float32
BF16 = mybir.dt.bfloat16
AF = mybir.ActivationFunctionType
ALU = mybir.AluOpType
AX = mybir.AxisListType.X

B, T, D = 2, 8192, 2048
NH, HD = 16, 128
NE, II, SI = 8, 1408, 2816
NI = II // 128
NSI = SI // 128
ND = D // 128
EPS = 1e-5
ISC = float(1.0 / np.sqrt(128.0))
TCORE = 2048
THALF = 1024


class KB:
    def __init__(self, dbg=()):
        self.dbg = set(dbg)
        nc = bass.Bass("TRN2", target_bir_lowering=False)
        self.nc = nc
        self.cnt = {}
        self.sems = {}
        self.es_all = ExitStack()

        g = lambda n, s, k="ExternalInput": nc.dram_tensor(n, s, F32, kind=k).ap()
        self.x = g("x", [T, D])
        self.xres = g("xres", [TCORE, D])
        self.wq = g("wq", [4, 128, ND, 128])
        self.wk = g("wk", [4, 128, ND, 128])
        self.wv = g("wv", [128, ND, 512])
        self.wo = g("wo", [ND, 128, D])
        self.bo_row = g("bo_row", [1, D])
        self.rw = g("rw", [NE, D])
        self.gw = g("gw", [NE, NI, 128, ND, 128])
        self.uw = g("uw", [NE, NI, 128, ND, 128])
        self.dw = g("dw", [NE, NI, 128, D])
        self.sgw = g("sgw", [NSI, 128, ND, 128])
        self.suw = g("suw", [NSI, 128, ND, 128])
        self.sdw = g("sdw", [NSI, 128, D])
        self.sgb_t = g("sgb_t", [128, NSI])
        self.sub_t = g("sub_t", [128, NSI])
        self.sdb = g("sdb", [1, D])
        self.identf = g("identf", [128, 128])
        self.onesf = g("onesf", [1, 128])
        self.out = nc.dram_tensor("out", [TCORE, D], F32, kind="ExternalOutput").ap()

        kind = "ExternalOutput" if "qkv" in self.dbg else "Internal"
        self.qT_d = nc.dram_tensor("qT_d", [512, T], BF16, kind=kind).ap()
        self.kT_d = nc.dram_tensor("kT_d", [512, T], BF16, kind=kind).ap()
        self.vT_d = nc.dram_tensor("vT_d", [4, 128, T], BF16, kind=kind).ap()
        self.scr1_d = nc.dram_tensor("scr1_d", [128, ND * THALF], BF16, kind="Internal").ap()
        if "scr" in self.dbg:
            self.y_scr = nc.dram_tensor("y_scr", [2, 128, ND * THALF], BF16,
                                        kind="ExternalOutput").ap()
        if "s1" in self.dbg:
            self.y_s1 = nc.dram_tensor("y_s1", [8, 128, D], F32, kind="ExternalOutput").ap()
        if "ffn" in self.dbg:
            self.y_ln2 = nc.dram_tensor("y_ln2", [ND, 128, THALF], BF16, kind="ExternalOutput").ap()
            self.y_cmb = nc.dram_tensor("y_cmb", [128, THALF], F32, kind="ExternalOutput").ap()

        names = (["x", "w", "o", "pe", "act", "dve", "vf", "sf"]
                 + [f"r{i}" for i in range(24)] + [f"g{i}" for i in range(16)])
        for s in names:
            self.sems[s] = self.es_all.enter_context(nc.semaphore("sem_" + s))
            self.cnt[s] = 0

    def inc(self, ins, key, n=1):
        self.cnt[key] += n
        ins.then_inc(self.sems[key], n)
        return self.cnt[key]

    def w(self, eng, key, val):
        if val <= 0:
            return
        ename = getattr(eng, "engine", None) or getattr(eng, "name", None) or id(eng)
        wk = (str(ename), key)
        if not hasattr(self, "_wmax"):
            self._wmax = {}
        if self._wmax.get(wk, -1) >= val:
            return
        self._wmax[wk] = val
        eng.wait_ge(self.sems[key], val)

    def vfence(self, v, ins):
        c = self.inc(ins, "vf")
        v.wait_ge(self.sems["vf"], c)

    def sfence(self, s, ins):
        c = self.inc(ins, "sf")
        s.wait_ge(self.sems["sf"], c)

    def sb(self, es, name, shape, dt=F32):
        self._uid = getattr(self, "_uid", 0) + 1
        return es.enter_context(self.nc.sbuf_tensor(f"{name}_u{self._uid}", shape, dt))

    def ps(self, es, name, shape, dt=F32):
        self._uid = getattr(self, "_uid", 0) + 1
        return es.enter_context(self.nc.psum_tensor(f"{name}_u{self._uid}", shape, dt))

    def p0(self, es, e1):
        nc, blk = self.nc, self.blk
        self.identb = self.sb(es, "identb", [128, 128], BF16)
        self.onesb = self.sb(es, "onesb", [128, 128], BF16)
        self.onesf_sb = self.sb(es, "onesf_sb", [1, 128], F32)

        self.sgb_sb = self.sb(es, "sgb_sb", [128, NSI], F32)
        self.sub_sb = self.sb(es, "sub_sb", [128, NSI], F32)
        self.rwT = self.sb(es, "rwT", [128, 128], BF16)
        self.eps_c = self.sb(es, "eps_c", [128, 1], F32)
        self.wq_sb = self.sb(e1, "wq_sb", [128, 4 * D], BF16)
        self.wk_sb = self.sb(e1, "wk_sb", [128, 4 * D], BF16)
        self.wv_sb = self.sb(e1, "wv_sb", [128, 4 * D], BF16)

        rw_pad = self.sb(e1, "rw_pad", [128, D], BF16)

        def gp(g):
            for dst, src in ((self.identb, self.identf),
                             (self.onesf_sb, self.onesf),
                             (self.sgb_sb, self.sgb_t), (self.sub_sb, self.sub_t)):
                self.inc(g.dma_start(out=dst[:], in_=src), "w", 16)
            self.c_consts = self.cnt["w"]
            for w_sb, w_in in ((self.wq_sb, self.wq), (self.wk_sb, self.wk)):
                for oc in range(4):
                    ins = g.dma_start(
                        out=w_sb[:, oc * D:(oc + 1) * D].rearrange("p (d c) -> p d c", c=128),
                        in_=w_in[oc])
                    self.inc(ins, "w", 16)
            ins = g.dma_start(
                out=self.wv_sb[:].rearrange("p (d c) -> p d c", c=512), in_=self.wv)
            self.inc(ins, "w", 16)
            self.c_wqkv = self.cnt["w"]
            g.wait_ge(self.sems["dve"], self.c_rwpad_ms)
            ins = g.dma_start(out=rw_pad[0:8, :], in_=self.rw)
            self.inc(ins, "w", 16)
            self.c_rw = self.cnt["w"]

        def ve(v):
            v.memset(self.onesb[:], 1.0)
            v.memset(self.eps_c[:], EPS)
            ins = v.memset(rw_pad[:], 0.0)
            self.c_rwpad_ms = self.inc(ins, "dve")

        self.blk.vector(ve)
        self.blk.gpsimd(gp)

        pstr = [self.ps(e1, f"p0tr{i}", [128, 128], BF16) for i in range(2)]
        tr_pe = []
        rwT_evac = [self.cnt["dve"] + j + 1 for j in range(ND)]

        def pe(t):
            t.wait_ge(self.sems["w"], self.c_rw)
            for d in range(ND):
                if d >= 2:
                    self.w(t, "dve", rwT_evac[d - 2])
                ins = t.transpose(pstr[d % 2][:], rw_pad[:, 128 * d:128 * (d + 1)], self.identb[:])
                tr_pe.append(self.inc(ins, "pe"))

        def ve2(v):
            for d in range(ND):
                self.w(v, "pe", tr_pe[d])
                ins = v.tensor_copy(self.rwT[:, 8 * d:8 * (d + 1)], pstr[d % 2][:, 0:8])
                assert self.inc(ins, "dve") == rwT_evac[d]

        self.blk.tensor(pe)
        self.blk.vector(ve2)
        self.c_rwT_done = rwT_evac[-1]

    def p1(self, es):
        nc = self.nc
        NX = 8
        NL = 5
        NS = 12
        xa = [self.sb(es, f"xa{i}", [128, D]) for i in range(NX)]
        junk = self.sb(es, "junk", [128, D])
        ln_tm = [self.sb(es, f"lntm{i}", [128, D], BF16) for i in range(NL)]
        lnT = [self.sb(es, f"lnT{i}", [128, ND * 512], BF16) for i in range(2)]
        stage = [self.sb(es, f"qst{i}", [128, 512], BF16) for i in range(NS)]
        st_sr = self.sb(es, "st_sr", [128, 8])
        st_nm = self.sb(es, "st_nm", [128, 8])
        st_sq = self.sb(es, "st_sq", [128, 8])
        st_sd = self.sb(es, "st_sd", [128, 8])
        st_rs = self.sb(es, "st_rs", [128, 8])
        ps_tr = [self.ps(es, f"p1tr{i}", [128, 1024], BF16) for i in range(2)]
        ps_mm = [self.ps(es, f"p1mm{i}", [128, 512], F32) for i in range(4)]

        NG = 16
        xload = [0] * 64
        norm_done = [0] * 64
        trans_pe = {}
        trans_done = [0] * 64
        tr_evac_bank = [0, 0]
        te_last = [0] * NG
        mm_done = [0] * NG
        mm_cnt = {}
        evac_cnt = {}
        store_done = {}
        stats_war = [0, 0]
        mats = [(self.wq_sb, self.qT_d), (self.wk_sb, self.kT_d)]
        self.p1_store_last = 0

        def sync_loads(g):
            if g >= NG:
                return
            def f(sy):
                for k in range(4):
                    tt = 4 * g + k
                    slot = tt % NX
                    if tt >= NX:
                        self.w(sy, "dve", norm_done[tt - NX])
                    ins = sy.dma_start(out=xa[slot][:], in_=self.x[128 * tt:128 * (tt + 1), :])
                    xload[tt] = self.inc(ins, f"r{tt % NX}", 16)
            self.blk.sync(f)

        def vec_a(g):
            p = g % 2
            def f(v):
                for k in range(4):
                    tt = 4 * g + k
                    self.w(v, f"r{tt % NX}", xload[tt])
                if g >= 2:
                    self.w(v, "act", stats_war[p])
                last = None
                for k in range(4):
                    c = 4 * p + k
                    last = v.reduce_sum(st_sr[:, c:c + 1], xa[(4 * g + k) % NX][:], axis=AX)
                self.vfence(v, last)
                for k in range(4):
                    c = 4 * p + k
                    last = v.tensor_scalar_mul(st_nm[:, c:c + 1], st_sr[:, c:c + 1], -1.0 / D)
                self.vfence(v, last)
                self.ln_stats = self.inc(v.engine_nop(), "dve")
            self.blk.vector(f)

        def sca_a(g):
            p = g % 2
            def f(s):
                self.w(s, "dve", self.ln_stats)
                last = None
                for k in range(4):
                    c = 4 * p + k
                    last = s.activation(junk[:], xa[(4 * g + k) % NX][:], AF.Square,
                                        bias=st_nm[:, c:c + 1], accum_out=st_sq[:, c:c + 1])
                    self.sfence(s, last)
                for k in range(4):
                    c = 4 * p + k
                    last = s.activation(st_sd[:, c:c + 1], st_sq[:, c:c + 1], AF.Sqrt,
                                        scale=1.0 / D, bias=self.eps_c[:])
                self.sqrt_done = self.inc(last, "act")
                stats_war[p] = self.sqrt_done
            self.blk.scalar(f)

        def vec_b(g):
            p = g % 2
            def f(v):
                self.w(v, "act", self.sqrt_done)
                last = None
                for k in range(4):
                    c = 4 * p + k
                    last = v.reciprocal(st_rs[:, c:c + 1], st_sd[:, c:c + 1])
                self.vfence(v, last)
                for k in range(4):
                    tt = 4 * g + k
                    c = 4 * p + k
                    if tt >= NL:
                        self.w(v, "pe", trans_done[tt - NL])
                    ins = v.tensor_scalar(ln_tm[tt % NL][:], xa[tt % NX][:],
                                          st_nm[:, c:c + 1], st_rs[:, c:c + 1],
                                          ALU.add, ALU.mult)
                    norm_done[tt] = self.inc(ins, "dve")
            self.blk.vector(f)

        def pe_trans(g):
            ev_base = self.cnt["dve"]
            ev_plan = {d: ev_base + d + 1 for d in range(ND)}
            def f(t):
                if g == 0:
                    t.wait_ge(self.sems["w"], self.c_rw)
                for d in range(ND):
                    bank = d % 2
                    for k in range(4):
                        tt = 4 * g + k
                        self.w(t, "dve", norm_done[tt])
                        if d >= 2:
                            self.w(t, "dve", ev_plan[d - 2])
                        else:
                            self.w(t, "dve", tr_evac_bank[bank])
                        ins = t.transpose(ps_tr[bank][:, 128 * k:128 * (k + 1)],
                                          ln_tm[tt % NL][:, 128 * d:128 * (d + 1)],
                                          self.identb[:])
                        trans_pe[(tt, d)] = self.inc(ins, "pe")
                for k in range(4):
                    trans_done[4 * g + k] = self.cnt["pe"]
            self.blk.tensor(f)
            return ev_plan

        def vec_evac(g, ev_plan):
            pbuf = g % 2
            def f(v):
                if g >= 2:
                    self.w(v, "pe", mm_done[g - 2])
                for d in range(ND):
                    self.w(v, "pe", trans_pe[(4 * g + 3, d)])
                    ins = v.tensor_copy(lnT[pbuf][:, 512 * d: 512 * (d + 1)],
                                        ps_tr[d % 2][:, 0:512])
                    assert self.inc(ins, "dve") == ev_plan[d]
                    tr_evac_bank[d % 2] = ev_plan[d]
                te_last[g] = self.cnt["dve"]
            self.blk.vector(f)

        def pe_mm(g, ev_plan):
            pbuf = g % 2
            ev_base = self.cnt["act"]
            off = 1 if g + 1 < NG else 0
            for j in range(12):
                evac_cnt[g * 12 + j] = ev_base + j + 1 + (off if j >= 2 else 0)
            def f(t):
                for j in range(12):
                    bk = (g * 12 + j) % 4
                    prev = (g * 12 + j - 4)
                    if prev >= 0:
                        self.w(t, "act", evac_cnt.get(prev, 0))
                    if j < 8:
                        m, oc = j // 4, j % 4
                        w_sb = mats[m][0]
                        for d in range(ND):
                            self.w(t, "dve", ev_plan[d])
                            ins = t.matmul(ps_mm[bk][:],
                                           w_sb[:, oc * D + 128 * d: oc * D + 128 * (d + 1)],
                                           lnT[pbuf][:, 512 * d: 512 * (d + 1)],
                                           start=(d == 0), stop=(d == ND - 1))
                    else:
                        k = j - 8
                        for d in range(ND):
                            self.w(t, "dve", ev_plan[d])
                            ins = t.matmul(ps_mm[bk][:],
                                           lnT[pbuf][:, 512 * d + 128 * k:
                                               512 * d + 128 * (k + 1)],
                                           self.wv_sb[:, 512 * d: 512 * (d + 1)],
                                           start=(d == 0), stop=(d == ND - 1))
                    mm_cnt[(g, j)] = self.inc(ins, "pe")
                mm_done[g] = self.cnt["pe"]
            self.blk.tensor(f)

        def sca_evac(g, js):
            def f(s):
                for j in js:
                    gi = g * 12 + j
                    slot = gi % NS
                    bk = gi % 4
                    self.w(s, "pe", mm_cnt[(g, j)])
                    if gi >= NS:
                        self.w(s, f"r{8 + (gi - NS) % NS}", store_done[gi - NS])
                    ins = s.activation(stage[slot][:], ps_mm[bk][:], AF.Copy)
                    assert self.inc(ins, "act") == evac_cnt[gi], (g, j)
            self.blk.scalar(f)

        def sync_store(g):
            def f(sy):
                for j in range(12):
                    gi = g * 12 + j
                    slot = gi % NS
                    self.w(sy, "act", evac_cnt[gi])
                    if j < 8:
                        m, oc = j // 4, j % 4
                        ins = sy.dma_start(
                            out=mats[m][1][128 * oc:128 * (oc + 1), 512 * g:512 * (g + 1)],
                            in_=stage[slot][:])
                    else:
                        k = j - 8
                        tt = 4 * g + k
                        for hl in range(4):
                            ins = sy.dma_start(
                                out=self.vT_d[hl][:, 128 * tt:128 * (tt + 1)],
                                in_=stage[slot][:, 128 * hl:128 * (hl + 1)])
                            if hl < 3:
                                self.inc(ins, f"r{8 + gi % NS}", 16)
                    store_done[gi] = self.inc(ins, f"r{8 + gi % NS}", 16)
            self.blk.sync(f)

        sync_loads(0); sync_loads(1)
        vec_a(0); sca_a(0); vec_b(0)
        evp = {0: pe_trans(0)}
        vec_evac(0, evp[0])
        for g in range(NG):
            pe_mm(g, evp[g])
            sca_evac(g, range(2))
            if g + 1 < NG:
                sync_loads(g + 2)
                vec_a(g + 1); sca_a(g + 1); vec_b(g + 1)
                evp[g + 1] = pe_trans(g + 1)
                vec_evac(g + 1, evp[g + 1])
            sca_evac(g, range(2, 12))
            sync_store(g)
        self.p1_store_final = {f"r{8 + i}": self.cnt[f"r{8 + i}"] for i in range(NS)}
        self.p1_eng_final = {e: self.cnt[e] for e in ("pe", "dve", "act")}

    def p2(self, es):
        nc = self.nc
        qr = [self.sb(es, f"qr{i}", [128, 1024], BF16) for i in range(2)]
        kT = self.sb(es, "kT", [128, T], BF16)
        v_tm = self.sb(es, "v_tm", [128, T], BF16)
        wT = [self.sb(es, f"wT{i}", [128, 1024], BF16) for i in range(2)]
        ctx_sb = [self.sb(es, f"ctx{i}", [128, 1024], BF16) for i in range(2)]
        lnD_sb = self.sb(es, "lnD", [128, 1024])
        psA = self.ps(es, "psA", [128, 1024], F32)
        psB = self.ps(es, "psB", [128, 1024], F32)
        ps_sums = self.ps(es, "ps_sums", [128, 1024], F32)
        ps_ctx = self.ps(es, "ps_ctx", [128, 1024], F32)
        vtA = [psA[:, 0:64].bitcast(BF16), psA[:, 512:576].bitcast(BF16)]

        NQC = 8
        self.p2_gates = (self.cnt["act"], self.cnt["dve"])
        st = dict(q_ld=0, k_ld=0, vt_pe={}, sc={}, sp={}, lnD=0,
                  ctxm=[0, 0], last_pv=0, ctr_pe={}, sc_last={},
                  exp_last_prev=0, sp_prev2=None, vt_evac_last=0,
                  ctxm_last_head=0, recip=0)
        self.scr_evac_last = 0

        def emit_ctx_evac(v, h, qc, hl2):
            g = qc // 2
            scrT = self.scrT[h // 2]
            for dloc in range(8):
                dchunk = 8 * (qc % 2) + dloc
                self.w(v, "pe", st["ctr_pe"][(qc, dloc)])
                ins = v.tensor_copy(
                    scrT[:, dchunk * THALF + 512 * hl2 + 128 * g:
                         dchunk * THALF + 512 * hl2 + 128 * g + 128],
                    vtA[dloc % 2][:])
                self.scr_evac_last = self.inc(ins, "dve")

        def loads(h):
            def f(sy):
                if h == 0:
                    for k2, v2 in self.p1_store_final.items():
                        self.w(sy, k2, v2)
                else:
                    self.w(sy, "pe", st["sc_last"][h - 1])
                if h > 0:
                    self.w(sy, "pe", st["sc_last"][h - 1])
                ins = sy.dma_start(out=qr[0][:],
                                   in_=self.qT_d[128 * h:128 * (h + 1), 0:1024])
                st[("q_ld", 0)] = self.inc(ins, "r20", 16)
                ins = sy.dma_start(out=qr[1][:],
                                   in_=self.qT_d[128 * h:128 * (h + 1), 1024:2048])
                st[("q_ld", 1)] = self.inc(ins, "r23", 16)
                ins = sy.dma_start(out=kT[:], in_=self.kT_d[128 * h:128 * (h + 1), :])
                st["k_ld"] = self.inc(ins, "r21", 16)
                if h > 0:
                    self.w(sy, "pe", st["last_pv"])
                ins = sy.dma_start(out=v_tm[:], in_=self.vT_d[h])
                st["v_ld"] = self.inc(ins, "r22", 16)
            self.blk.sync(f)

        def pe_chunk(h, qc, hl2):
            act_base = self.cnt["act"]
            exp_plan = {j: act_base + j + 1 for j in range(64)}
            dve_base = self.cnt["dve"]
            ctx_evac_plan = {dloc: dve_base + dloc + 1 for dloc in range(8)}
            prev_scr_last = self.scr_evac_last

            def f(t):
                if qc > 0:
                    self.w(t, "dve", st["ctxm"][(qc - 1) % 2])
                    for dloc in range(8):
                        if dloc >= 2:
                            self.w(t, "dve", ctx_evac_plan[dloc - 2])
                        else:
                            self.w(t, "act", st["exp_last_prev"])
                            self.w(t, "dve", prev_scr_last)
                        ins = t.transpose(vtA[dloc % 2][:],
                                          ctx_sb[(qc - 1) % 2][:, 128 * dloc:128 * (dloc + 1)],
                                          self.identb[:])
                        st["ctr_pe"][(qc - 1, dloc)] = self.inc(ins, "pe")
                def emit_scores(j):
                    psS = psA if j % 2 == 0 else psB
                    if j <= 1:
                        self.w(t, "act", st["exp_last_prev"])
                        if j == 0:
                            self.w(t, f"r{20 if qc % 2 == 0 else 23}",
                                   st[("q_ld", qc % 2)])
                        if qc == 0:
                            if j == 0:
                                self.w(t, "r21", st["k_ld"])
                                if h > 0:
                                    self.w(t, "dve", prev_scr_last)
                        elif j == 0:
                            self.w(t, "dve", ctx_evac_plan[7])
                    else:
                        self.w(t, "act", exp_plan[j - 2])
                    ins = None
                    for n in range(2):
                        ins = t.matmul(psS[:, 512 * n:512 * (n + 1)],
                                       kT[:, 128 * j:128 * (j + 1)],
                                       qr[qc % 2][:, 512 * n:512 * (n + 1)],
                                       start=True, stop=True)
                    st["sc"][j] = self.inc(ins, "pe")

                emit_scores(0)
                emit_scores(1)
                for j in range(64):
                    self.w(t, "act", exp_plan[j])
                    if j == 0:
                        if qc == 0:
                            self.w(t, "r22", st["v_ld"])
                            if st["lnD"]:
                                self.w(t, "act", st["lnD"])
                            self.w(t, "dve", st["ctxm_last_head"])
                        else:
                            if st["lnD"]:
                                self.w(t, "act", st["lnD"])
                            self.w(t, "dve", st["ctxm"][(qc - 1) % 2])
                    for n in range(2):
                        t.matmul(ps_sums[:, 512 * n:512 * (n + 1)], self.onesb[:],
                                 wT[j % 2][:, 512 * n:512 * (n + 1)],
                                 start=(j == 0), stop=(j == 63))
                    for n in range(2):
                        ins = t.matmul(ps_ctx[:, 512 * n:512 * (n + 1)],
                                       v_tm[:, 128 * j:128 * (j + 1)],
                                       wT[j % 2][:, 512 * n:512 * (n + 1)],
                                       start=(j == 0), stop=(j == 63))
                    st["sp"][j] = self.inc(ins, "pe")
                    if j + 2 < 64:
                        emit_scores(j + 2)
                st["sc_last"][h] = st["sc"][63]
            self.blk.tensor(f)

            def fa(s):
                for j in range(64):
                    self.w(s, "pe", st["sc"][j])
                    if j >= 2:
                        self.w(s, "pe", st["sp"][j - 2])
                    elif st["sp_prev2"] is not None:
                        self.w(s, "pe", st["sp_prev2"][j])
                    ins = s.activation(wT[j % 2][:],
                                       (psA if j % 2 == 0 else psB)[:],
                                       AF.Exp, scale=ISC)
                    assert self.inc(ins, "act") == exp_plan[j]
                self.w(s, "pe", st["sp"][63])
                if st["ctxm"][(qc - 1) % 2]:
                    self.w(s, "dve", st["ctxm"][(qc - 1) % 2])
                ins = s.activation(lnD_sb[:], ps_sums[:], AF.Ln)
                st["lnD"] = self.inc(ins, "act")
                self.sfence(s, ins) if False else None
                s.wait_ge(self.sems["act"], st["lnD"])
                ins = s.activation(lnD_sb[:], lnD_sb[:], AF.Exp, scale=-1.0)
                st["recip"] = self.inc(ins, "act")
                st["exp_last_prev"] = exp_plan[63]
                st["sp_prev2"] = {0: st["sp"][62], 1: st["sp"][63]}
            self.blk.scalar(fa)

            def fv(v):
                if qc > 0:
                    emit_ctx_evac(v, h, qc - 1, hl2)
                self.w(v, "act", st["recip"])
                self.w(v, "pe", st["sp"][63])
                if qc >= 2:
                    self.w(v, "pe", st["ctr_pe"][(qc - 2, 7)])
                elif st["ctxm_last_head"] and qc == 0 and h > 0:
                    self.w(v, "pe", st["ctr_pe"][(7, 7)])
                elif qc == 1:
                    pass
                ins = v.tensor_mul(ctx_sb[qc % 2][:], ps_ctx[:], lnD_sb[:])
                st["ctxm"][qc % 2] = self.inc(ins, "dve")
            self.blk.vector(fv)

            if qc + 2 < NQC:
                sc63 = st["sc"][63]
                def fq(sy, qc=qc, h=h, sc63=sc63):
                    self.w(sy, "pe", sc63)
                    ins = sy.dma_start(
                        out=qr[qc % 2][:],
                        in_=self.qT_d[128 * h:128 * (h + 1),
                                      1024 * (qc + 2):1024 * (qc + 3)])
                    st[("q_ld", qc % 2)] = self.inc(ins, f"r{20 if qc % 2 == 0 else 23}",
                                                    16)
                self.blk.sync(fq)

        def pe_tail(h, hl2):
            dve_base = self.cnt["dve"]
            evac_plan = {dloc: dve_base + dloc + 1 for dloc in range(8)}
            prev_scr_last = self.scr_evac_last
            def f(t):
                self.w(t, "dve", st["ctxm"][1])
                for dloc in range(8):
                    if dloc >= 2:
                        self.w(t, "dve", evac_plan[dloc - 2])
                    else:
                        self.w(t, "act", st["exp_last_prev"])
                        self.w(t, "dve", prev_scr_last)
                    ins = t.transpose(vtA[dloc % 2][:],
                                      ctx_sb[1][:, 128 * dloc:128 * (dloc + 1)],
                                      self.identb[:])
                    st["ctr_pe"][(7, dloc)] = self.inc(ins, "pe")
            self.blk.tensor(f)
            def fv(v):
                emit_ctx_evac(v, h, 7, hl2)
            self.blk.vector(fv)

        for h in range(4):
            hl2 = h % 2
            loads(h)
            for qc in range(NQC):
                pe_chunk(h, qc, hl2)
            pe_tail(h, hl2)
            st["ctxm_last_head"] = st["ctxm"][1]
            st["last_pv"] = st["sp"][63]

        if "scr" in self.dbg:
            last = self.scr_evac_last
            def fd(sy):
                self.w(sy, "dve", last)
                self.inc(sy.dma_start(out=self.y_scr[0], in_=self.scrT[0][:]), "x", 16)
                self.inc(sy.dma_start(out=self.y_scr[1], in_=self.scrT[1][:]), "x", 16)
            self.blk.sync(fd)

        p2_last = self.scr_evac_last
        def fsp(sy):
            self.w(sy, "dve", p2_last)
            self.spill_done = self.inc(sy.dma_start(out=self.scr1_d, in_=self.scrT[1][:]),
                                       "r18", 16)
        self.blk.sync(fsp)
        self.p2_last = p2_last

    def ln2_prep(self, es):
        self.ln2tm = [self.sb(es, f"ln2tm{i}", [128, D], BF16) for i in range(7)]
        s2 = self.sb(es, "st2", [128, 40])
        st = self.ln2_st = dict(norm={}, tr={}, lnst=0, sqrt=0)
        ln2tm = self.ln2tm
        self.ln2_chains = []
        for grp in range(4):
            uts = (2 * grp, 2 * grp + 1)
            def ve_a(v, uts=uts):
                for ut in uts:
                    self.w(v, "dve", self.p3_add[ut])
                last = None
                for ut in uts:
                    last = v.reduce_sum(s2[:, ut:ut + 1], self.S1h[ut][:], axis=AX)
                self.vfence(v, last)
                for ut in uts:
                    last = v.tensor_scalar_mul(s2[:, 8 + ut:9 + ut], s2[:, ut:ut + 1],
                                               -1.0 / D)
                self.vfence(v, last)
                st["lnst"] = self.inc(v.engine_nop(), "dve")

            def sc_a(sc, uts=uts):
                self.w(sc, "dve", st["lnst"])
                for ut in uts:
                    dump = 6 if ut == 7 else ut
                    ins = sc.activation(ln2tm[dump][:], self.S1h[ut][:], AF.Square,
                                        bias=s2[:, 8 + ut:9 + ut],
                                        accum_out=s2[:, 16 + ut:17 + ut])
                    self.sfence(sc, ins)
                last = None
                for ut in uts:
                    last = sc.activation(s2[:, 24 + ut:25 + ut], s2[:, 16 + ut:17 + ut],
                                         AF.Sqrt, scale=1.0 / D, bias=self.eps_c[:])
                st["sqrt"] = self.inc(last, "act")

            def ve_b(v, uts=uts):
                self.w(v, "act", st["sqrt"])
                last = None
                for ut in uts:
                    last = v.reciprocal(s2[:, 32 + ut:33 + ut], s2[:, 24 + ut:25 + ut])
                self.vfence(v, last)
                for ut in uts:
                    if ut == 7:
                        self.w(v, "pe", st["tr"][(0, ND - 1)])
                    ins = v.tensor_scalar(ln2tm[ut % 7][:], self.S1h[ut][:],
                                          s2[:, 8 + ut:9 + ut], s2[:, 32 + ut:33 + ut],
                                          ALU.add, ALU.mult)
                    st["norm"][ut] = self.inc(ins, "dve")
            self.ln2_chains.append((ve_a, sc_a, ve_b))

    def p3(self, es, hf):
        nc = self.nc
        g_pe, g_act, g_dve = self.cnt["pe"], self.cnt["act"], self.cnt["dve"]
        if hf == 0:
            scrT = self.scrT[0]
            gate_r18 = self.spill_done
        else:
            scrT = self.sb(es, "scrT1b", [128, ND * THALF], BF16)
            gate_r18 = 0
        wo_sb = [self.sb(es, f"wo{i}", [128, D], BF16) for i in range(ND)]
        bo_sb = self.sb(es, "bo_sb", [1, D], F32)
        psP = [self.ps(es, f"psP{i}", [128, D], F32) for i in range(2)]

        st = dict(ld={}, mm={}, add={})
        self.p3_add = {}

        if hf == 1:
            def frl(sy):
                self.w(sy, "dve", g_dve)
                self.w(sy, "pe", g_pe)
                st["rl"] = self.inc(sy.dma_start(out=scrT[:], in_=self.scr1_d), "r18", 16)
            self.blk.sync(frl)

        def init_s1(sy):
            self.w(sy, "dve", g_dve)
            st["bo"] = self.inc(sy.dma_start(out=bo_sb[:], in_=self.bo_row), "r16", 16)
        self.blk.sync(init_s1)

        def init_s1g(g):
            if hf == 0:
                for e, c in self.p1_eng_final.items():
                    self.w(g, e, c)
                for k2, v2 in self.p1_store_final.items():
                    self.w(g, k2, v2)
            else:
                self.w(g, "o", self.out_done_half0)
            for ut in range(8):
                base = 1024 * hf + 512 * (ut // 4) + (ut % 4)
                ins = g.dma_start(out=self.S1h[ut][:],
                                  in_=self.xres[base: base + 509: 4, :])
                self.inc(ins, "w", 16)
            st["s1all"] = self.cnt["w"]
        self.blk.gpsimd(init_s1g)

        def loads(g):
            for d in range(ND):
                if d < 2:
                    self.w(g, "dve", g_dve)
                    self.w(g, "pe", g_pe)
                    if gate_r18:
                        self.w(g, "r18", gate_r18)
                ins = g.dma_start(out=wo_sb[d][:], in_=self.wo[d])
                st["ld"][d] = self.inc(ins, f"g{d}", 16)
        self.blk.gpsimd(loads)

        def pe_mm(ut):
            def f(t):
                if ut == 0:
                    self.w(t, "r16", st["bo"])
                    if hf == 1:
                        self.w(t, "r18", st["rl"])
                if ut < 2:
                    self.w(t, "dve", g_dve)
                    self.w(t, "act", g_act)
                else:
                    self.w(t, "dve", st["add"][ut - 2])
                ins = None
                for n in range(4):
                    ins = t.matmul(psP[ut % 2][:, 512 * n:512 * (n + 1)],
                                   self.onesf_sb[:], bo_sb[0:1, 512 * n:512 * (n + 1)],
                                   start=True, stop=False)
                for d in range(ND):
                    if ut == 0:
                        self.w(t, f"g{d}", st["ld"][d])
                    for n in range(4):
                        ins = t.matmul(psP[ut % 2][:, 512 * n:512 * (n + 1)],
                                       scrT[:, d * THALF + 128 * ut:d * THALF + 128 * (ut + 1)],
                                       wo_sb[d][:, 512 * n:512 * (n + 1)],
                                       start=False, stop=(d == ND - 1))
                st["mm"][ut] = self.inc(ins, "pe")
            self.blk.tensor(f)

        def dve_add(ut):
            def f(v):
                self.w(v, "pe", st["mm"][ut])
                self.w(v, "w", st["s1all"])
                ins = v.tensor_tensor(self.S1h[ut][:], self.S1h[ut][:],
                                      psP[ut % 2][:], ALU.add)
                st["add"][ut] = self.inc(ins, "dve")
                self.p3_add[ut] = st["add"][ut]
            self.blk.vector(f)

        for ut in range(8):
            pe_mm(ut)
            dve_add(ut)
        self.p3_last = self.cnt["dve"]

        if "s1" in self.dbg and hf == 0:
            last = self.p3_last
            def fd(sy):
                self.w(sy, "dve", last)
                for ut in range(8):
                    self.inc(sy.dma_start(out=self.y_s1[ut], in_=self.S1h[ut][:]), "x", 16)
            self.blk.sync(fd)

    def ffn_half(self, es, hf):
        nc = self.nc
        g_pe, g_act, g_dve = self.cnt["pe"], self.cnt["act"], self.cnt["dve"]
        self.gate3 = (g_pe, g_act, g_dve)
        ln2T = self.sb(es, "ln2T", [128, ND * THALF], BF16)
        self.combT = self.sb(es, "combT", [128, THALF])
        self.ln2T_ap = ln2T
        silu_sb = self.sb(es, "silu_sb", [128, THALF], BF16)

        with ExitStack() as e45:
            self.ln2_prep(e45)
            ln2tm = self.ln2tm
            st = dict(norm=self.ln2_st["norm"], tr=self.ln2_st["tr"],
                      tr_last=0, lnst=0)
            pstr = [self.ps(e45, f"p4tr{i}", [128, 1024], BF16) for i in range(4)]
            identf_sb = self.sb(e45, "identf_sb", [128, 128], F32)
            def f_idf(sy):
                self.w(sy, "dve", g_dve)
                self.w(sy, "pe", g_pe)
                st_idf[0] = self.inc(sy.dma_start(out=identf_sb[:], in_=self.identf),
                                     "r8", 16)
            st_idf = [0]
            self.blk.sync(f_idf)

            dve_base4 = self.cnt["dve"]
            evac_plan = {}
            _c = dve_base4 + 1
            for ut in range(8):
                if ut % 2 == 0:
                    _c += 3
                for d in range(ND):
                    evac_plan[(ut, d)] = _c + d + 1
                _c += 16 + (1 if ut >= 1 else 0) + 2 + 1

            pad = self.sb(e45, "pad", [128, 128])
            lg = [self.sb(e45, f"lg{i}", [128, 8]) for i in range(2)]
            top8 = [self.sb(e45, f"top8{i}", [128, 8]) for i in range(2)]
            cstat = [self.sb(e45, f"cstat{i}", [128, 8]) for i in range(2)]
            sel = [self.sb(e45, f"sel{i}", [128, 8]) for i in range(2)]
            e1t = [self.sb(e45, f"e1t{i}", [128, 8]) for i in range(2)]
            comb = [self.sb(e45, f"comb{i}", [128, 8]) for i in range(2)]
            ps_lgF = [self.ps(e45, f"ps_lg{i}", [128, 512], F32) for i in range(2)]
            ps_lg = [p[:, 0:8] for p in ps_lgF]
            ps_ptF = self.ps(e45, "ps_pt", [128, 512], F32)
            ps_pt = ps_ptF[:, 0:128]
            rst = dict(mm={}, evac={}, e1={}, sel={}, combw={}, ptr={}, pev={})

            def ms(v):
                ins = v.memset(pad[:], 0.0)
                rst["ms"] = self.inc(ins, "dve")
                assert rst["ms"] == dve_base4 + 1
            self.blk.vector(ms)

            for ut in range(8):
                if ut % 2 == 0:
                    ve_a_f, sc_a_f, ve_b_f = self.ln2_chains[ut // 2]
                    self.blk.vector(ve_a_f)
                    self.blk.scalar(sc_a_f)
                    self.blk.vector(ve_b_f)
                def f_tr(t, ut=ut):
                    self.w(t, "dve", st["norm"][ut])
                    if ut == 0:
                        self.w(t, "dve", g_dve)
                        self.w(t, "act", g_act)
                    for d in range(ND):
                        j = ut * ND + d
                        if j >= 4:
                            self.w(t, "dve", evac_plan[((j - 4) // ND, (j - 4) % ND)])
                        ins = t.transpose(pstr[j % 4][:, 0:128],
                                          ln2tm[ut % 7][:, 128 * d:128 * (d + 1)],
                                          self.identb[:])
                        st["tr"][(ut, d)] = self.inc(ins, "pe")
                    if ut >= 1:
                        self.w(t, "dve", rst["combw"][ut - 1])
                        if ut >= 2:
                            self.w(t, "dve", rst["pev"][ut - 2])
                        else:
                            self.w(t, "dve", g_dve)
                        if ut == 1:
                            self.w(t, "r8", st_idf[0])
                        ins = t.transpose(ps_pt, pad[:], identf_sb[:])
                        rst["ptr"][ut - 1] = self.inc(ins, "pe")
                    ins = None
                    if ut >= 2:
                        self.w(t, "dve", rst["evac"][ut - 2])
                    for d in range(ND):
                        self.w(t, "dve", evac_plan[(ut, d)])
                        ins = t.matmul(ps_lg[ut % 2],
                                       ln2T[:, d * THALF + 128 * ut:d * THALF + 128 * (ut + 1)],
                                       self.rwT[:, 8 * d:8 * (d + 1)],
                                       start=(d == 0), stop=(d == ND - 1))
                    rst["mm"][ut] = self.inc(ins, "pe")
                self.blk.tensor(f_tr)

                def f_evac(v, ut=ut):
                    for d in range(ND):
                        j = ut * ND + d
                        self.w(v, "pe", st["tr"][(ut, d)])
                        ins = v.tensor_copy(ln2T[:, d * THALF + 128 * ut:
                                                 d * THALF + 128 * (ut + 1)],
                                            pstr[j % 4][:, 0:128])
                        assert self.inc(ins, "dve") == evac_plan[(ut, d)], (ut, d)
                    if ut >= 1:
                        self.w(v, "pe", rst["ptr"][ut - 1])
                        ins = v.tensor_copy(self.combT[:, 128 * (ut - 1):128 * ut],
                                            ps_pt)
                        rst["pev"][ut - 1] = self.inc(ins, "dve")
                self.blk.vector(f_evac)

                def f_ve1(v, ut=ut):
                    p = ut % 2
                    self.w(v, "pe", rst["mm"][ut])
                    ins = v.tensor_copy(lg[p][:], ps_lg[p])
                    rst["evac"][ut] = self.inc(ins, "dve")
                    v.wait_ge(self.sems["dve"], rst["evac"][ut])
                    ins = v.max(out=top8[p][:], in_=lg[p][:])
                    self.vfence(v, ins)
                    ins = v.tensor_single_scalar(cstat[p][:, 0:1], top8[p][:, 2:3],
                                                 top8[p][:, 1:2], ALU.add)
                    self.vfence(v, ins)
                    ins = v.tensor_scalar_mul(cstat[p][:, 0:1], cstat[p][:, 0:1], 0.5)
                    ins2 = v.tensor_scalar_mul(cstat[p][:, 1:2], top8[p][:, 0:1], -1.0)
                    self.vfence(v, ins2)
                    ins = v.tensor_scalar_sub(sel[p][:], lg[p][:], cstat[p][:, 0:1])
                    self.vfence(v, ins)
                    ins = v.tensor_scalar_mul(sel[p][:], sel[p][:], 1e7)
                    self.vfence(v, ins)
                    ins = v.tensor_scalar_min(sel[p][:], sel[p][:], 1.0)
                    self.vfence(v, ins)
                    ins = v.tensor_scalar_max(sel[p][:], sel[p][:], 0.0)
                    rst["sel"][ut] = self.inc(ins, "dve")
                self.blk.vector(f_ve1)

                def f_sc(s, ut=ut):
                    p = ut % 2
                    self.w(s, "dve", rst["sel"][ut])
                    ins = s.activation(e1t[p][:], lg[p][:], AF.Exp, bias=cstat[p][:, 1:2])
                    rst["e1"][ut] = self.inc(ins, "act")
                self.blk.scalar(f_sc)

                def f_ve2(v, ut=ut):
                    p = ut % 2
                    self.w(v, "act", rst["e1"][ut])
                    ins = v.tensor_mul(e1t[p][:], e1t[p][:], sel[p][:])
                    self.vfence(v, ins)
                    ins = v.reduce_sum(cstat[p][:, 2:3], e1t[p][:], axis=AX)
                    self.vfence(v, ins)
                    ins = v.reciprocal(cstat[p][:, 3:4], cstat[p][:, 2:3])
                    self.vfence(v, ins)
                    ins = v.tensor_scalar_mul(comb[p][:], e1t[p][:], cstat[p][:, 3:4])
                    self.vfence(v, ins)
                    if ut == 0:
                        self.w(v, "dve", rst["ms"])
                    if ut >= 1:
                        self.w(v, "pe", rst["ptr"][ut - 1])
                    ins = v.tensor_copy(pad[:, 0:8], comb[p][:])
                    rst["combw"][ut] = self.inc(ins, "dve")
                self.blk.vector(f_ve2)

            def f_pe2_last(t):
                self.w(t, "dve", rst["combw"][7])
                self.w(t, "dve", rst["pev"][6])
                ins = t.transpose(ps_pt, pad[:], identf_sb[:])
                rst["ptr"][7] = self.inc(ins, "pe")
            self.blk.tensor(f_pe2_last)

            def f_ve3_last(v):
                self.w(v, "pe", rst["ptr"][7])
                ins = v.tensor_copy(self.combT[:, 128 * 7:128 * 8], ps_pt)
                rst["pev"][7] = self.inc(ins, "dve")
            self.blk.vector(f_ve3_last)
            self.comb_done = self.cnt["dve"]
            self.ln2_done = self.comb_done
            self.p4_act_last = self.cnt["act"]
            self.p5_act_last = self.cnt["act"]
            if "ffn" in self.dbg and hf == 0:
                cd = self.comb_done
                def fdmp(sy):
                    self.w(sy, "dve", cd)
                    self.inc(sy.dma_start(
                        out=self.y_ln2,
                        in_=ln2T[:].rearrange("p (d u) -> d p u", u=THALF)), "x", 16)
                    self.inc(sy.dma_start(out=self.y_cmb, in_=self.combT[:]), "x", 16)
                self.blk.sync(fdmp)

        with ExitStack() as e6:
            A = self.sb(e6, "A_sb", [128, NI * THALF], BF16)
            wd_sb = self.sb(e6, "wd_sb", [128, NI * D], BF16)
            crow = self.sb(e6, "crow", [1, THALF])
            cb_sb = self.sb(e6, "cb_sb", [128, THALF], BF16)
            gw_r = [self.sb(e6, f"gw{i}", [128, D], BF16) for i in range(2)]
            uw_r = [self.sb(e6, f"uw{i}", [128, D], BF16) for i in range(2)]
            psG = [self.ps(e6, f"psG{i}", [128, THALF], F32) for i in range(2)]
            psU = [self.ps(e6, f"psU{i}", [128, THALF], F32) for i in range(2)]
            prev = dict(silu={}, t2={}, dadd_last=0, dmm_all=0, cbe=0)
            for e in range(NE):
                self._expert(hf, e, crow, cb_sb, silu_sb, A, wd_sb, gw_r, uw_r,
                             psG, psU, prev)

        with ExitStack() as e7:
            A7 = self.sb(e7, "A7_sb", [128, NSI * THALF], BF16)
            u7 = self.sb(e7, "u7", [128, THALF])
            psG = [self.ps(e7, f"psG7{i}", [128, THALF], F32) for i in range(2)]
            psU = [self.ps(e7, f"psU7{i}", [128, THALF], F32) for i in range(2)]
            self._p7(hf, e7, silu_sb, A7, u7, psG, psU, prev)

        def fout(s):
            for ut in range(8):
                self.w(s, "dve", self.sh_add_last[ut])
                base = 1024 * hf + 512 * (ut // 4) + (ut % 4)
                ins = s.dma_start(out=self.out[base: base + 509: 4, :],
                                  in_=self.S1h[ut][:])
                self.inc(ins, "o", 16)
        self.blk.scalar(fout)
        if hf == 0:
            self.out_done_half0 = self.cnt["o"]

    def _expert(self, hf, e, crow, cb_sb, silu_sb, A, wd_sb, gw_r, uw_r,
                psG, psU, prev):
        st6 = dict(gld={}, uld={}, dld={}, gmm={}, umm={}, silu={}, t2={},
                   dmm={}, dadd={})
        g_pe, g_act, g_dve = self.gate3

        def f_crow(sy):
            if e >= 1:
                self.w(sy, "dve", prev["cbe"])
            else:
                self.w(sy, "dve", self.comb_done)
            ins = sy.dma_start(out=crow[:], in_=self.combT[e:e + 1, :])
            st6["crow_ld"] = self.inc(ins, "r17", 16)
        self.blk.sync(f_crow)

        def f_cb(t):
            self.w(t, "r17", st6["crow_ld"])
            if e == 0:
                self.w(t, "dve", self.comb_done)
                self.w(t, "act", self.p5_act_last)
            else:
                self.w(t, "dve", prev["dadd_last"])
            ins = None
            for n in range(2):
                ins = t.matmul(psG[(NI - 1) % 2][:, 512 * n:512 * (n + 1)],
                               self.onesf_sb[:], crow[0:1, 512 * n:512 * (n + 1)],
                               start=True, stop=True)
            st6["cbmm"] = self.inc(ins, "pe")
        self.blk.tensor(f_cb)

        def f_cbe(v):
            self.w(v, "pe", st6["cbmm"])
            if e >= 1:
                self.w(v, "dve", prev["t2"][NI - 1])
            ins = v.tensor_copy(cb_sb[:], psG[(NI - 1) % 2][:])
            st6["cb"] = self.inc(ins, "dve")
        self.blk.vector(f_cbe)

        for i in range(NI):
            def f_ld(g, i=i):
                if e == 0 and i < 2:
                    self.w(g, "dve", self.comb_done)
                    self.w(g, "pe", g_pe)
                elif i < 2:
                    self.w(g, "pe", prev["umm_last"])
                else:
                    self.w(g, "pe", st6["umm"][i - 2])
                ins = g.dma_start(out=gw_r[i % 2][:].rearrange("p (d c) -> p d c", c=128),
                                  in_=self.gw[e, i])
                st6["gld"][i] = self.inc(ins, f"g{i % 2}", 16)
                ins = g.dma_start(out=uw_r[i % 2][:].rearrange("p (d c) -> p d c", c=128),
                                  in_=self.uw[e, i])
                st6["uld"][i] = self.inc(ins, f"g{2 + i % 2}", 16)
                if e >= 1:
                    self.w(g, "pe", prev["dmm_all"])
                ins = g.dma_start(out=wd_sb[:, i * D:(i + 1) * D], in_=self.dw[e, i])
                st6["dld"][i] = self.inc(ins, f"g{4 + i}", 16)
            self.blk.gpsimd(f_ld)

            def f_mm(t, i=i):
                self.w(t, f"g{i % 2}", st6["gld"][i])
                self.w(t, f"g{2 + i % 2}", st6["uld"][i])
                if i >= 2:
                    self.w(t, "act", st6["silu"][i - 2])
                    self.w(t, "dve", st6["t2"][i - 2])
                elif e >= 1:
                    self.w(t, "dve", prev["dadd_last"])
                if i == 0:
                    self.w(t, "dve", st6["cb"])
                ins = None
                for d in range(ND):
                    for n in range(2):
                        ins = t.matmul(psG[i % 2][:, 512 * n:512 * (n + 1)],
                                       gw_r[i % 2][:, 128 * d:128 * (d + 1)],
                                       self.ln2T_ap[:, d * THALF + 512 * n:
                                                    d * THALF + 512 * (n + 1)],
                                       start=(d == 0), stop=(d == ND - 1))
                st6["gmm"][i] = self.inc(ins, "pe")
                for d in range(ND):
                    for n in range(2):
                        ins = t.matmul(psU[i % 2][:, 512 * n:512 * (n + 1)],
                                       uw_r[i % 2][:, 128 * d:128 * (d + 1)],
                                       self.ln2T_ap[:, d * THALF + 512 * n:
                                                    d * THALF + 512 * (n + 1)],
                                       start=(d == 0), stop=(d == ND - 1))
                st6["umm"][i] = self.inc(ins, "pe")
            self.blk.tensor(f_mm)

            def f_silu(s, i=i):
                self.w(s, "pe", st6["gmm"][i])
                if i >= 1:
                    self.w(s, "dve", st6["t2"][i - 1])
                elif e >= 1:
                    self.w(s, "dve", prev["t2"][NI - 1])
                elif e == 0:
                    self.w(s, "dve", g_dve)
                ins = s.activation(silu_sb[:], psG[i % 2][:], AF.Silu)
                st6["silu"][i] = self.inc(ins, "act")
            self.blk.scalar(f_silu)

            def f_t2(v, i=i):
                self.w(v, "act", st6["silu"][i])
                self.w(v, "pe", st6["umm"][i])
                ins = v.tensor_mul(silu_sb[:], silu_sb[:], psU[i % 2][:])
                c1 = self.inc(ins, "dve")
                v.wait_ge(self.sems["dve"], c1)
                if e >= 1 and i == 0:
                    self.w(v, "pe", prev["dmm_all"])
                self.w(v, "dve", st6["cb"])
                ins = v.tensor_mul(A[:, i * THALF:(i + 1) * THALF], silu_sb[:], cb_sb[:])
                st6["t2"][i] = self.inc(ins, "dve")
            self.blk.vector(f_t2)

        for ut in range(8):
            def f_dmm(t, ut=ut):
                pair = psG if ut % 2 == 0 else psU
                if ut < 2:
                    self.w(t, "act", st6["silu"][NI - 1])
                    self.w(t, "dve", st6["t2"][NI - 1])
                    for i in range(NI):
                        self.w(t, f"g{4 + i}", st6["dld"][i])
                else:
                    self.w(t, "dve", st6["dadd"][(ut - 2, 1)])
                ins = None
                for k in range(2):
                    for i in range(NI):
                        for n in range(2):
                            ins = t.matmul(
                                pair[k][:, 512 * n:512 * (n + 1)],
                                A[:, i * THALF + 128 * ut:i * THALF + 128 * (ut + 1)],
                                wd_sb[:, i * D + 1024 * k + 512 * n:
                                      i * D + 1024 * k + 512 * (n + 1)],
                                start=(i == 0), stop=(i == NI - 1))
                    st6["dmm"][(ut, k)] = self.inc(ins, "pe")
            self.blk.tensor(f_dmm)

            def f_dadd(v, ut=ut):
                pair = psG if ut % 2 == 0 else psU
                for k in range(2):
                    self.w(v, "pe", st6["dmm"][(ut, k)])
                    if e == 0 and k == 0:
                        self.w(v, "dve", self.ln2_done)
                        self.w(v, "act", self.p4_act_last)
                    ins = v.tensor_tensor(self.S1h[ut][:, 1024 * k:1024 * (k + 1)],
                                          self.S1h[ut][:, 1024 * k:1024 * (k + 1)],
                                          pair[k][:], ALU.add)
                    st6["dadd"][(ut, k)] = self.inc(ins, "dve")
            self.blk.vector(f_dadd)

        prev["silu"] = dict(st6["silu"]); prev["t2"] = dict(st6["t2"])
        prev["umm_last"] = st6["umm"][NI - 1]
        prev["dmm_all"] = st6["dmm"][(7, 1)]
        prev["cbe"] = st6["cb"]
        prev["dadd_last"] = st6["dadd"][(7, 1)]
        self.p6_dadd_last = {ut: st6["dadd"][(ut, 1)] for ut in range(8)}
        self.p6_dmm_all = st6["dmm"][(7, 1)]
        self.p6_prev = prev

    def _p7(self, hf, e7, silu_sb, A, u7, psG, psU, prev):
        st = dict(gld={}, uld={}, gmm={}, umm={}, silu={}, t2={}, sld={},
                  dmm={}, dadd={})

        with ExitStack() as e7a:
            gw_r = [self.sb(e7a, f"sgwr{i}", [128, D], BF16) for i in range(2)]
            uw_r = [self.sb(e7a, f"suwr{i}", [128, D], BF16) for i in range(2)]
            for i in range(NSI):
                def f_ld(g, i=i):
                    if i < 2:
                        self.w(g, "pe", self.p6_dmm_all)
                        self.w(g, "dve", prev["dadd_last"])
                    else:
                        self.w(g, "pe", st["umm"][i - 2])
                    ins = g.dma_start(out=gw_r[i % 2][:].rearrange("p (d c) -> p d c", c=128),
                                      in_=self.sgw[i])
                    st["gld"][i] = self.inc(ins, f"g{i % 2}", 16)
                    ins = g.dma_start(out=uw_r[i % 2][:].rearrange("p (d c) -> p d c", c=128),
                                      in_=self.suw[i])
                    st["uld"][i] = self.inc(ins, f"g{2 + i % 2}", 16)
                self.blk.gpsimd(f_ld)

                def f_mm(t, i=i):
                    self.w(t, f"g{i % 2}", st["gld"][i])
                    self.w(t, f"g{2 + i % 2}", st["uld"][i])
                    if i < 2:
                        self.w(t, "dve", prev["dadd_last"])
                    else:
                        self.w(t, "act", st["silu"][i - 2])
                        self.w(t, "dve", st["t2"][i - 2])
                    ins = None
                    for d in range(ND):
                        for n in range(2):
                            ins = t.matmul(psG[i % 2][:, 512 * n:512 * (n + 1)],
                                           gw_r[i % 2][:, 128 * d:128 * (d + 1)],
                                           self.ln2T_ap[:, d * THALF + 512 * n:
                                                        d * THALF + 512 * (n + 1)],
                                           start=(d == 0), stop=(d == ND - 1))
                    st["gmm"][i] = self.inc(ins, "pe")
                    for d in range(ND):
                        for n in range(2):
                            ins = t.matmul(psU[i % 2][:, 512 * n:512 * (n + 1)],
                                           uw_r[i % 2][:, 128 * d:128 * (d + 1)],
                                           self.ln2T_ap[:, d * THALF + 512 * n:
                                                        d * THALF + 512 * (n + 1)],
                                           start=(d == 0), stop=(d == ND - 1))
                    st["umm"][i] = self.inc(ins, "pe")
                self.blk.tensor(f_mm)

                def f_act(s, i=i):
                    self.w(s, "pe", st["gmm"][i])
                    if i >= 1:
                        self.w(s, "dve", st["t2"][i - 1])
                    else:
                        self.w(s, "dve", prev["t2"][NI - 1])
                    ins = s.activation(silu_sb[:], psG[i % 2][:], AF.Silu,
                                       bias=self.sgb_sb[:, i:i + 1])
                    c1 = self.inc(ins, "act")
                    s.wait_ge(self.sems["act"], c1)
                    self.w(s, "pe", st["umm"][i])
                    ins = s.activation(u7[:], psU[i % 2][:], AF.Identity,
                                       bias=self.sub_sb[:, i:i + 1])
                    st["silu"][i] = self.inc(ins, "act")
                self.blk.scalar(f_act)

                def f_t2(v, i=i):
                    self.w(v, "act", st["silu"][i])
                    if i == 0:
                        self.w(v, "pe", self.p6_dmm_all)
                    ins = v.tensor_mul(A[:, i * THALF:(i + 1) * THALF], silu_sb[:], u7[:])
                    st["t2"][i] = self.inc(ins, "dve")
                self.blk.vector(f_t2)

        self.sh_add_last = {}
        with ExitStack() as e7b:
            sd_sb = self.sb(e7b, "sd_sb", [128, 11 * D], BF16)
            sdb_row = self.sb(e7b, "sdb_row", [1, D])
            def f_sdb(sy):
                self.w(sy, "pe", st["umm"][NSI - 1])
                ins = sy.dma_start(out=sdb_row[:], in_=self.sdb)
                st["sdb_ld"] = self.inc(ins, "r18", 16)
            self.blk.sync(f_sdb)

            for ih in range(2):
                ibase = ih * 11
                def f_sld(g, ih=ih, ibase=ibase):
                    for i in range(11):
                        if ih == 0:
                            self.w(g, "pe", st["umm"][NSI - 1])
                        else:
                            self.w(g, "pe", st["dmm"][(0, 7, 1)])
                        ins = g.dma_start(out=sd_sb[:, i * D:(i + 1) * D],
                                          in_=self.sdw[ibase + i])
                        st["sld"][(ih, i)] = self.inc(ins, f"g{4 + i}", 16)
                self.blk.gpsimd(f_sld)

                for ut in range(8):
                    def f_dmm(t, ih=ih, ut=ut, ibase=ibase):
                        pair = psG if ut % 2 == 0 else psU
                        if ut < 2:
                            self.w(t, "act", st["silu"][NSI - 1])
                            self.w(t, "dve", st["t2"][NSI - 1])
                            if ih == 1:
                                self.w(t, "dve", st["dadd"][(0, 7, 1)])
                                self.w(t, "r18", st["sdb_ld"])
                        else:
                            self.w(t, "dve", st["dadd"][(ih, ut - 2, 1)])
                        ins = None
                        for k in range(2):
                            if ih == 1:
                                for n in range(2):
                                    ins = t.matmul(pair[k][:, 512 * n:512 * (n + 1)],
                                                   self.onesf_sb[:],
                                                   sdb_row[0:1, 1024 * k + 512 * n:
                                                           1024 * k + 512 * (n + 1)],
                                                   start=True, stop=False)
                            for i in range(11):
                                if ut == 0:
                                    self.w(t, f"g{4 + i}", st["sld"][(ih, i)])
                                for n in range(2):
                                    ins = t.matmul(
                                        pair[k][:, 512 * n:512 * (n + 1)],
                                        A[:, (ibase + i) * THALF + 128 * ut:
                                          (ibase + i) * THALF + 128 * (ut + 1)],
                                        sd_sb[:, i * D + 1024 * k + 512 * n:
                                              i * D + 1024 * k + 512 * (n + 1)],
                                        start=(ih == 0 and i == 0), stop=(i == 10))
                            st["dmm"][(ih, ut, k)] = self.inc(ins, "pe")
                    self.blk.tensor(f_dmm)

                    def f_dadd(v, ih=ih, ut=ut):
                        pair = psG if ut % 2 == 0 else psU
                        for k in range(2):
                            self.w(v, "pe", st["dmm"][(ih, ut, k)])
                            ins = v.tensor_tensor(self.S1h[ut][:, 1024 * k:1024 * (k + 1)],
                                                  self.S1h[ut][:, 1024 * k:1024 * (k + 1)],
                                                  pair[k][:], ALU.add)
                            st["dadd"][(ih, ut, k)] = self.inc(ins, "dve")
                        if ih == 1:
                            self.sh_add_last[ut] = st["dadd"][(1, ut, 1)]
                    self.blk.vector(f_dadd)

    def build(self):
        nc = self.nc
        with ExitStack() as top:
            self.blk = top.enter_context(nc.Block())
            ep = top.enter_context(ExitStack())
            with ExitStack() as e1:
                self.p0(ep, e1)
                self.p1(e1)
            if not getattr(self, "only_p1", False):
                self.S1h = [self.sb(ep, f"s1h{i}", [128, D]) for i in range(8)]
                with ExitStack() as e_scr0:
                    self.scrT = [None, None]
                    self.scrT[0] = self.sb(e_scr0, "scrT0", [128, ND * THALF], BF16)
                    with ExitStack() as e_scr1:
                        self.scrT[1] = self.sb(e_scr1, "scrT1", [128, ND * THALF], BF16)
                        with ExitStack() as e2:
                            self.p2(e2)
                    if not getattr(self, "only_p2", False):
                        with ExitStack() as e3:
                            self.p3(e3, 0)
                if not getattr(self, "only_p2", False):
                    with ExitStack() as e4:
                        self.ffn_half(e4, 0)
                    with ExitStack() as e3:
                        self.p3(e3, 1)
                    with ExitStack() as e4:
                        self.ffn_half(e4, 1)
            rem = nc.sbuf_bytes_remaining
            final_counts = [(k, self.cnt[k]) for k in self.cnt
                            if k.startswith("r") or k in ("x", "w", "o")]
            def fin(sy):
                for k, v in final_counts:
                    self.w(sy, k, v)
            self.blk.sync(fin)
        return nc



def _restage(inp, core):
    b, qd = core // 4, core % 4
    d = {}
    x = inp["L0_attn_x"][b]
    d["x"] = np.ascontiguousarray(x)
    d["xres"] = np.ascontiguousarray(x[TCORE * qd: TCORE * (qd + 1)])
    def restage_w(w):
        noc = w.shape[1] // 128
        return np.ascontiguousarray(
            w.reshape(ND, 128, noc, 128).transpose(2, 1, 0, 3))

    for nm, w in (("wq", inp["L0_attn_Wq"]), ("wk", inp["L0_attn_Wk"])):
        d[nm] = restage_w(w[:, 512 * qd:512 * (qd + 1)])
    d["wv"] = np.ascontiguousarray(
        inp["L0_attn_Wv"][:, 512 * qd:512 * (qd + 1)]
        .reshape(ND, 128, 512).transpose(1, 0, 2))
    d["wo"] = np.ascontiguousarray(inp["L0_attn_Wo"].reshape(ND, 128, D))
    d["bo_row"] = np.ascontiguousarray(inp["L0_attn_bo"].reshape(1, D))
    d["rw"] = np.ascontiguousarray(inp["L0_ffn_router_w"])
    d["gw"] = np.stack([restage_w(inp["L0_ffn_gate_w"][e]) for e in range(NE)])
    d["uw"] = np.stack([restage_w(inp["L0_ffn_up_w"][e]) for e in range(NE)])
    d["dw"] = np.ascontiguousarray(inp["L0_ffn_down_w"].reshape(NE, NI, 128, D))
    d["sgw"] = restage_w(inp["L0_ffn_s_gate_w"])
    d["suw"] = restage_w(inp["L0_ffn_s_up_w"])
    d["sdw"] = np.ascontiguousarray(inp["L0_ffn_s_down_w"].reshape(NSI, 128, D))
    d["sgb_t"] = np.ascontiguousarray(inp["L0_ffn_s_gate_b"].reshape(NSI, 128).T)
    d["sub_t"] = np.ascontiguousarray(inp["L0_ffn_s_up_b"].reshape(NSI, 128).T)
    d["sdb"] = np.ascontiguousarray(inp["L0_ffn_s_down_b"].reshape(1, D))
    d["identf"] = np.eye(128, dtype=np.float32)
    d["onesf"] = np.ones((1, 128), dtype=np.float32)
    return d


def kernel(**inputs):
    kb = KB()
    nc = kb.build()
    in_maps = [_restage(inputs, c) for c in range(8)]
    res = run_bass_kernel_spmd(nc, in_maps, core_ids=list(range(8)))
    out = np.empty((B, T, D), np.float32)
    for c in range(8):
        b, qd = c // 4, c % 4
        out[b, TCORE * qd:TCORE * (qd + 1)] = res.results[c]["out"]
    return out


# version 16
# speedup vs baseline: 1.1155x; 1.0913x over previous
import numpy as np
from contextlib import ExitStack

import concourse.bass as bass
import concourse.mybir as mybir
from concourse.bass_utils import run_bass_kernel_spmd

F32 = mybir.dt.float32
BF16 = mybir.dt.bfloat16
AF = mybir.ActivationFunctionType
ALU = mybir.AluOpType
AX = mybir.AxisListType.X

B, T, D = 2, 8192, 2048
NH, HD = 16, 128
NE, II, SI = 8, 1408, 2816
NI = II // 128
NSI = SI // 128
ND = D // 128
EPS = 1e-5
ISC = float(1.0 / np.sqrt(128.0))
TCORE = 2048
THALF = 1024


class KB:
    def __init__(self, dbg=()):
        self.dbg = set(dbg)
        nc = bass.Bass("TRN2", target_bir_lowering=False)
        self.nc = nc
        self.cnt = {}
        self.sems = {}
        self.es_all = ExitStack()

        g = lambda n, s, k="ExternalInput": nc.dram_tensor(n, s, F32, kind=k).ap()
        self.x = g("x", [T, D])
        self.xres = g("xres", [TCORE, D])
        self.wq = g("wq", [4, 128, ND, 128])
        self.wk = g("wk", [4, 128, ND, 128])
        self.wv = g("wv", [128, ND, 512])
        self.wo = g("wo", [ND, 128, D])
        self.bo_row = g("bo_row", [1, D])
        self.rw = g("rw", [NE, D])
        self.gw = g("gw", [NE, NI, 128, ND, 128])
        self.uw = g("uw", [NE, NI, 128, ND, 128])
        self.dw = g("dw", [NE, NI, 128, D])
        self.sgw = g("sgw", [NSI, 128, ND, 128])
        self.suw = g("suw", [NSI, 128, ND, 128])
        self.sdw = g("sdw", [NSI, 128, D])
        self.sgb_t = g("sgb_t", [128, NSI])
        self.sub_t = g("sub_t", [128, NSI])
        self.sdb = g("sdb", [1, D])
        self.identf = g("identf", [128, 128])
        self.onesf = g("onesf", [1, 128])
        self.out = nc.dram_tensor("out", [TCORE, D], F32, kind="ExternalOutput").ap()

        kind = "ExternalOutput" if "qkv" in self.dbg else "Internal"
        self.qT_d = nc.dram_tensor("qT_d", [512, T], BF16, kind=kind).ap()
        self.kT_d = nc.dram_tensor("kT_d", [512, T], BF16, kind=kind).ap()
        self.vT_d = nc.dram_tensor("vT_d", [4, 128, T], BF16, kind=kind).ap()
        self.scr1_d = nc.dram_tensor("scr1_d", [128, ND * THALF], BF16, kind="Internal").ap()
        if "scr" in self.dbg:
            self.y_scr = nc.dram_tensor("y_scr", [2, 128, ND * THALF], BF16,
                                        kind="ExternalOutput").ap()
        if "s1" in self.dbg:
            self.y_s1 = nc.dram_tensor("y_s1", [8, 128, D], F32, kind="ExternalOutput").ap()
        if "ffn" in self.dbg:
            self.y_ln2 = nc.dram_tensor("y_ln2", [ND, 128, THALF], BF16, kind="ExternalOutput").ap()
            self.y_cmb = nc.dram_tensor("y_cmb", [128, THALF], F32, kind="ExternalOutput").ap()

        names = (["x", "w", "o", "pe", "act", "dve", "vf", "sf"]
                 + [f"r{i}" for i in range(24)] + [f"g{i}" for i in range(16)])
        for s in names:
            self.sems[s] = self.es_all.enter_context(nc.semaphore("sem_" + s))
            self.cnt[s] = 0

    def inc(self, ins, key, n=1):
        self.cnt[key] += n
        ins.then_inc(self.sems[key], n)
        return self.cnt[key]

    def w(self, eng, key, val):
        if val <= 0:
            return
        ename = getattr(eng, "engine", None) or getattr(eng, "name", None) or id(eng)
        wk = (str(ename), key)
        if not hasattr(self, "_wmax"):
            self._wmax = {}
        if self._wmax.get(wk, -1) >= val:
            return
        self._wmax[wk] = val
        eng.wait_ge(self.sems[key], val)

    def vfence(self, v, ins):
        c = self.inc(ins, "vf")
        v.wait_ge(self.sems["vf"], c)

    def sfence(self, s, ins):
        c = self.inc(ins, "sf")
        s.wait_ge(self.sems["sf"], c)

    def sb(self, es, name, shape, dt=F32):
        self._uid = getattr(self, "_uid", 0) + 1
        return es.enter_context(self.nc.sbuf_tensor(f"{name}_u{self._uid}", shape, dt))

    def ps(self, es, name, shape, dt=F32):
        self._uid = getattr(self, "_uid", 0) + 1
        return es.enter_context(self.nc.psum_tensor(f"{name}_u{self._uid}", shape, dt))

    def p0(self, es, e1):
        nc, blk = self.nc, self.blk
        self.identb = self.sb(es, "identb", [128, 128], BF16)
        self.identf_sb = self.sb(es, "identf_sb", [128, 128], F32)
        self.onesb = self.sb(es, "onesb", [128, 128], BF16)
        self.onesf_sb = self.sb(es, "onesf_sb", [1, 128], F32)

        self.sgb_sb = self.sb(es, "sgb_sb", [128, NSI], F32)
        self.sub_sb = self.sb(es, "sub_sb", [128, NSI], F32)
        self.rwT = self.sb(es, "rwT", [128, 128], BF16)
        self.eps_c = self.sb(es, "eps_c", [128, 1], F32)
        self.wq_sb = self.sb(e1, "wq_sb", [128, 4 * D], BF16)
        self.wk_sb = self.sb(e1, "wk_sb", [128, 4 * D], BF16)
        self.wv_sb = self.sb(e1, "wv_sb", [128, 4 * D], BF16)

        rw_pad = self.sb(e1, "rw_pad", [128, D], BF16)

        def gp(g):
            for dst, src in ((self.identb, self.identf), (self.identf_sb, self.identf),
                             (self.onesf_sb, self.onesf),
                             (self.sgb_sb, self.sgb_t), (self.sub_sb, self.sub_t)):
                self.inc(g.dma_start(out=dst[:], in_=src), "w", 16)
            self.c_consts = self.cnt["w"]
            for w_sb, w_in in ((self.wq_sb, self.wq), (self.wk_sb, self.wk)):
                for oc in range(4):
                    ins = g.dma_start(
                        out=w_sb[:, oc * D:(oc + 1) * D].rearrange("p (d c) -> p d c", c=128),
                        in_=w_in[oc])
                    self.inc(ins, "w", 16)
            ins = g.dma_start(
                out=self.wv_sb[:].rearrange("p (d c) -> p d c", c=512), in_=self.wv)
            self.inc(ins, "w", 16)
            self.c_wqkv = self.cnt["w"]
            g.wait_ge(self.sems["dve"], self.c_rwpad_ms)
            ins = g.dma_start(out=rw_pad[0:8, :], in_=self.rw)
            self.inc(ins, "w", 16)
            self.c_rw = self.cnt["w"]

        def ve(v):
            v.memset(self.onesb[:], 1.0)
            v.memset(self.eps_c[:], EPS)
            ins = v.memset(rw_pad[:], 0.0)
            self.c_rwpad_ms = self.inc(ins, "dve")

        self.blk.vector(ve)
        self.blk.gpsimd(gp)

        pstr = [self.ps(e1, f"p0tr{i}", [128, 128], BF16) for i in range(2)]
        tr_pe = []
        rwT_evac = [self.cnt["dve"] + j + 1 for j in range(ND)]

        def pe(t):
            t.wait_ge(self.sems["w"], self.c_rw)
            for d in range(ND):
                if d >= 2:
                    self.w(t, "dve", rwT_evac[d - 2])
                ins = t.transpose(pstr[d % 2][:], rw_pad[:, 128 * d:128 * (d + 1)], self.identb[:])
                tr_pe.append(self.inc(ins, "pe"))

        def ve2(v):
            for d in range(ND):
                self.w(v, "pe", tr_pe[d])
                ins = v.tensor_copy(self.rwT[:, 8 * d:8 * (d + 1)], pstr[d % 2][:, 0:8])
                assert self.inc(ins, "dve") == rwT_evac[d]

        self.blk.tensor(pe)
        self.blk.vector(ve2)
        self.c_rwT_done = rwT_evac[-1]

    def p1(self, es):
        nc = self.nc
        NX = 8
        NL = 5
        NS = 12
        xa = [self.sb(es, f"xa{i}", [128, D]) for i in range(NX)]
        junk = self.sb(es, "junk", [128, D])
        ln_tm = [self.sb(es, f"lntm{i}", [128, D], BF16) for i in range(NL)]
        lnT = [self.sb(es, f"lnT{i}", [128, ND * 512], BF16) for i in range(2)]
        stage = [self.sb(es, f"qst{i}", [128, 512], BF16) for i in range(NS)]
        st_sr = self.sb(es, "st_sr", [128, 8])
        st_nm = self.sb(es, "st_nm", [128, 8])
        st_sq = self.sb(es, "st_sq", [128, 8])
        st_sd = self.sb(es, "st_sd", [128, 8])
        st_rs = self.sb(es, "st_rs", [128, 8])
        ps_tr = [self.ps(es, f"p1tr{i}", [128, 1024], BF16) for i in range(2)]
        ps_mm = [self.ps(es, f"p1mm{i}", [128, 512], F32) for i in range(4)]

        NG = 16
        xload = [0] * 64
        norm_done = [0] * 64
        trans_pe = {}
        trans_done = [0] * 64
        tr_evac_bank = [0, 0]
        te_last = [0] * NG
        mm_done = [0] * NG
        mm_cnt = {}
        evac_cnt = {}
        store_done = {}
        stats_war = [0, 0]
        mats = [(self.wq_sb, self.qT_d), (self.wk_sb, self.kT_d)]
        self.p1_store_last = 0

        def sync_loads(g):
            if g >= NG:
                return
            def f(sy):
                for k in range(4):
                    tt = 4 * g + k
                    slot = tt % NX
                    if tt >= NX:
                        self.w(sy, "dve", norm_done[tt - NX])
                    ins = sy.dma_start(out=xa[slot][:], in_=self.x[128 * tt:128 * (tt + 1), :])
                    xload[tt] = self.inc(ins, f"r{tt % NX}", 16)
            self.blk.sync(f)

        def vec_a(g):
            p = g % 2
            def f(v):
                for k in range(4):
                    tt = 4 * g + k
                    self.w(v, f"r{tt % NX}", xload[tt])
                if g >= 2:
                    self.w(v, "act", stats_war[p])
                last = None
                for k in range(4):
                    c = 4 * p + k
                    last = v.reduce_sum(st_sr[:, c:c + 1], xa[(4 * g + k) % NX][:], axis=AX)
                self.vfence(v, last)
                for k in range(4):
                    c = 4 * p + k
                    last = v.tensor_scalar_mul(st_nm[:, c:c + 1], st_sr[:, c:c + 1], -1.0 / D)
                self.vfence(v, last)
                self.ln_stats = self.inc(v.engine_nop(), "dve")
            self.blk.vector(f)

        def sca_a(g):
            p = g % 2
            def f(s):
                self.w(s, "dve", self.ln_stats)
                last = None
                for k in range(4):
                    c = 4 * p + k
                    last = s.activation(junk[:], xa[(4 * g + k) % NX][:], AF.Square,
                                        bias=st_nm[:, c:c + 1], accum_out=st_sq[:, c:c + 1])
                    self.sfence(s, last)
                for k in range(4):
                    c = 4 * p + k
                    last = s.activation(st_sd[:, c:c + 1], st_sq[:, c:c + 1], AF.Sqrt,
                                        scale=1.0 / D, bias=self.eps_c[:])
                self.sqrt_done = self.inc(last, "act")
                stats_war[p] = self.sqrt_done
            self.blk.scalar(f)

        def vec_b(g):
            p = g % 2
            def f(v):
                self.w(v, "act", self.sqrt_done)
                last = None
                for k in range(4):
                    c = 4 * p + k
                    last = v.reciprocal(st_rs[:, c:c + 1], st_sd[:, c:c + 1])
                self.vfence(v, last)
                for k in range(4):
                    tt = 4 * g + k
                    c = 4 * p + k
                    if tt >= NL:
                        self.w(v, "pe", trans_done[tt - NL])
                    ins = v.tensor_scalar(ln_tm[tt % NL][:], xa[tt % NX][:],
                                          st_nm[:, c:c + 1], st_rs[:, c:c + 1],
                                          ALU.add, ALU.mult)
                    norm_done[tt] = self.inc(ins, "dve")
            self.blk.vector(f)

        def pe_trans(g):
            ev_base = self.cnt["dve"]
            ev_plan = {d: ev_base + d + 1 for d in range(ND)}
            def f(t):
                if g == 0:
                    t.wait_ge(self.sems["w"], self.c_rw)
                for d in range(ND):
                    bank = d % 2
                    for k in range(4):
                        tt = 4 * g + k
                        self.w(t, "dve", norm_done[tt])
                        if d >= 2:
                            self.w(t, "dve", ev_plan[d - 2])
                        else:
                            self.w(t, "dve", tr_evac_bank[bank])
                        ins = t.transpose(ps_tr[bank][:, 128 * k:128 * (k + 1)],
                                          ln_tm[tt % NL][:, 128 * d:128 * (d + 1)],
                                          self.identb[:])
                        trans_pe[(tt, d)] = self.inc(ins, "pe")
                for k in range(4):
                    trans_done[4 * g + k] = self.cnt["pe"]
            self.blk.tensor(f)
            return ev_plan

        def vec_evac(g, ev_plan):
            pbuf = g % 2
            def f(v):
                if g >= 2:
                    self.w(v, "pe", mm_done[g - 2])
                for d in range(ND):
                    self.w(v, "pe", trans_pe[(4 * g + 3, d)])
                    ins = v.tensor_copy(lnT[pbuf][:, 512 * d: 512 * (d + 1)],
                                        ps_tr[d % 2][:, 0:512])
                    assert self.inc(ins, "dve") == ev_plan[d]
                    tr_evac_bank[d % 2] = ev_plan[d]
                te_last[g] = self.cnt["dve"]
            self.blk.vector(f)

        def pe_mm(g, ev_plan):
            pbuf = g % 2
            ev_base = self.cnt["act"]
            off = 1 if g + 1 < NG else 0
            for j in range(12):
                evac_cnt[g * 12 + j] = ev_base + j + 1 + (off if j >= 2 else 0)
            def f(t):
                for j in range(12):
                    bk = (g * 12 + j) % 4
                    prev = (g * 12 + j - 4)
                    if prev >= 0:
                        self.w(t, "act", evac_cnt.get(prev, 0))
                    if j < 8:
                        m, oc = j // 4, j % 4
                        w_sb = mats[m][0]
                        for d in range(ND):
                            self.w(t, "dve", ev_plan[d])
                            ins = t.matmul(ps_mm[bk][:],
                                           w_sb[:, oc * D + 128 * d: oc * D + 128 * (d + 1)],
                                           lnT[pbuf][:, 512 * d: 512 * (d + 1)],
                                           start=(d == 0), stop=(d == ND - 1))
                    else:
                        k = j - 8
                        for d in range(ND):
                            self.w(t, "dve", ev_plan[d])
                            ins = t.matmul(ps_mm[bk][:],
                                           lnT[pbuf][:, 512 * d + 128 * k:
                                               512 * d + 128 * (k + 1)],
                                           self.wv_sb[:, 512 * d: 512 * (d + 1)],
                                           start=(d == 0), stop=(d == ND - 1))
                    mm_cnt[(g, j)] = self.inc(ins, "pe")
                mm_done[g] = self.cnt["pe"]
            self.blk.tensor(f)

        def sca_evac(g, js):
            def f(s):
                for j in js:
                    gi = g * 12 + j
                    slot = gi % NS
                    bk = gi % 4
                    self.w(s, "pe", mm_cnt[(g, j)])
                    if gi >= NS:
                        self.w(s, f"r{8 + (gi - NS) % NS}", store_done[gi - NS])
                    ins = s.activation(stage[slot][:], ps_mm[bk][:], AF.Copy)
                    assert self.inc(ins, "act") == evac_cnt[gi], (g, j)
            self.blk.scalar(f)

        def sync_store(g):
            def f(sy):
                for j in range(12):
                    gi = g * 12 + j
                    slot = gi % NS
                    self.w(sy, "act", evac_cnt[gi])
                    if j < 8:
                        m, oc = j // 4, j % 4
                        ins = sy.dma_start(
                            out=mats[m][1][128 * oc:128 * (oc + 1), 512 * g:512 * (g + 1)],
                            in_=stage[slot][:])
                    else:
                        k = j - 8
                        tt = 4 * g + k
                        for hl in range(4):
                            ins = sy.dma_start(
                                out=self.vT_d[hl][:, 128 * tt:128 * (tt + 1)],
                                in_=stage[slot][:, 128 * hl:128 * (hl + 1)])
                            if hl < 3:
                                self.inc(ins, f"r{8 + gi % NS}", 16)
                    store_done[gi] = self.inc(ins, f"r{8 + gi % NS}", 16)
            self.blk.sync(f)

        sync_loads(0); sync_loads(1)
        vec_a(0); sca_a(0); vec_b(0)
        evp = {0: pe_trans(0)}
        vec_evac(0, evp[0])
        for g in range(NG):
            pe_mm(g, evp[g])
            sca_evac(g, range(2))
            if g + 1 < NG:
                sync_loads(g + 2)
                vec_a(g + 1); sca_a(g + 1); vec_b(g + 1)
                evp[g + 1] = pe_trans(g + 1)
                vec_evac(g + 1, evp[g + 1])
            sca_evac(g, range(2, 12))
            sync_store(g)
        self.p1_store_final = {f"r{8 + i}": self.cnt[f"r{8 + i}"] for i in range(NS)}

    def p2(self, es):
        nc = self.nc
        qT = self.sb(es, "qT", [128, T], BF16)
        kT = self.sb(es, "kT", [128, T], BF16)
        v_tm = self.sb(es, "v_tm", [128, T], BF16)
        wT = [self.sb(es, f"wT{i}", [128, 1024], BF16) for i in range(2)]
        ctx_sb = [self.sb(es, f"ctx{i}", [128, 1024], BF16) for i in range(2)]
        lnD_sb = self.sb(es, "lnD", [128, 1024])
        recip_sb = self.sb(es, "recip", [128, 1024])
        psA = self.ps(es, "psA", [128, 1024], F32)
        psB = self.ps(es, "psB", [128, 1024], F32)
        ps_sums = self.ps(es, "ps_sums", [128, 1024], F32)
        ps_ctx = self.ps(es, "ps_ctx", [128, 1024], F32)
        vtA = [psA[:, 0:64].bitcast(BF16), psA[:, 512:576].bitcast(BF16)]

        NQC = 8
        self.p2_gates = (self.cnt["act"], self.cnt["dve"])
        st = dict(q_ld=0, k_ld=0, vt_pe={}, sc={}, sp={}, lnD=0,
                  ctxm=[0, 0], last_pv=0, ctr_pe={}, sc_last={},
                  exp_last_prev=0, exp_prev_a=0, sp_prev2=None, vt_evac_last=0,
                  ctxm_last_head=0, recip=0)
        self.scr_evac_last = 0

        def emit_ctx_evac(v, h, qc, hl2):
            g = qc // 2
            scrT = self.scrT[h // 2]
            for dloc in range(8):
                dchunk = 8 * (qc % 2) + dloc
                self.w(v, "pe", st["ctr_pe"][(qc, dloc)])
                ins = v.tensor_copy(
                    scrT[:, dchunk * THALF + 512 * hl2 + 128 * g:
                         dchunk * THALF + 512 * hl2 + 128 * g + 128],
                    vtA[dloc % 2][:])
                self.scr_evac_last = self.inc(ins, "dve")

        def loads(h):
            def f(sy):
                if h == 0:
                    for k2, v2 in self.p1_store_final.items():
                        self.w(sy, k2, v2)
                else:
                    self.w(sy, "pe", st["sc_last"][h - 1])
                ins = sy.dma_start(out=qT[:], in_=self.qT_d[128 * h:128 * (h + 1), :])
                st["q_ld"] = self.inc(ins, "r20", 16)
                ins = sy.dma_start(out=kT[:], in_=self.kT_d[128 * h:128 * (h + 1), :])
                st["k_ld"] = self.inc(ins, "r21", 16)
                if h > 0:
                    self.w(sy, "pe", st["last_pv"])
                ins = sy.dma_start(out=v_tm[:], in_=self.vT_d[h])
                st["v_ld"] = self.inc(ins, "r22", 16)
            self.blk.sync(f)

        def pe_chunk(h, qc, hl2):
            act_base = self.cnt["act"]
            exp_plan = {j: act_base + j + 1 for j in range(64)}
            dve_base = self.cnt["dve"]
            ctx_evac_plan = {dloc: dve_base + dloc + 1 for dloc in range(8)}
            prev_scr_last = self.scr_evac_last

            def f(t):
                if qc > 0:
                    self.w(t, "dve", st["ctxm"][(qc - 1) % 2])
                    for dloc in range(8):
                        if dloc >= 2:
                            self.w(t, "dve", ctx_evac_plan[dloc - 2])
                        else:
                            self.w(t, "act", st["exp_prev_a"])
                            self.w(t, "dve", prev_scr_last)
                        ins = t.transpose(vtA[dloc % 2][:],
                                          ctx_sb[(qc - 1) % 2][:, 128 * dloc:128 * (dloc + 1)],
                                          self.identb[:])
                        st["ctr_pe"][(qc - 1, dloc)] = self.inc(ins, "pe")
                def emit_scores(j):
                    psS = psA if j % 2 == 0 else psB
                    if j <= 1:
                        self.w(t, "act", st["exp_prev_a"] if j == 0
                               else st["exp_last_prev"])
                        if qc == 0:
                            if j == 0:
                                self.w(t, "r20", st["q_ld"])
                                self.w(t, "r21", st["k_ld"])
                                if h > 0:
                                    self.w(t, "dve", prev_scr_last)
                        elif j == 0:
                            self.w(t, "dve", ctx_evac_plan[7])
                    else:
                        self.w(t, "act", exp_plan[j - 2])
                    ins = None
                    for n in range(2):
                        ins = t.matmul(psS[:, 512 * n:512 * (n + 1)],
                                       kT[:, 128 * j:128 * (j + 1)],
                                       qT[:, 1024 * qc + 512 * n:1024 * qc + 512 * (n + 1)],
                                       start=True, stop=True)
                    st["sc"][j] = self.inc(ins, "pe")

                emit_scores(0)
                emit_scores(1)
                for j in range(64):
                    self.w(t, "act", exp_plan[j])
                    if j == 0:
                        if qc == 0:
                            self.w(t, "r22", st["v_ld"])
                            if st["lnD"]:
                                self.w(t, "act", st["lnD"])
                            self.w(t, "dve", st["ctxm_last_head"])
                        else:
                            if st["lnD"]:
                                self.w(t, "act", st["lnD"])
                            self.w(t, "dve", st["ctxm"][(qc - 1) % 2])
                    for n in range(2):
                        t.matmul(ps_sums[:, 512 * n:512 * (n + 1)], self.onesb[:],
                                 wT[j % 2][:, 512 * n:512 * (n + 1)],
                                 start=(j == 0), stop=(j == 63))
                    for n in range(2):
                        ins = t.matmul(ps_ctx[:, 512 * n:512 * (n + 1)],
                                       v_tm[:, 128 * j:128 * (j + 1)],
                                       wT[j % 2][:, 512 * n:512 * (n + 1)],
                                       start=(j == 0), stop=(j == 63))
                    st["sp"][j] = self.inc(ins, "pe")
                    if j + 2 < 64:
                        emit_scores(j + 2)
                st["sc_last"][h] = st["sc"][63]
            self.blk.tensor(f)

            def fa(s):
                for j in range(64):
                    self.w(s, "pe", st["sc"][j])
                    if j >= 2:
                        self.w(s, "pe", st["sp"][j - 2])
                    elif st["sp_prev2"] is not None:
                        self.w(s, "pe", st["sp_prev2"][j])
                    ins = s.activation(wT[j % 2][:],
                                       (psA if j % 2 == 0 else psB)[:],
                                       AF.Exp, scale=ISC)
                    assert self.inc(ins, "act") == exp_plan[j]
                self.w(s, "pe", st["sp"][63])
                ins = s.activation(lnD_sb[:], ps_sums[:], AF.Ln)
                st["lnD"] = self.inc(ins, "act")
                self.sfence(s, ins) if False else None
                s.wait_ge(self.sems["act"], st["lnD"])
                ins = s.activation(recip_sb[:], lnD_sb[:], AF.Exp, scale=-1.0)
                st["recip"] = self.inc(ins, "act")
                st["exp_last_prev"] = exp_plan[63]
                st["exp_prev_a"] = exp_plan[62]
                st["sp_prev2"] = {0: st["sp"][62], 1: st["sp"][63]}
            self.blk.scalar(fa)

            def fv(v):
                if qc > 0:
                    emit_ctx_evac(v, h, qc - 1, hl2)
                self.w(v, "act", st["recip"])
                self.w(v, "pe", st["sp"][63])
                if qc >= 2:
                    self.w(v, "pe", st["ctr_pe"][(qc - 2, 7)])
                elif st["ctxm_last_head"] and qc == 0 and h > 0:
                    self.w(v, "pe", st["ctr_pe"][(7, 7)])
                elif qc == 1:
                    pass
                ins = v.tensor_mul(ctx_sb[qc % 2][:], ps_ctx[:], recip_sb[:])
                st["ctxm"][qc % 2] = self.inc(ins, "dve")
            self.blk.vector(fv)

        def pe_tail(h, hl2):
            dve_base = self.cnt["dve"]
            evac_plan = {dloc: dve_base + dloc + 1 for dloc in range(8)}
            prev_scr_last = self.scr_evac_last
            def f(t):
                self.w(t, "dve", st["ctxm"][1])
                for dloc in range(8):
                    if dloc >= 2:
                        self.w(t, "dve", evac_plan[dloc - 2])
                    else:
                        self.w(t, "act", st["exp_prev_a"])
                        self.w(t, "dve", prev_scr_last)
                    ins = t.transpose(vtA[dloc % 2][:],
                                      ctx_sb[1][:, 128 * dloc:128 * (dloc + 1)],
                                      self.identb[:])
                    st["ctr_pe"][(7, dloc)] = self.inc(ins, "pe")
            self.blk.tensor(f)
            def fv(v):
                emit_ctx_evac(v, h, 7, hl2)
            self.blk.vector(fv)

        for h in range(4):
            hl2 = h % 2
            loads(h)
            for qc in range(NQC):
                pe_chunk(h, qc, hl2)
            pe_tail(h, hl2)
            st["ctxm_last_head"] = st["ctxm"][1]
            st["last_pv"] = st["sp"][63]

        if "scr" in self.dbg:
            last = self.scr_evac_last
            def fd(sy):
                self.w(sy, "dve", last)
                self.inc(sy.dma_start(out=self.y_scr[0], in_=self.scrT[0][:]), "x", 16)
                self.inc(sy.dma_start(out=self.y_scr[1], in_=self.scrT[1][:]), "x", 16)
            self.blk.sync(fd)

        p2_last = self.scr_evac_last
        def fsp(sy):
            self.w(sy, "dve", p2_last)
            self.spill_done = self.inc(sy.dma_start(out=self.scr1_d, in_=self.scrT[1][:]),
                                       "r18", 16)
        self.blk.sync(fsp)
        self.p2_last = p2_last

    def p3(self, es, hf):
        nc = self.nc
        g_pe, g_act, g_dve = self.cnt["pe"], self.cnt["act"], self.cnt["dve"]
        if hf == 0:
            scrT = self.scrT[0]
            gate_r18 = self.spill_done
        else:
            scrT = self.sb(es, "scrT1b", [128, ND * THALF], BF16)
            gate_r18 = 0
        wo_sb = [self.sb(es, f"wo{i}", [128, D], BF16) for i in range(ND)]
        bo_sb = self.sb(es, "bo_sb", [1, D], F32)
        psP = [self.ps(es, f"psP{i}", [128, D], F32) for i in range(2)]

        st = dict(ld={}, mm={}, add={})
        self.p3_add = {}

        if hf == 1:
            def frl(sy):
                self.w(sy, "dve", g_dve)
                self.w(sy, "pe", g_pe)
                st["rl"] = self.inc(sy.dma_start(out=scrT[:], in_=self.scr1_d), "r18", 16)
            self.blk.sync(frl)

        def init_s1(sy):
            if hf == 1 or True:
                self.w(sy, "dve", g_dve)
            st["bo"] = self.inc(sy.dma_start(out=bo_sb[:], in_=self.bo_row), "r16", 16)
            for ut in range(8):
                if hf == 1:
                    self.w(sy, "o", self.out_done_half0)
                base = 1024 * hf + 512 * (ut // 4) + (ut % 4)
                ins = sy.dma_start(out=self.S1h[ut][:],
                                   in_=self.xres[base: base + 509: 4, :])
                st[("s1", ut)] = self.inc(ins, f"r{8 + ut}", 16)
        self.blk.sync(init_s1)

        def loads(g):
            for d in range(ND):
                if d < 2:
                    self.w(g, "dve", g_dve)
                    self.w(g, "pe", g_pe)
                    if gate_r18:
                        self.w(g, "r18", gate_r18)
                ins = g.dma_start(out=wo_sb[d][:], in_=self.wo[d])
                st["ld"][d] = self.inc(ins, f"g{d}", 16)
        self.blk.gpsimd(loads)

        def pe_mm(ut):
            def f(t):
                if ut == 0:
                    self.w(t, "r16", st["bo"])
                    if hf == 1:
                        self.w(t, "r18", st["rl"])
                if ut < 2:
                    self.w(t, "dve", g_dve)
                    self.w(t, "act", g_act)
                else:
                    self.w(t, "dve", st["add"][ut - 2])
                ins = None
                for n in range(4):
                    ins = t.matmul(psP[ut % 2][:, 512 * n:512 * (n + 1)],
                                   self.onesf_sb[:], bo_sb[0:1, 512 * n:512 * (n + 1)],
                                   start=True, stop=False)
                for d in range(ND):
                    if ut == 0:
                        self.w(t, f"g{d}", st["ld"][d])
                    for n in range(4):
                        ins = t.matmul(psP[ut % 2][:, 512 * n:512 * (n + 1)],
                                       scrT[:, d * THALF + 128 * ut:d * THALF + 128 * (ut + 1)],
                                       wo_sb[d][:, 512 * n:512 * (n + 1)],
                                       start=False, stop=(d == ND - 1))
                st["mm"][ut] = self.inc(ins, "pe")
            self.blk.tensor(f)

        def dve_add(ut):
            def f(v):
                self.w(v, "pe", st["mm"][ut])
                self.w(v, f"r{8 + ut}", st[("s1", ut)])
                ins = v.tensor_tensor(self.S1h[ut][:], self.S1h[ut][:],
                                      psP[ut % 2][:], ALU.add)
                st["add"][ut] = self.inc(ins, "dve")
                self.p3_add[ut] = st["add"][ut]
            self.blk.vector(f)

        for ut in range(8):
            pe_mm(ut)
            dve_add(ut)
        self.p3_last = self.cnt["dve"]

        if "s1" in self.dbg and hf == 0:
            last = self.p3_last
            def fd(sy):
                self.w(sy, "dve", last)
                for ut in range(8):
                    self.inc(sy.dma_start(out=self.y_s1[ut], in_=self.S1h[ut][:]), "x", 16)
            self.blk.sync(fd)

    def ffn_half(self, es, hf):
        nc = self.nc
        g_pe, g_act, g_dve = self.cnt["pe"], self.cnt["act"], self.cnt["dve"]
        self.gate3 = (g_pe, g_act, g_dve)
        ln2T = self.sb(es, "ln2T", [128, ND * THALF], BF16)
        self.ln2T_ap = ln2T
        silu_sb = self.sb(es, "silu_sb", [128, THALF])

        with ExitStack() as e45:
            ln2tm = [self.sb(e45, f"ln2tm{i}", [128, D], BF16) for i in range(8)]
            junk2 = self.sb(e45, "junk2", [128, D])
            s2 = self.sb(e45, "st2", [128, 40])
            pstr = [self.ps(e45, f"p4tr{i}", [128, 1024], BF16) for i in range(4)]
            st = dict(norm={}, tr={}, tr_last=0, lnst=0)

            tr_plan_base = None

            chains = {}
            for grp in range(4):
                uts = (2 * grp, 2 * grp + 1)
                def ve_a(v, uts=uts):
                    for ut in uts:
                        self.w(v, "dve", self.p3_add[ut])
                    last = None
                    for ut in uts:
                        last = v.reduce_sum(s2[:, ut:ut + 1], self.S1h[ut][:], axis=AX)
                    self.vfence(v, last)
                    for ut in uts:
                        last = v.tensor_scalar_mul(s2[:, 8 + ut:9 + ut], s2[:, ut:ut + 1],
                                                   -1.0 / D)
                    self.vfence(v, last)
                    st["lnst"] = self.inc(v.engine_nop(), "dve")

                def sc_a(s, uts=uts):
                    self.w(s, "dve", st["lnst"])
                    for ut in uts:
                        ins = s.activation(junk2[:], self.S1h[ut][:], AF.Square,
                                           bias=s2[:, 8 + ut:9 + ut],
                                           accum_out=s2[:, 16 + ut:17 + ut])
                        self.sfence(s, ins)
                    last = None
                    for ut in uts:
                        last = s.activation(s2[:, 24 + ut:25 + ut], s2[:, 16 + ut:17 + ut],
                                            AF.Sqrt, scale=1.0 / D, bias=self.eps_c[:])
                    st["sqrt"] = self.inc(last, "act")

                def ve_b(v, uts=uts):
                    self.w(v, "act", st["sqrt"])
                    last = None
                    for ut in uts:
                        last = v.reciprocal(s2[:, 32 + ut:33 + ut], s2[:, 24 + ut:25 + ut])
                    self.vfence(v, last)
                    for ut in uts:
                        ins = v.tensor_scalar(ln2tm[ut][:], self.S1h[ut][:],
                                              s2[:, 8 + ut:9 + ut], s2[:, 32 + ut:33 + ut],
                                              ALU.add, ALU.mult)
                        st["norm"][ut] = self.inc(ins, "dve")
                chains[grp] = (ve_a, sc_a, ve_b)
            dve_base4 = self.cnt["dve"]
            evac_plan = {}
            _c = dve_base4 + 1
            for ut in range(8):
                if ut % 2 == 0:
                    _c += 3
                for d in range(ND):
                    evac_plan[(ut, d)] = _c + d + 1
                _c += 16 + (1 if ut >= 1 else 0) + 2 + 1

            pad = self.sb(e45, "pad", [128, 128])
            lg = [self.sb(e45, f"lg{i}", [128, 8]) for i in range(2)]
            top8 = [self.sb(e45, f"top8{i}", [128, 8]) for i in range(2)]
            cstat = [self.sb(e45, f"cstat{i}", [128, 8]) for i in range(2)]
            sel = [self.sb(e45, f"sel{i}", [128, 8]) for i in range(2)]
            e1t = [self.sb(e45, f"e1t{i}", [128, 8]) for i in range(2)]
            comb = [self.sb(e45, f"comb{i}", [128, 8]) for i in range(2)]
            ps_lgF = [self.ps(e45, f"ps_lg{i}", [128, 512], F32) for i in range(2)]
            ps_lg = [p[:, 0:8] for p in ps_lgF]
            ps_ptF = self.ps(e45, "ps_pt", [128, 512], F32)
            ps_pt = ps_ptF[:, 0:128]
            rst = dict(mm={}, evac={}, e1={}, sel={}, combw={}, ptr={}, pev={})

            def ms(v):
                ins = v.memset(pad[:], 0.0)
                rst["ms"] = self.inc(ins, "dve")
                assert rst["ms"] == dve_base4 + 1
            self.blk.vector(ms)

            for ut in range(8):
                if ut % 2 == 0:
                    ve_a_f, sc_a_f, ve_b_f = chains[ut // 2]
                    self.blk.vector(ve_a_f)
                    self.blk.scalar(sc_a_f)
                    self.blk.vector(ve_b_f)
                def f_tr(t, ut=ut):
                    self.w(t, "dve", st["norm"][ut])
                    if ut == 0:
                        self.w(t, "dve", g_dve)
                        self.w(t, "act", g_act)
                    for d in range(ND):
                        j = ut * ND + d
                        if j >= 4:
                            self.w(t, "dve", evac_plan[((j - 4) // ND, (j - 4) % ND)])
                        ins = t.transpose(pstr[j % 4][:, 0:128],
                                          ln2tm[ut][:, 128 * d:128 * (d + 1)],
                                          self.identb[:])
                        st["tr"][(ut, d)] = self.inc(ins, "pe")
                    if ut >= 1:
                        self.w(t, "dve", rst["combw"][ut - 1])
                        if ut >= 2:
                            self.w(t, "dve", rst["pev"][ut - 2])
                        else:
                            self.w(t, "dve", g_dve)
                        ins = t.transpose(ps_pt, pad[:], self.identf_sb[:])
                        rst["ptr"][ut - 1] = self.inc(ins, "pe")
                    ins = None
                    if ut >= 2:
                        self.w(t, "dve", rst["evac"][ut - 2])
                    for d in range(ND):
                        self.w(t, "dve", evac_plan[(ut, d)])
                        ins = t.matmul(ps_lg[ut % 2],
                                       ln2T[:, d * THALF + 128 * ut:d * THALF + 128 * (ut + 1)],
                                       self.rwT[:, 8 * d:8 * (d + 1)],
                                       start=(d == 0), stop=(d == ND - 1))
                    rst["mm"][ut] = self.inc(ins, "pe")
                self.blk.tensor(f_tr)

                def f_evac(v, ut=ut):
                    for d in range(ND):
                        j = ut * ND + d
                        self.w(v, "pe", st["tr"][(ut, d)])
                        ins = v.tensor_copy(ln2T[:, d * THALF + 128 * ut:
                                                 d * THALF + 128 * (ut + 1)],
                                            pstr[j % 4][:, 0:128])
                        assert self.inc(ins, "dve") == evac_plan[(ut, d)], (ut, d)
                    if ut >= 1:
                        self.w(v, "pe", rst["ptr"][ut - 1])
                        ins = v.tensor_copy(self.combT[:, 128 * (ut - 1):128 * ut],
                                            ps_pt)
                        rst["pev"][ut - 1] = self.inc(ins, "dve")
                self.blk.vector(f_evac)

                def f_ve1(v, ut=ut):
                    p = ut % 2
                    self.w(v, "pe", rst["mm"][ut])
                    ins = v.tensor_copy(lg[p][:], ps_lg[p])
                    rst["evac"][ut] = self.inc(ins, "dve")
                    v.wait_ge(self.sems["dve"], rst["evac"][ut])
                    ins = v.max(out=top8[p][:], in_=lg[p][:])
                    self.vfence(v, ins)
                    ins = v.tensor_single_scalar(cstat[p][:, 0:1], top8[p][:, 2:3],
                                                 top8[p][:, 1:2], ALU.add)
                    self.vfence(v, ins)
                    ins = v.tensor_scalar_mul(cstat[p][:, 0:1], cstat[p][:, 0:1], 0.5)
                    ins2 = v.tensor_scalar_mul(cstat[p][:, 1:2], top8[p][:, 0:1], -1.0)
                    self.vfence(v, ins2)
                    ins = v.tensor_scalar_sub(sel[p][:], lg[p][:], cstat[p][:, 0:1])
                    self.vfence(v, ins)
                    ins = v.tensor_scalar_mul(sel[p][:], sel[p][:], 1e7)
                    self.vfence(v, ins)
                    ins = v.tensor_scalar_min(sel[p][:], sel[p][:], 1.0)
                    self.vfence(v, ins)
                    ins = v.tensor_scalar_max(sel[p][:], sel[p][:], 0.0)
                    rst["sel"][ut] = self.inc(ins, "dve")
                self.blk.vector(f_ve1)

                def f_sc(s, ut=ut):
                    p = ut % 2
                    self.w(s, "dve", rst["sel"][ut])
                    ins = s.activation(e1t[p][:], lg[p][:], AF.Exp, bias=cstat[p][:, 1:2])
                    rst["e1"][ut] = self.inc(ins, "act")
                self.blk.scalar(f_sc)

                def f_ve2(v, ut=ut):
                    p = ut % 2
                    self.w(v, "act", rst["e1"][ut])
                    ins = v.tensor_mul(e1t[p][:], e1t[p][:], sel[p][:])
                    self.vfence(v, ins)
                    ins = v.reduce_sum(cstat[p][:, 2:3], e1t[p][:], axis=AX)
                    self.vfence(v, ins)
                    ins = v.reciprocal(cstat[p][:, 3:4], cstat[p][:, 2:3])
                    self.vfence(v, ins)
                    ins = v.tensor_scalar_mul(comb[p][:], e1t[p][:], cstat[p][:, 3:4])
                    self.vfence(v, ins)
                    if ut == 0:
                        self.w(v, "dve", rst["ms"])
                    if ut >= 1:
                        self.w(v, "pe", rst["ptr"][ut - 1])
                    ins = v.tensor_copy(pad[:, 0:8], comb[p][:])
                    rst["combw"][ut] = self.inc(ins, "dve")
                self.blk.vector(f_ve2)

            def f_pe2_last(t):
                self.w(t, "dve", rst["combw"][7])
                self.w(t, "dve", rst["pev"][6])
                ins = t.transpose(ps_pt, pad[:], self.identf_sb[:])
                rst["ptr"][7] = self.inc(ins, "pe")
            self.blk.tensor(f_pe2_last)

            def f_ve3_last(v):
                self.w(v, "pe", rst["ptr"][7])
                ins = v.tensor_copy(self.combT[:, 128 * 7:128 * 8], ps_pt)
                rst["pev"][7] = self.inc(ins, "dve")
            self.blk.vector(f_ve3_last)
            self.comb_done = self.cnt["dve"]
            self.ln2_done = self.comb_done
            self.p4_act_last = self.cnt["act"]
            self.p5_act_last = self.cnt["act"]
            if "ffn" in self.dbg and hf == 0:
                cd = self.comb_done
                def fdmp(sy):
                    self.w(sy, "dve", cd)
                    self.inc(sy.dma_start(
                        out=self.y_ln2,
                        in_=ln2T[:].rearrange("p (d u) -> d p u", u=THALF)), "x", 16)
                    self.inc(sy.dma_start(out=self.y_cmb, in_=self.combT[:]), "x", 16)
                self.blk.sync(fdmp)

        with ExitStack() as e6:
            A = self.sb(e6, "A_sb", [128, NI * THALF], BF16)
            wd_sb = self.sb(e6, "wd_sb", [128, NI * D], BF16)
            crow = self.sb(e6, "crow", [1, THALF])
            cb_sb = self.sb(e6, "cb_sb", [128, THALF])
            gw_r = [self.sb(e6, f"gw{i}", [128, D], BF16) for i in range(2)]
            uw_r = [self.sb(e6, f"uw{i}", [128, D], BF16) for i in range(2)]
            psG = [self.ps(e6, f"psG{i}", [128, THALF], F32) for i in range(2)]
            psU = [self.ps(e6, f"psU{i}", [128, THALF], F32) for i in range(2)]
            prev = dict(silu={}, t2={}, dadd_last=0, dmm_all=0, cbe=0)
            for e in range(NE):
                self._expert(hf, e, crow, cb_sb, silu_sb, A, wd_sb, gw_r, uw_r,
                             psG, psU, prev)

        with ExitStack() as e7:
            A7 = self.sb(e7, "A7_sb", [128, NSI * THALF], BF16)
            u7 = self.sb(e7, "u7", [128, THALF])
            psG = [self.ps(e7, f"psG7{i}", [128, THALF], F32) for i in range(2)]
            psU = [self.ps(e7, f"psU7{i}", [128, THALF], F32) for i in range(2)]
            self._p7(hf, e7, silu_sb, A7, u7, psG, psU, prev)

        def fout(s):
            for ut in range(8):
                self.w(s, "dve", self.sh_add_last[ut])
                base = 1024 * hf + 512 * (ut // 4) + (ut % 4)
                ins = s.dma_start(out=self.out[base: base + 509: 4, :],
                                  in_=self.S1h[ut][:])
                self.inc(ins, "o", 16)
        self.blk.scalar(fout)
        if hf == 0:
            self.out_done_half0 = self.cnt["o"]

    def _expert(self, hf, e, crow, cb_sb, silu_sb, A, wd_sb, gw_r, uw_r,
                psG, psU, prev):
        st6 = dict(gld={}, uld={}, dld={}, gmm={}, umm={}, silu={}, t2={},
                   dmm={}, dadd={})
        g_pe, g_act, g_dve = self.gate3

        def f_crow(sy):
            if e >= 1:
                self.w(sy, "dve", prev["cbe"])
            else:
                self.w(sy, "dve", self.comb_done)
            ins = sy.dma_start(out=crow[:], in_=self.combT[e:e + 1, :])
            st6["crow_ld"] = self.inc(ins, "r17", 16)
        self.blk.sync(f_crow)

        def f_cb(t):
            self.w(t, "r17", st6["crow_ld"])
            if e == 0:
                self.w(t, "dve", self.comb_done)
                self.w(t, "act", self.p5_act_last)
            else:
                self.w(t, "dve", prev["dadd_last"])
            ins = None
            for n in range(2):
                ins = t.matmul(psG[(NI - 1) % 2][:, 512 * n:512 * (n + 1)],
                               self.onesf_sb[:], crow[0:1, 512 * n:512 * (n + 1)],
                               start=True, stop=True)
            st6["cbmm"] = self.inc(ins, "pe")
        self.blk.tensor(f_cb)

        def f_cbe(v):
            self.w(v, "pe", st6["cbmm"])
            if e >= 1:
                self.w(v, "dve", prev["t2"][NI - 1])
            ins = v.tensor_copy(cb_sb[:], psG[(NI - 1) % 2][:])
            st6["cb"] = self.inc(ins, "dve")
        self.blk.vector(f_cbe)

        for i in range(NI):
            def f_ld(g, i=i):
                if e == 0 and i < 2:
                    self.w(g, "dve", self.comb_done)
                    self.w(g, "pe", g_pe)
                elif i < 2:
                    self.w(g, "pe", prev["umm_last"])
                else:
                    self.w(g, "pe", st6["umm"][i - 2])
                ins = g.dma_start(out=gw_r[i % 2][:].rearrange("p (d c) -> p d c", c=128),
                                  in_=self.gw[e, i])
                st6["gld"][i] = self.inc(ins, f"g{i % 2}", 16)
                ins = g.dma_start(out=uw_r[i % 2][:].rearrange("p (d c) -> p d c", c=128),
                                  in_=self.uw[e, i])
                st6["uld"][i] = self.inc(ins, f"g{2 + i % 2}", 16)
                if e >= 1:
                    self.w(g, "pe", prev["dmm_all"])
                ins = g.dma_start(out=wd_sb[:, i * D:(i + 1) * D], in_=self.dw[e, i])
                st6["dld"][i] = self.inc(ins, f"g{4 + i}", 16)
            self.blk.gpsimd(f_ld)

            def f_mm(t, i=i):
                self.w(t, f"g{i % 2}", st6["gld"][i])
                self.w(t, f"g{2 + i % 2}", st6["uld"][i])
                if i >= 2:
                    self.w(t, "act", st6["silu"][i - 2])
                    self.w(t, "dve", st6["t2"][i - 2])
                elif e >= 1:
                    self.w(t, "dve", prev["dadd_last"])
                if i == 0:
                    self.w(t, "dve", st6["cb"])
                ins = None
                for d in range(ND):
                    for n in range(2):
                        ins = t.matmul(psG[i % 2][:, 512 * n:512 * (n + 1)],
                                       gw_r[i % 2][:, 128 * d:128 * (d + 1)],
                                       self.ln2T_ap[:, d * THALF + 512 * n:
                                                    d * THALF + 512 * (n + 1)],
                                       start=(d == 0), stop=(d == ND - 1))
                st6["gmm"][i] = self.inc(ins, "pe")
                for d in range(ND):
                    for n in range(2):
                        ins = t.matmul(psU[i % 2][:, 512 * n:512 * (n + 1)],
                                       uw_r[i % 2][:, 128 * d:128 * (d + 1)],
                                       self.ln2T_ap[:, d * THALF + 512 * n:
                                                    d * THALF + 512 * (n + 1)],
                                       start=(d == 0), stop=(d == ND - 1))
                st6["umm"][i] = self.inc(ins, "pe")
            self.blk.tensor(f_mm)

            def f_silu(s, i=i):
                self.w(s, "pe", st6["gmm"][i])
                if i >= 1:
                    self.w(s, "dve", st6["t2"][i - 1])
                elif e >= 1:
                    self.w(s, "dve", prev["t2"][NI - 1])
                elif e == 0:
                    self.w(s, "dve", g_dve)
                ins = s.activation(silu_sb[:], psG[i % 2][:], AF.Silu)
                st6["silu"][i] = self.inc(ins, "act")
            self.blk.scalar(f_silu)

            def f_t2(v, i=i):
                self.w(v, "act", st6["silu"][i])
                self.w(v, "pe", st6["umm"][i])
                ins = v.tensor_mul(silu_sb[:], silu_sb[:], psU[i % 2][:])
                c1 = self.inc(ins, "dve")
                v.wait_ge(self.sems["dve"], c1)
                if e >= 1 and i == 0:
                    self.w(v, "pe", prev["dmm_all"])
                self.w(v, "dve", st6["cb"])
                ins = v.tensor_mul(A[:, i * THALF:(i + 1) * THALF], silu_sb[:], cb_sb[:])
                st6["t2"][i] = self.inc(ins, "dve")
            self.blk.vector(f_t2)

        for ut in range(8):
            def f_dmm(t, ut=ut):
                pair = psG if ut % 2 == 0 else psU
                if ut < 2:
                    self.w(t, "act", st6["silu"][NI - 1])
                    self.w(t, "dve", st6["t2"][NI - 1])
                    for i in range(NI):
                        self.w(t, f"g{4 + i}", st6["dld"][i])
                else:
                    self.w(t, "dve", st6["dadd"][(ut - 2, 1)])
                ins = None
                for k in range(2):
                    for i in range(NI):
                        for n in range(2):
                            ins = t.matmul(
                                pair[k][:, 512 * n:512 * (n + 1)],
                                A[:, i * THALF + 128 * ut:i * THALF + 128 * (ut + 1)],
                                wd_sb[:, i * D + 1024 * k + 512 * n:
                                      i * D + 1024 * k + 512 * (n + 1)],
                                start=(i == 0), stop=(i == NI - 1))
                    st6["dmm"][(ut, k)] = self.inc(ins, "pe")
            self.blk.tensor(f_dmm)

            def f_dadd(v, ut=ut):
                pair = psG if ut % 2 == 0 else psU
                for k in range(2):
                    self.w(v, "pe", st6["dmm"][(ut, k)])
                    if e == 0 and k == 0:
                        self.w(v, "dve", self.ln2_done)
                        self.w(v, "act", self.p4_act_last)
                    ins = v.tensor_tensor(self.S1h[ut][:, 1024 * k:1024 * (k + 1)],
                                          self.S1h[ut][:, 1024 * k:1024 * (k + 1)],
                                          pair[k][:], ALU.add)
                    st6["dadd"][(ut, k)] = self.inc(ins, "dve")
            self.blk.vector(f_dadd)

        prev["silu"] = dict(st6["silu"]); prev["t2"] = dict(st6["t2"])
        prev["umm_last"] = st6["umm"][NI - 1]
        prev["dmm_all"] = st6["dmm"][(7, 1)]
        prev["cbe"] = st6["cb"]
        prev["dadd_last"] = st6["dadd"][(7, 1)]
        self.p6_dadd_last = {ut: st6["dadd"][(ut, 1)] for ut in range(8)}
        self.p6_dmm_all = st6["dmm"][(7, 1)]
        self.p6_prev = prev

    def _p7(self, hf, e7, silu_sb, A, u7, psG, psU, prev):
        st = dict(gld={}, uld={}, gmm={}, umm={}, silu={}, t2={}, sld={},
                  dmm={}, dadd={})

        with ExitStack() as e7a:
            gw_r = [self.sb(e7a, f"sgwr{i}", [128, D], BF16) for i in range(2)]
            uw_r = [self.sb(e7a, f"suwr{i}", [128, D], BF16) for i in range(2)]
            for i in range(NSI):
                def f_ld(g, i=i):
                    if i < 2:
                        self.w(g, "pe", self.p6_dmm_all)
                        self.w(g, "dve", prev["dadd_last"])
                    else:
                        self.w(g, "pe", st["umm"][i - 2])
                    ins = g.dma_start(out=gw_r[i % 2][:].rearrange("p (d c) -> p d c", c=128),
                                      in_=self.sgw[i])
                    st["gld"][i] = self.inc(ins, f"g{i % 2}", 16)
                    ins = g.dma_start(out=uw_r[i % 2][:].rearrange("p (d c) -> p d c", c=128),
                                      in_=self.suw[i])
                    st["uld"][i] = self.inc(ins, f"g{2 + i % 2}", 16)
                self.blk.gpsimd(f_ld)

                def f_mm(t, i=i):
                    self.w(t, f"g{i % 2}", st["gld"][i])
                    self.w(t, f"g{2 + i % 2}", st["uld"][i])
                    if i < 2:
                        self.w(t, "dve", prev["dadd_last"])
                    else:
                        self.w(t, "act", st["silu"][i - 2])
                        self.w(t, "dve", st["t2"][i - 2])
                    ins = None
                    for d in range(ND):
                        for n in range(2):
                            ins = t.matmul(psG[i % 2][:, 512 * n:512 * (n + 1)],
                                           gw_r[i % 2][:, 128 * d:128 * (d + 1)],
                                           self.ln2T_ap[:, d * THALF + 512 * n:
                                                        d * THALF + 512 * (n + 1)],
                                           start=(d == 0), stop=(d == ND - 1))
                    st["gmm"][i] = self.inc(ins, "pe")
                    for d in range(ND):
                        for n in range(2):
                            ins = t.matmul(psU[i % 2][:, 512 * n:512 * (n + 1)],
                                           uw_r[i % 2][:, 128 * d:128 * (d + 1)],
                                           self.ln2T_ap[:, d * THALF + 512 * n:
                                                        d * THALF + 512 * (n + 1)],
                                           start=(d == 0), stop=(d == ND - 1))
                    st["umm"][i] = self.inc(ins, "pe")
                self.blk.tensor(f_mm)

                def f_act(s, i=i):
                    self.w(s, "pe", st["gmm"][i])
                    if i >= 1:
                        self.w(s, "dve", st["t2"][i - 1])
                    else:
                        self.w(s, "dve", prev["t2"][NI - 1])
                    ins = s.activation(silu_sb[:], psG[i % 2][:], AF.Silu,
                                       bias=self.sgb_sb[:, i:i + 1])
                    c1 = self.inc(ins, "act")
                    s.wait_ge(self.sems["act"], c1)
                    self.w(s, "pe", st["umm"][i])
                    ins = s.activation(u7[:], psU[i % 2][:], AF.Identity,
                                       bias=self.sub_sb[:, i:i + 1])
                    st["silu"][i] = self.inc(ins, "act")
                self.blk.scalar(f_act)

                def f_t2(v, i=i):
                    self.w(v, "act", st["silu"][i])
                    if i == 0:
                        self.w(v, "pe", self.p6_dmm_all)
                    ins = v.tensor_mul(A[:, i * THALF:(i + 1) * THALF], silu_sb[:], u7[:])
                    st["t2"][i] = self.inc(ins, "dve")
                self.blk.vector(f_t2)

        self.sh_add_last = {}
        with ExitStack() as e7b:
            sd_sb = self.sb(e7b, "sd_sb", [128, 11 * D], BF16)
            sdb_row = self.sb(e7b, "sdb_row", [1, D])
            def f_sdb(sy):
                self.w(sy, "pe", st["umm"][NSI - 1])
                ins = sy.dma_start(out=sdb_row[:], in_=self.sdb)
                st["sdb_ld"] = self.inc(ins, "r18", 16)
            self.blk.sync(f_sdb)

            for ih in range(2):
                ibase = ih * 11
                def f_sld(g, ih=ih, ibase=ibase):
                    for i in range(11):
                        if ih == 0:
                            self.w(g, "pe", st["umm"][NSI - 1])
                        else:
                            self.w(g, "pe", st["dmm"][(0, 7, 1)])
                        ins = g.dma_start(out=sd_sb[:, i * D:(i + 1) * D],
                                          in_=self.sdw[ibase + i])
                        st["sld"][(ih, i)] = self.inc(ins, f"g{4 + i}", 16)
                self.blk.gpsimd(f_sld)

                for ut in range(8):
                    def f_dmm(t, ih=ih, ut=ut, ibase=ibase):
                        pair = psG if ut % 2 == 0 else psU
                        if ut < 2:
                            self.w(t, "act", st["silu"][NSI - 1])
                            self.w(t, "dve", st["t2"][NSI - 1])
                            if ih == 1:
                                self.w(t, "dve", st["dadd"][(0, 7, 1)])
                                self.w(t, "r18", st["sdb_ld"])
                        else:
                            self.w(t, "dve", st["dadd"][(ih, ut - 2, 1)])
                        ins = None
                        for k in range(2):
                            if ih == 1:
                                for n in range(2):
                                    ins = t.matmul(pair[k][:, 512 * n:512 * (n + 1)],
                                                   self.onesf_sb[:],
                                                   sdb_row[0:1, 1024 * k + 512 * n:
                                                           1024 * k + 512 * (n + 1)],
                                                   start=True, stop=False)
                            for i in range(11):
                                if ut == 0:
                                    self.w(t, f"g{4 + i}", st["sld"][(ih, i)])
                                for n in range(2):
                                    ins = t.matmul(
                                        pair[k][:, 512 * n:512 * (n + 1)],
                                        A[:, (ibase + i) * THALF + 128 * ut:
                                          (ibase + i) * THALF + 128 * (ut + 1)],
                                        sd_sb[:, i * D + 1024 * k + 512 * n:
                                              i * D + 1024 * k + 512 * (n + 1)],
                                        start=(ih == 0 and i == 0), stop=(i == 10))
                            st["dmm"][(ih, ut, k)] = self.inc(ins, "pe")
                    self.blk.tensor(f_dmm)

                    def f_dadd(v, ih=ih, ut=ut):
                        pair = psG if ut % 2 == 0 else psU
                        for k in range(2):
                            self.w(v, "pe", st["dmm"][(ih, ut, k)])
                            ins = v.tensor_tensor(self.S1h[ut][:, 1024 * k:1024 * (k + 1)],
                                                  self.S1h[ut][:, 1024 * k:1024 * (k + 1)],
                                                  pair[k][:], ALU.add)
                            st["dadd"][(ih, ut, k)] = self.inc(ins, "dve")
                        if ih == 1:
                            self.sh_add_last[ut] = st["dadd"][(1, ut, 1)]
                    self.blk.vector(f_dadd)

    def build(self):
        nc = self.nc
        with ExitStack() as top:
            self.blk = top.enter_context(nc.Block())
            ep = top.enter_context(ExitStack())
            with ExitStack() as e1:
                self.p0(ep, e1)
                self.p1(e1)
            if not getattr(self, "only_p1", False):
                self.S1h = [self.sb(ep, f"s1h{i}", [128, D]) for i in range(8)]
                self.combT = self.sb(ep, "combT", [128, THALF])
                with ExitStack() as e_scr0:
                    self.scrT = [None, None]
                    self.scrT[0] = self.sb(e_scr0, "scrT0", [128, ND * THALF], BF16)
                    with ExitStack() as e_scr1:
                        self.scrT[1] = self.sb(e_scr1, "scrT1", [128, ND * THALF], BF16)
                        with ExitStack() as e2:
                            self.p2(e2)
                    if not getattr(self, "only_p2", False):
                        with ExitStack() as e3:
                            self.p3(e3, 0)
                if not getattr(self, "only_p2", False):
                    with ExitStack() as e4:
                        self.ffn_half(e4, 0)
                    with ExitStack() as e3:
                        self.p3(e3, 1)
                    with ExitStack() as e4:
                        self.ffn_half(e4, 1)
            rem = nc.sbuf_bytes_remaining
            final_counts = [(k, self.cnt[k]) for k in self.cnt
                            if k.startswith("r") or k in ("x", "w", "o")]
            def fin(sy):
                for k, v in final_counts:
                    self.w(sy, k, v)
            self.blk.sync(fin)
        return nc



def _restage(inp, core):
    b, qd = core // 4, core % 4
    d = {}
    x = inp["L0_attn_x"][b]
    d["x"] = np.ascontiguousarray(x)
    d["xres"] = np.ascontiguousarray(x[TCORE * qd: TCORE * (qd + 1)])
    def restage_w(w):
        noc = w.shape[1] // 128
        return np.ascontiguousarray(
            w.reshape(ND, 128, noc, 128).transpose(2, 1, 0, 3))

    for nm, w in (("wq", inp["L0_attn_Wq"]), ("wk", inp["L0_attn_Wk"])):
        d[nm] = restage_w(w[:, 512 * qd:512 * (qd + 1)])
    d["wv"] = np.ascontiguousarray(
        inp["L0_attn_Wv"][:, 512 * qd:512 * (qd + 1)]
        .reshape(ND, 128, 512).transpose(1, 0, 2))
    d["wo"] = np.ascontiguousarray(inp["L0_attn_Wo"].reshape(ND, 128, D))
    d["bo_row"] = np.ascontiguousarray(inp["L0_attn_bo"].reshape(1, D))
    d["rw"] = np.ascontiguousarray(inp["L0_ffn_router_w"])
    d["gw"] = np.stack([restage_w(inp["L0_ffn_gate_w"][e]) for e in range(NE)])
    d["uw"] = np.stack([restage_w(inp["L0_ffn_up_w"][e]) for e in range(NE)])
    d["dw"] = np.ascontiguousarray(inp["L0_ffn_down_w"].reshape(NE, NI, 128, D))
    d["sgw"] = restage_w(inp["L0_ffn_s_gate_w"])
    d["suw"] = restage_w(inp["L0_ffn_s_up_w"])
    d["sdw"] = np.ascontiguousarray(inp["L0_ffn_s_down_w"].reshape(NSI, 128, D))
    d["sgb_t"] = np.ascontiguousarray(inp["L0_ffn_s_gate_b"].reshape(NSI, 128).T)
    d["sub_t"] = np.ascontiguousarray(inp["L0_ffn_s_up_b"].reshape(NSI, 128).T)
    d["sdb"] = np.ascontiguousarray(inp["L0_ffn_s_down_b"].reshape(1, D))
    d["identf"] = np.eye(128, dtype=np.float32)
    d["onesf"] = np.ones((1, 128), dtype=np.float32)
    return d


def kernel(**inputs):
    kb = KB()
    nc = kb.build()
    in_maps = [_restage(inputs, c) for c in range(8)]
    res = run_bass_kernel_spmd(nc, in_maps, core_ids=list(range(8)))
    out = np.empty((B, T, D), np.float32)
    for c in range(8):
        b, qd = c // 4, c % 4
        out[b, TCORE * qd:TCORE * (qd + 1)] = res.results[c]["out"]
    return out
